# version 07
# baseline (speedup 1.0000x reference)
import numpy as np
import ml_dtypes
from contextlib import ExitStack
import concourse.bass as bass
import concourse.mybir as mybir
from concourse.bass_utils import run_bass_kernel_spmd

F32 = mybir.dt.float32
BF16 = mybir.dt.bfloat16
Alu = mybir.AluOpType
Act = mybir.ActivationFunctionType
AX = mybir.AxisListType
BF = ml_dtypes.bfloat16

T, D, NH, DH, CF, CK, II = 2048, 1024, 16, 64, 4, 512, 2816
NT, NDC, NTC, NEC, NIC = T // 128, D // 128, T // 512, 3 * D // 128, II // 128
MIDS = [0.25, 0.75, 1.25, 1.75, 2.5, 3.5, 5.0]
CNTS = [1.0, 1.0, 1.0, 1.0, 2.0, 2.0, 4.0]
ICS = [(0, 512), (512, 512), (1024, 512), (1536, 512), (2048, 512), (2560, 256)]
ENG = ("sync", "gpsimd", "tensor", "vector", "scalar")


class Phase:

    def __init__(self, b, name):
        self.b, self.name = b, name
        self.jobs = {e: [] for e in ENG}

    def j(self, eng, fn, waits=(), tick=None, n=1):
        v = None
        if tick is not None:
            self.b.sv[tick] += n
            v = self.b.sv[tick]
        self.jobs[eng].append((fn, tuple(waits), tick, n))
        return v

    def tail(self):
        for e in ("sync", "gpsimd"):
            if self.jobs[e]:
                nm = "io" if e == "sync" else "w"
                self.jobs[e].append(
                    (lambda h: h.nop(), (( nm, self.b.sv[nm]),), None, 1))

    def run(self, tail=True):
        b = self.b
        if tail:
            self.tail()
        with b.nc.Block(self.name) as blk:
            for e in ENG:
                jl = self.jobs[e]
                if not jl:
                    continue

                def body(h, jl=jl):
                    cur = {}
                    for fn, waits, tick, n in jl:
                        for snm, val in waits:
                            if val and cur.get(snm, -1) < val:
                                h.wait_ge(getattr(b, "s_" + snm), val)
                                cur[snm] = val
                        inst = fn(h)
                        if tick is not None:
                            inst.then_inc(getattr(b, "s_" + tick), n)
                getattr(blk, e)(body)


class Bld:
    def __init__(self, upto=4, nseq=2):
        self.upto, self.nseq = upto, nseq
        self.nc = bass.Bass("TRN2", target_bir_lowering=False)
        self.st = ExitStack()

    def declare(self):
        nc, st = self.nc, self.st
        self.sv = {}
        for n in ["w", "io", "mm", "ev", "v", "sc", "g"]:
            setattr(self, "s_" + n, st.enter_context(nc.semaphore("s_" + n)))
            self.sv[n] = 0

        def din(name, shape, dtype=F32):
            return nc.dram_tensor(name, list(shape), dtype, kind="ExternalInput").ap()

        self.x_in = din("x", (self.nseq, T, D))
        self.y_out = nc.dram_tensor("y", [self.nseq, T, D], F32,
                                    kind="ExternalOutput").ap()
        self.dbg_out = nc.dram_tensor("dbg", [128, NT * D], F32,
                                      kind="ExternalOutput").ap()
        self.wqkv_a = din("wqkv_a", (D, 3 * D))
        self.convw_i = din("convw", (128, 4 * D))
        self.convb_i = din("convb", (128, NDC))
        self.nullk_i = din("nullk", (128, NH))
        self.nullv_i = din("nullv", (1, NH * 65))
        self.wout_a = din("wout_a", (D, D))
        self.bouta_i = din("bout_a", (128, NDC))
        self.gam_i = din("gam_b", (128, NDC))
        self.bet_i = din("bet_b", (128, NDC))
        self.wqkvT_b = din("wqkvT_b", (D, 3 * D))
        self.woutT_b = din("woutT_b", (D, D))
        self.boutb_i = din("bout_b", (1, D))
        self.w_gate = din("w_gate", (II, D))
        self.w_up = din("w_up", (II, D))
        self.w_down = din("w_down", (D, II))
        self.maskA_i = din("maskA", (128, 512), BF16)
        self.maskB_i = din("maskB", (128, 896), BF16)
        self.ident_i = din("identc", (128, 128), BF16)
        self.hlhs_i = din("hlhs", (128, NDC * NH), BF16)
        self.rowsel_i = din("rowsel", (128, NH * 128), BF16)
        self.cosD_hi = din("cosD_hi", (128, NDC * 513))
        self.cosD_lo = din("cosD_lo", (128, NDC * 513))
        self.sinD_hi = din("sinD_hi", (128, NDC * 513))
        self.sinD_lo = din("sinD_lo", (128, NDC * 513))
        self.c1_hi = din("c1_hi", (128, NT * 16)); self.c1_lo = din("c1_lo", (128, NT * 16))
        self.s1_hi = din("s1_hi", (128, NT * 16)); self.s1_lo = din("s1_lo", (128, NT * 16))
        self.c2_hi = din("c2_hi", (128, NT * 128)); self.c2_lo = din("c2_lo", (128, NT * 128))
        self.s2_hi = din("s2_hi", (128, NT * 128)); self.s2_lo = din("s2_lo", (128, NT * 128))

        itn = lambda n, s, dt=BF16: nc.dram_tensor(n, list(s), dt, kind="Internal").ap()
        self.sc_cosT = itn("sc_cosT", (128, NT * T))
        self.sc_sinT = itn("sc_sinT", (128, NT * T))
        self.sc_wg = itn("sc_wg", (128, NDC * II))
        self.sc_wu = itn("sc_wu", (128, NDC * II))
        self.sc_wd = itn("sc_wd", (128, NIC * D))
        self.sc_wqkvb = itn("sc_wqkvb", (128, NDC * 3 * D))
        self.sc_hqT = itn("sc_hqT", (128, NIC * T))

        sb = lambda n, s, dt=F32: st.enter_context(
            nc.sbuf_tensor("sb_" + n, list(s), dt))
        self.x_sb = sb("x_sb", (128, NT * D))
        self.ident = sb("ident", (128, 128), BF16)
        self.hlhs = sb("hlhs", (128, NDC * NH), BF16)
        self.convb = sb("convb", (128, NDC))
        self.nullk = sb("nullk", (128, NH), BF16)
        self.nullv = sb("nullv", (1, NH * 65), BF16)
        self.bouta = sb("bouta", (128, NDC))
        self.gam = sb("gam", (128, NDC))
        self.bet = sb("bet", (128, NDC))
        self.betW = sb("betW", (128, NEC))
        self.boutb = sb("boutb", (1, D), BF16)
        self.onesr = sb("onesr", (1, 128), BF16)
        self.onesc = sb("onesc", (128, 1), BF16)
        self.onesF = sb("onesF", (128, 128))
        self.onesB = sb("onesB", (128, 64), BF16)
        self.negc = sb("negc", (128, NH))
        self.smal = sb("smal", (128, 96))
        self.ps = [st.enter_context(nc.psum_tensor(f"ps{i}", [128, 512], F32))
                   for i in range(8)]

    def uname(self, n):
        self._uid = getattr(self, "_uid", 0) + 1
        return f"{n}_u{self._uid}"

    def dbuf(self, es, name, nbuf, cols, dt=F32):
        t = es.enter_context(
            self.nc.sbuf_tensor(self.uname(name), [128, nbuf * cols], dt))
        return [t[:, i * cols:(i + 1) * cols] for i in range(nbuf)]

    def mk_qtmp(self, es, n):
        nc = self.nc
        t = {}
        for nm, sz, dt in [("amax", n // 16, F32), ("trc", n // 16, F32),
                           ("ntb", n // 16, F32), ("ws", n, F32), ("aw", n, F32),
                           ("e1", n, BF16), ("e2", n, BF16), ("e3", n, BF16),
                           ("ms", n, BF16)]:
            t[nm] = es.enter_context(nc.sbuf_tensor(self.uname("q_" + nm), [128, sz], dt))
        t["msn"] = t["ws"]
        return t

    def quant_head(self, v, src_ap, q, n):
        g = n // 16
        amax, trc, ntb = q["amax"], q["trc"], q["ntb"]
        r3 = lambda ap: ap.rearrange("p (g k) -> p g k", k=16)
        v.drain()
        v.tensor_reduce(amax[:, 0:g], r3(src_ap), AX.X, Alu.max,
                        apply_absolute_value=True)
        v.drain()
        v.tensor_scalar(ntb[:, 0:g], amax[:, 0:g], 6e-12, -1.0 / 12.0,
                        Alu.max, Alu.mult)
        return v.tensor_scalar(trc[:, 0:g], amax[:, 0:g], 6e-12, 1.0 / 6.0,
                               Alu.max, Alu.mult)

    def quant_recip(self, e, q, n):
        g = n // 16
        e.activation(q["amax"][:, 0:g], q["trc"][:, 0:g], Act.Ln)
        e.drain()
        return e.activation(q["amax"][:, 0:g], q["amax"][:, 0:g], Act.Exp,
                            scale=-1.0)

    def quant_sign(self, e, src_ap, q, n, c0, c1):
        return e.activation(q["ms"][:, c0:c1], src_ap[:, c0:c1], Act.Sign,
                            scale=-1.0)

    def quant_steps(self, e, src_ap, q, n, c0, c1):
        g0, g1 = c0 // 16, c1 // 16
        gg = g1 - g0
        rc = q["amax"]
        ws, aw, e1, e2, e3 = q["ws"], q["aw"], q["e1"], q["e2"], q["e3"]
        sl = lambda ap: ap[:, c0:c1]
        bc = rc[:, g0:g1].unsqueeze(2).broadcast_to([128, gg, 16])
        e.tensor_tensor(sl(ws).rearrange("p (g k) -> p g k", k=16),
                        src_ap[:, c0:c1].rearrange("p (g k) -> p g k", k=16),
                        bc, Alu.mult)
        e.scalar_tensor_tensor(sl(aw), sl(ws), -1.0, sl(ws), Alu.mult, Alu.max)
        e.tensor_scalar(sl(e1), sl(aw), MIDS[0], CNTS[0], Alu.is_gt, Alu.mult)
        e.tensor_scalar(sl(e2), sl(aw), MIDS[1], CNTS[1], Alu.is_gt, Alu.mult)
        e.tensor_scalar(sl(e3), sl(aw), MIDS[2], CNTS[2], Alu.is_gt, Alu.mult)
        e.tensor_add(sl(e1), sl(e1), sl(e2))
        e.tensor_scalar(sl(e2), sl(aw), MIDS[3], CNTS[3], Alu.is_gt, Alu.mult)
        e.tensor_add(sl(e1), sl(e1), sl(e3))
        e.tensor_scalar(sl(e3), sl(aw), MIDS[4], CNTS[4], Alu.is_gt, Alu.mult)
        e.tensor_add(sl(e1), sl(e1), sl(e2))
        e.tensor_scalar(sl(e2), sl(aw), MIDS[5], CNTS[5], Alu.is_gt, Alu.mult)
        e.tensor_add(sl(e1), sl(e1), sl(e3))
        e.tensor_scalar(sl(e3), sl(aw), MIDS[6], CNTS[6], Alu.is_gt, Alu.mult)
        e.tensor_add(sl(e1), sl(e1), sl(e2))
        return e.tensor_add(sl(e1), sl(e1), sl(e3))

    def quant_fin(self, e, out_ap, q, n, c0, c1):
        g0, g1 = c0 // 16, c1 // 16
        gg = g1 - g0
        ntb, e1, ms, msn = q["ntb"], q["e1"], q["ms"], q["msn"]
        r3 = lambda ap: ap[:, c0:c1].rearrange("p (g k) -> p g k", k=16)
        bn = ntb[:, g0:g1].unsqueeze(2).broadcast_to([128, gg, 16])
        e.tensor_tensor(r3(msn), r3(ms), bn, Alu.mult)
        return e.tensor_tensor(
            out_ap[:, c0:c1].rearrange("p (g k) -> p g k", k=16), r3(e1),
            r3(msn), Alu.mult)

    def quant(self, v, src_ap, out_ap, q, n):
        self.quant_head(v, src_ap, q, n)
        v.drain()
        v.reciprocal(q["amax"][:, 0:n // 16], q["trc"][:, 0:n // 16])
        v.drain()
        self.quant_steps(v, src_ap, q, n, 0, n)
        return self.quant_fin(v, out_ap, q, n, 0, n)

    def maybe_probe(self, name, ap):
        if getattr(self, "probe", None) != name or ap is None:
            return
        ph = Phase(self, "probe_" + name)
        cols = ap.shape[-1] if len(ap.shape) == 2 else ap.shape[1]
        if ap.dtype == BF16:
            ph.j("gpsimd", lambda h: h.dma_start(
                out=self.dbg_out[0:ap.shape[0], 0:cols], in_=ap), tick="w", n=16)
        else:
            ph.j("sync", lambda h: h.dma_start(
                out=self.dbg_out[0:ap.shape[0], 0:cols], in_=ap), tick="io", n=16)
        ph.run()

    def p_init(self):
        ph = Phase(self, "init")
        for dst, src in [(self.ident, self.ident_i), (self.hlhs, self.hlhs_i),
                         (self.convb, self.convb_i), (self.bouta, self.bouta_i),
                         (self.gam, self.gam_i), (self.bet, self.bet_i)]:
            ph.j("sync", lambda h, d=dst, s=src: h.dma_start(out=d[:], in_=s[:]),
                 tick="io", n=16)
        for dst, src in [(self.nullk, self.nullk_i), (self.nullv, self.nullv_i),
                         (self.boutb, self.boutb_i)]:
            ph.j("gpsimd", lambda h, d=dst, s=src: h.dma_start(out=d[:], in_=s[:]),
                 tick="w", n=16)
        ph.j("gpsimd", lambda h: h.memset(self.onesc[:], 1.0), tick="g")
        ph.j("gpsimd", lambda h: h.memset(self.onesr[:], 1.0), tick="g")
        ph.j("gpsimd", lambda h: h.memset(self.onesF[:], 1.0), tick="g")
        ph.j("gpsimd", lambda h: h.memset(self.onesB[:], 1.0), tick="g")
        ph.run()

    def p_quant_weights(self, w_ap, rows, sc_out, tag, cols=D):
        nc = self.nc
        nrt, nct = rows // 128, cols // 128
        with ExitStack() as es:
            q = self.mk_qtmp(es, cols)
            wt = self.dbuf(es, "qw_wt", 2, cols, F32)
            wq = self.dbuf(es, "qw_wq", 2, cols, BF16)
            stw = self.dbuf(es, "qw_st", 2, cols, BF16)
            ph = Phase(self, "qw_" + tag)
            sc3 = sc_out.rearrange("p (ct r) -> p ct r", r=rows)
            vtk, iotk, stk = {}, {}, {}
            for rt in range(nrt):
                ld = ph.j("sync", lambda h, rt=rt: h.dma_start(
                    out=wt[rt % 2][:], in_=w_ap[rt * 128:(rt + 1) * 128, :]),
                    waits=[("v", vtk.get(rt - 2, 0)),
                           ("sc", stk.get(rt - 2, 0))], tick="io", n=16)
                vh = ph.j("vector", lambda h, rt=rt: self.quant_head(
                    h, wt[rt % 2][:, 0:cols], q, cols),
                    waits=[("io", ld), ("io", iotk.get(rt - 2, 0))], tick="v")
                stk[rt] = ph.j("scalar", lambda h, rt=rt: self.quant_sign(
                    h, wt[rt % 2], q, cols, 0, cols),
                    waits=[("io", ld), ("v", vtk.get(rt - 1, 0))], tick="sc")
                rcq = ph.j("scalar", lambda h: self.quant_recip(h, q, cols),
                           waits=[("v", vh)], tick="sc")
                ph.j("vector", lambda h, rt=rt: self.quant_steps(
                    h, wt[rt % 2][:, 0:cols], q, cols, 0, cols),
                    waits=[("sc", rcq)], tick="v")
                vtk[rt] = ph.j("vector", lambda h, rt=rt: self.quant_fin(
                    h, wq[rt % 2][:, 0:cols], q, cols, 0, cols),
                    waits=[("sc", stk[rt])], tick="v")
                for ct in range(nct):
                    ph.j("sync", lambda h, rt=rt, ct=ct: h.dma_start_transpose(
                        stw[rt % 2][:, ct * 128:(ct + 1) * 128],
                        wq[rt % 2][:, ct * 128:(ct + 1) * 128]),
                        waits=[("v", vtk[rt])], tick="io", n=16)
                iotk[rt] = ph.j("sync", lambda h, rt=rt: h.dma_start(
                    out=sc3[:, 0:nct, rt * 128:(rt + 1) * 128],
                    in_=stw[rt % 2].rearrange("p (ct j) -> p ct j", j=128)),
                    tick="io", n=16)
            ph.run()

    def p_fold_wqkvb(self):
        nc = self.nc
        with ExitStack() as es:
            wf = self.dbuf(es, "fb_wf", 2, 3 * D, F32)
            wb = self.dbuf(es, "fb_wb", 2, 3 * D, BF16)
            btc = es.enter_context(nc.sbuf_tensor(self.uname("fb_bt"), [128, NDC], BF16))
            ph = Phase(self, "foldb")
            bt0 = ph.j("vector", lambda h: h.tensor_copy(btc[:], self.bet[:]),
                       tick="v")
            cast_tk, fold_tk, st_tk, mm_tk = {}, {}, {}, {}
            for dc in range(NDC):
                ld = ph.j("sync", lambda h, dc=dc: h.dma_start(
                    out=wf[dc % 2][:], in_=self.wqkvT_b[dc * 128:(dc + 1) * 128, :]),
                    waits=[("v", cast_tk.get(dc - 2, 0))], tick="io", n=16)
                cast_tk[dc] = ph.j("vector", lambda h, dc=dc: h.tensor_copy(
                    wb[dc % 2][:], wf[dc % 2][:]),
                    waits=[("io", ld), ("io", st_tk.get(dc - 2, 0))], tick="v")
                def mmfn(h, dc=dc):
                    last = None
                    for ec in range(NEC):
                        last = h.matmul(self.ps[7][0:128, ec:ec + 1],
                                        wb[dc % 2][:, ec * 128:(ec + 1) * 128],
                                        btc[:, dc:dc + 1],
                                        start=(dc == 0 and ec == 0),
                                        stop=(dc == NDC - 1 and ec == NEC - 1))
                    return last
                mm_tk[dc] = ph.j("tensor", mmfn,
                                 waits=[("v", cast_tk[dc]), ("v", bt0)], tick="mm")
                fold_tk[dc] = ph.j("vector", lambda h, dc=dc: h.tensor_scalar_mul(
                    wb[dc % 2][:], wb[dc % 2][:], self.gam[:, dc:dc + 1]),
                    waits=[("mm", mm_tk[dc])], tick="v")
                st_tk[dc] = ph.j("sync", lambda h, dc=dc: h.dma_start(
                    out=self.sc_wqkvb[:, dc * 3 * D:(dc + 1) * 3 * D],
                    in_=wb[dc % 2][:]), waits=[("v", fold_tk[dc])], tick="io", n=16)
            ph.j("vector", lambda h: h.tensor_copy(self.betW[:, 0:NEC],
                                                   self.ps[7][:, 0:NEC]),
                 waits=[("mm", mm_tk[NDC - 1])], tick="v")
            ph.run()

    def p_build_Ttab(self):
        nc = self.nc
        with ExitStack() as es:
            t1h = es.enter_context(nc.sbuf_tensor(self.uname("t1h"), [128, NT * 16], F32))
            t1l = es.enter_context(nc.sbuf_tensor(self.uname("t1l"), [128, NT * 16], F32))
            u1h = es.enter_context(nc.sbuf_tensor(self.uname("u1h"), [128, NT * 16], F32))
            u1l = es.enter_context(nc.sbuf_tensor(self.uname("u1l"), [128, NT * 16], F32))
            t2h = es.enter_context(nc.sbuf_tensor(self.uname("t2h"), [128, NT * 128], F32))
            t2l = es.enter_context(nc.sbuf_tensor(self.uname("t2l"), [128, NT * 128], F32))
            u2h = es.enter_context(nc.sbuf_tensor(self.uname("u2h"), [128, NT * 128], F32))
            u2l = es.enter_context(nc.sbuf_tensor(self.uname("u2l"), [128, NT * 128], F32))
            stg = self.dbuf(es, "stg", 2, 2 * T, BF16)
            ta = es.enter_context(nc.sbuf_tensor(self.uname("tta"), [128, 128], F32))
            tb = es.enter_context(nc.sbuf_tensor(self.uname("ttb"), [128, 128], F32))
            ph = Phase(self, "ttab")
            lv = 0
            for src, dst in [(self.c1_hi, t1h), (self.c1_lo, t1l),
                             (self.s1_hi, u1h), (self.s1_lo, u1l),
                             (self.c2_hi, t2h), (self.c2_lo, t2l),
                             (self.s2_hi, u2h), (self.s2_lo, u2l)]:
                lv = ph.j("sync", lambda h, d=dst, s=src:
                          h.dma_start(out=d[:], in_=s[:]), tick="io", n=16)

            def dd(h):
                h.tensor_add(t1h[:], t1h[:], t1l[:])
                h.tensor_add(u1h[:], u1h[:], u1l[:])
                h.tensor_add(t2h[:], t2h[:], t2l[:])
                return h.tensor_add(u2h[:], u2h[:], u2l[:])
            ph.j("vector", dd, waits=[("io", lv)], tick="v")
            vtk, iotk = {}, {}
            for tt in range(NT):
                def bt(h, tt=tt):
                    b = stg[tt % 2]
                    c1 = t1h[:, tt * 16:(tt + 1) * 16]
                    s1 = u1h[:, tt * 16:(tt + 1) * 16]
                    c2 = t2h[:, tt * 128:(tt + 1) * 128]
                    s2 = u2h[:, tt * 128:(tt + 1) * 128]
                    last = None
                    for sc in range(16):
                        h.tensor_scalar_mul(ta[:], c2[:], c1[:, sc:sc + 1])
                        h.tensor_scalar_mul(tb[:], s2[:], s1[:, sc:sc + 1])
                        h.tensor_sub(b[:, sc * 128:(sc + 1) * 128], ta[:], tb[:])
                        h.tensor_scalar_mul(ta[:], c2[:], s1[:, sc:sc + 1])
                        h.tensor_scalar_mul(tb[:], s2[:], c1[:, sc:sc + 1])
                        last = h.tensor_add(b[:, T + sc * 128:T + (sc + 1) * 128],
                                            ta[:], tb[:])
                    return last
                vtk[tt] = ph.j("vector", bt, waits=[("io", iotk.get(tt - 2, 0))],
                               tick="v")
                ph.j("sync", lambda h, tt=tt: h.dma_start(
                    out=self.sc_cosT[:, tt * T:(tt + 1) * T],
                    in_=stg[tt % 2][:, 0:T]), waits=[("v", vtk[tt])],
                    tick="io", n=16)
                iotk[tt] = ph.j("sync", lambda h, tt=tt: h.dma_start(
                    out=self.sc_sinT[:, tt * T:(tt + 1) * T],
                    in_=stg[tt % 2][:, T:2 * T]), tick="io", n=16)
            ph.run()

    def p_xload(self, sq):
        ph = Phase(self, f"xl{sq}")
        ph.j("sync", lambda h: h.dma_start(
            out=self.x_sb.rearrange("p (tt d) -> p tt d", d=D),
            in_=self.x_in[sq].rearrange("(tt p) d -> p tt d", p=128)),
            tick="io", n=16)
        ph.run()

    def p_store(self, sq):
        ph = Phase(self, f"st{sq}")
        ph.j("sync", lambda h: h.dma_start(
            out=self.y_out[sq].rearrange("(tt p) d -> p tt d", p=128),
            in_=self.x_sb.rearrange("p (tt d) -> p tt d", d=D)),
            tick="io", n=16)
        ph.run()

    def ln_block(self, ph, tiles, xlnT, es, double=False, span=None, tmp2=None,
                 tmp=None):
        nc = self.nc
        span = span or len(tiles) * 128
        if double and tmp is None:
            tmp = es.enter_context(
                nc.sbuf_tensor(self.uname("ln_tmp"), [128, D], F32))
        if tmp2 is None:
            tmp2 = es.enter_context(nc.sbuf_tensor(self.uname("ln_tm2"), [128, D], F32))
        sums = es.enter_context(nc.sbuf_tensor(self.uname("ln_sums"), [128, 8], F32))
        xb = self.dbuf(es, "ln_xb", 2, D, BF16)

        def stats(h, src, c0):
            h.drain()
            h.tensor_tensor(tmp2[:, 0:D], src, src, Alu.mult)
            h.reduce_sum(sums[:, c0:c0 + 1], src, AX.X)
            h.reduce_sum(sums[:, c0 + 1:c0 + 2], tmp2[:, 0:D], AX.X)
            h.drain()
            h.scalar_tensor_tensor(sums[:, c0 + 2:c0 + 3], sums[:, c0:c0 + 1],
                                   1.0 / (D * D), sums[:, c0:c0 + 1],
                                   Alu.mult, Alu.mult)
            h.tensor_scalar_mul(sums[:, c0:c0 + 1], sums[:, c0:c0 + 1], 1.0 / D)
            h.tensor_scalar_mul(sums[:, c0 + 1:c0 + 2], sums[:, c0 + 1:c0 + 2],
                                1.0 / D)
            h.drain()
            return h.tensor_scalar(sums[:, c0 + 1:c0 + 2], sums[:, c0 + 1:c0 + 2],
                                   sums[:, c0 + 2:c0 + 3], 1e-5,
                                   Alu.subtract, Alu.add)

        sc_last, io_tk = {}, {}
        for i, tt in enumerate(tiles):
            xs = self.x_sb[:, tt * D:(tt + 1) * D]
            v1 = ph.j("vector", lambda h, xs=xs: stats(h, xs, 0),
                      waits=[("sc", sc_last.get(i - 1, 0)),
                             ("io", io_tk.get(i - 2, 0))], tick="v")
            s1 = ph.j("scalar", lambda h: h.activation(sums[:, 2:3], sums[:, 1:2],
                                                       Act.Sqrt),
                      waits=[("v", v1)], tick="sc")
            if not double:
                sc_last[i] = s1
                def ap1(h, i=i, xs=xs):
                    h.reciprocal(sums[:, 3:4], sums[:, 2:3])
                    h.drain()
                    return h.tensor_scalar(xb[i % 2][:], xs, sums[:, 0:1],
                                           sums[:, 3:4], Alu.subtract, Alu.mult)
                vfin = ph.j("vector", ap1, waits=[("sc", s1)], tick="v")
            else:
                def ap1(h, i=i, xs=xs):
                    h.reciprocal(sums[:, 3:4], sums[:, 2:3])
                    h.drain()
                    h.tensor_scalar(tmp[:, 0:D], xs, sums[:, 0:1], sums[:, 3:4],
                                    Alu.subtract, Alu.mult)
                    return stats(h, tmp[:, 0:D], 4)
                v2 = ph.j("vector", ap1, waits=[("sc", s1)], tick="v")
                s2 = ph.j("scalar", lambda h: h.activation(
                    sums[:, 6:7], sums[:, 5:6], Act.Sqrt),
                    waits=[("v", v2)], tick="sc")
                sc_last[i] = s2
                def ap2(h, i=i):
                    h.reciprocal(sums[:, 7:8], sums[:, 6:7])
                    h.drain()
                    return h.tensor_scalar(xb[i % 2][:], tmp[:, 0:D], sums[:, 4:5],
                                           sums[:, 7:8], Alu.subtract, Alu.mult)
                vfin = ph.j("vector", ap2, waits=[("sc", s2)], tick="v")
            for dc in range(NDC):
                io_tk[i] = ph.j("sync", lambda h, i=i, dc=dc: h.dma_start_transpose(
                    xlnT[:, dc * span + i * 128:dc * span + (i + 1) * 128],
                    xb[i % 2][:, dc * 128:(dc + 1) * 128]),
                    waits=[("v", vfin)], tick="io", n=16)

    def qkv_phase(self, ph, nec_list, wload_fn, rhs_fn, evac_fn, toks,
                  weng="gpsimd", wsem="w", kv_hook=None):
        ntc = toks // 512
        mm_tk, ev_tk, w_tk = {}, {}, {}
        for gi, ec in enumerate(nec_list):
            w_tk[gi] = ph.j(weng, lambda h, ec=ec, gi=gi: wload_fn(h, ec, gi % 2),
                            waits=[("mm", mm_tk.get((gi - 2, ntc - 1), 0))],
                            tick=wsem, n=16)
            for c in range(ntc):
                j = gi * ntc + c
                bank = self.ps[j % 4]
                def mmf(h, ec=ec, gi=gi, c=c, bank=bank):
                    last = None
                    for dc in range(NDC):
                        last = h.matmul(bank[:, 0:512],
                                        self.wbuf[gi % 2][:, dc * 128:(dc + 1) * 128],
                                        rhs_fn(ec, c, dc),
                                        start=(dc == 0), stop=(dc == NDC - 1))
                    return last
                mm_tk[(gi, c)] = ph.j("tensor", mmf,
                                      waits=[(wsem, w_tk[gi]),
                                             ("ev", ev_tk.get(j - 4, 0))],
                                      tick="mm")
                ev_tk[j] = ph.j("scalar", lambda h, ec=ec, c=c, bank=bank:
                                evac_fn(h, ec, c, bank),
                                waits=[("mm", mm_tk[(gi, c)])], tick="ev")
            if kv_hook is not None:
                kv_hook(ph, gi, ec, ev_tk[gi * ntc + ntc - 1])
        return mm_tk, ev_tk

    def p_attn_a(self, sq):
        nc = self.nc
        with ExitStack() as esy:
            big = esy.enter_context(nc.sbuf_tensor(self.uname("big0"), [128, NDC * T], BF16))
            yT = big
            with ExitStack() as esa:
                qT = esa.enter_context(nc.sbuf_tensor(self.uname("qT"), [128, NDC * T], BF16))
                kcT = esa.enter_context(nc.sbuf_tensor(self.uname("kcT"), [128, NDC * CK], BF16))
                vna = esa.enter_context(nc.sbuf_tensor(self.uname("vna"),
                                                       [128, 4 * 1040], BF16))
                cw = esa.enter_context(nc.sbuf_tensor(self.uname("cw"), [128, 4 * D], BF16))
                mkA = esa.enter_context(nc.sbuf_tensor(self.uname("mkA"), [128, 512], BF16))
                with ExitStack() as esb:
                    xlnT = big
                    vcT = esb.enter_context(
                        nc.sbuf_tensor(self.uname("vcT"), [128, NDC * CK], BF16))
                    self.wbuf = self.dbuf(esb, "wbuf", 2, 1024, BF16)
                    kvt = self.dbuf(esb, "kvt", 2, T, BF16)
                    with ExitStack() as esl:
                        ph = Phase(self, f"a{sq}_ln1")
                        ph.j("gpsimd", lambda h: h.dma_start(out=cw[:],
                                                             in_=self.convw_i[:]),
                             tick="w", n=16)
                        ph.j("sync", lambda h: h.dma_start(out=mkA[:],
                                                           in_=self.maskA_i[:]),
                             tick="io", n=16)
                        self.ln_block(ph, list(range(NT)), xlnT, esl,
                                      tmp2=kvt[0].bitcast(F32)[:, 0:D])
                        ph.run()
                    if sq == 0:
                        self.maybe_probe("xlnT", xlnT[:])
                    self.p_qkv_a_body(sq, xlnT, kvt, qT, kcT, vcT, cw)
                    if sq == 0:
                        self.maybe_probe("qT", qT[:])
                        self.maybe_probe("kcT", kcT[:])
                        self.maybe_probe("vcT", vcT[:])
                    self.p_vnat_a(sq, vcT, vna)
                    if sq == 0:
                        self.maybe_probe("vna", vna[:])
                with ExitStack() as esn:
                    q2 = self.dbuf(esn, "q2", 2, T, BF16)
                    nk2 = esn.enter_context(nc.sbuf_tensor(self.uname("nk2"), [128, NH], BF16))
                    self.hn = esn.enter_context(nc.sbuf_tensor(self.uname("hn"), [16, T], BF16))
                    ph = Phase(self, f"a{sq}_hn")
                    self.hnorm_common(ph, qT, T, 1.0 / 32.0, q2, kcT, CK, nk2=nk2)
                    ph.run()
                    if sq == 0:
                        self.maybe_probe("negc", self.negc[:, 0:16])
                        self.maybe_probe("smal", self.smal[0:128, 0:96])
                        self.maybe_probe("hn", self.hn[0:16, 0:T])
                with ExitStack() as esc:
                    outT = esc.enter_context(
                        nc.sbuf_tensor(self.uname("outT"), [128, NDC * T], BF16))
                    pb = self.dbuf(esc, "pb", 2, 512, BF16)
                    pn = esc.enter_context(nc.sbuf_tensor(self.uname("pn"), [128, 512], BF16))
                    self.rd = esc.enter_context(nc.sbuf_tensor(self.uname("rd"), [128, 512], F32))
                    self.rdb = esc.enter_context(nc.sbuf_tensor(self.uname("rdb"), [128, 512], F32))
                    self._core_es = esc
                    self.p_attn_core(sq, "a", qT, kcT, vna, outT, pb, pn, mkA)
                    if sq == 0:
                        self.maybe_probe("outT", outT[:])
                    with ExitStack() as esd:
                        wcol = self.dbuf(esd, "wcol", 2, 1024, BF16)
                        self.p_proj_a(sq, outT, wcol, yT)
            if sq == 0:
                self.maybe_probe("yT", yT[:])
            self.p_fft(sq, yT)

    def p_qkv_a_body(self, sq, xlnT, kvt, qT, kcT, vcT, cw):
        ph = Phase(self, f"a{sq}_qkv")
        wq3 = self.wqkv_a.rearrange("(dc p) e -> p dc e", p=128)

        def wload(h, ec, buf):
            return h.dma_start(
                out=self.wbuf[buf].rearrange("p (dc j) -> p dc j", j=128),
                in_=wq3[:, :, ec * 128:(ec + 1) * 128])

        def rhs(ec, c, dc):
            return xlnT[:, dc * T + c * 512:dc * T + c * 512 + 512]

        def evac(h, ec, c, bank):
            if ec < 8:
                return h.copy(qT[:, ec * T + c * 512:ec * T + c * 512 + 512],
                              bank[:, 0:512])
            return h.copy(kvt[ec % 2][:, c * 512:c * 512 + 512], bank[:, 0:512])

        conv_ev = {}
        conv_n = [0]

        def kv_hook(ph, gi, ec, evlast):
            if ec < 8:
                return
            is_v = ec >= 16
            c2 = ec - 16 if is_v else ec - 8
            for gl in range(2):
                gs = conv_n[0]
                conv_n[0] += 1
                grp = 2 * c2 + gl
                po = 64 * gl
                bank = self.ps[4 + gs % 2]
                def cmm(h, ec=ec, grp=grp, po=po, bank=bank):
                    last = None
                    for kap in range(CF):
                        last = h.matmul(
                            bank[po:po + 64, 0:512],
                            cw[po:po + 64,
                               kap * D + grp * 64:kap * D + (grp + 1) * 64],
                            kvt[ec % 2][po:po + 64, 0:T].rearrange(
                                "p (j k) -> p j k", k=CF)[:, :, kap],
                            start=(kap == 0), stop=(kap == CF - 1),
                            tile_position=(po, po))
                    return last
                cm = ph.j("tensor", cmm,
                          waits=[("ev", evlast), ("v", conv_ev.get(gs - 2, 0)),
                                 ("w", self.sv["w"])],
                          tick="mm")
                img = vcT if is_v else kcT
                def cev(h, c2=c2, po=po, bank=bank, img=img):
                    dst = img[po:po + 64, c2 * CK:(c2 + 1) * CK]
                    return h.tensor_scalar_add(dst, bank[po:po + 64, 0:512],
                                               self.convb[po:po + 64, c2:c2 + 1])
                conv_ev[gs] = ph.j("vector", cev, waits=[("mm", cm)], tick="v")

        self.qkv_phase(ph, list(range(NEC)), wload, rhs, evac, T, kv_hook=kv_hook)
        ph.run()

    def p_vnat_a(self, sq, vcT, vna):
        ph = Phase(self, f"a{sq}_vn")
        on = ph.j("gpsimd", lambda h: h.memset(
            vna.rearrange("p (k o) -> p k o", o=65)[:, :, 64:65], 1.0), tick="g")
        tk = {}
        for c2 in range(NDC):
            for kt in range(4):
                i = c2 * 4 + kt
                bank = self.ps[i % 2]
                m = ph.j("tensor", lambda h, c2=c2, kt=kt, bank=bank: h.transpose(
                    bank[:, 0:64].bitcast(BF16),
                    vcT[:, c2 * CK + kt * 128:c2 * CK + (kt + 1) * 128],
                    self.ident[:]),
                    waits=[("v", tk.get(i - 2, 0)), ("g", on)], tick="mm")
                def cpy(h, c2=c2, kt=kt, bank=bank):
                    h.tensor_copy(
                        vna[:, kt * 1040 + (2 * c2) * 65:
                            kt * 1040 + (2 * c2) * 65 + 64],
                        bank[:, 0:64].bitcast(BF16)[:, 0:64])
                    return h.tensor_copy(
                        vna[:, kt * 1040 + (2 * c2 + 1) * 65:
                            kt * 1040 + (2 * c2 + 1) * 65 + 64],
                        bank[:, 0:64].bitcast(BF16)[:, 64:128])
                tk[i] = ph.j("vector", cpy, waits=[("mm", m)], tick="v")
        ph.run()

    def hnorm_common(self, ph, qT, qlen, scale, q2, kimg, klen, nk2=None):
        nqc, nkc = qlen // 512, klen // 512
        sq_tk, mm_tk = {}, {}
        for dc in range(NDC):
            sq_tk[dc] = ph.j("scalar", lambda h, dc=dc: h.activation(
                q2[dc % 2][:, 0:qlen], qT[:, dc * qlen:(dc + 1) * qlen], Act.Square),
                waits=[("mm", mm_tk.get((dc - 2, nqc - 1), 0))], tick="sc")
            for c in range(nqc):
                mm_tk[(dc, c)] = ph.j("tensor", lambda h, dc=dc, c=c: h.matmul(
                    self.ps[c][0:16, 0:512], self.hlhs[:, dc * NH:(dc + 1) * NH],
                    q2[dc % 2][:, c * 512:(c + 1) * 512],
                    start=(dc == 0), stop=(dc == NDC - 1)),
                    waits=[("sc", sq_tk[dc])], tick="mm")
        ev = {}
        for c in range(nqc):
            ev[c] = ph.j("scalar", lambda h, c=c: h.activation(
                self.hn[0:16, c * 512:(c + 1) * 512], self.ps[c][0:16, 0:512],
                Act.Sqrt), waits=[("mm", mm_tk[(NDC - 1, nqc - 1)])], tick="sc")
        def vqf(h):
            h.reduce_max(self.smal[0:16, 0:1], self.hn[0:16, 0:qlen], AX.X)
            return h.drain()
        vq = ph.j("vector", vqf, waits=[("sc", ev[nqc - 1])], tick="v")
        mm2, s2_tk = {}, {}
        for dc in range(NDC):
            s2_tk[dc] = ph.j("scalar", lambda h, dc=dc: h.activation(
                q2[dc % 2][:, 0:klen], kimg[:, dc * klen:(dc + 1) * klen],
                Act.Square),
                waits=[("v", vq), ("mm", mm2.get((dc - 2, nkc - 1), 0))], tick="sc")
            for c in range(nkc):
                mm2[(dc, c)] = ph.j("tensor", lambda h, dc=dc, c=c: h.matmul(
                    self.ps[4 + c][0:16, 0:512], self.hlhs[:, dc * NH:(dc + 1) * NH],
                    q2[dc % 2][:, c * 512:(c + 1) * 512],
                    start=(dc == 0), stop=(dc == NDC - 1)),
                    waits=[("sc", s2_tk[dc])], tick="mm")
        ev2 = {}
        for c in range(nkc):
            ev2[c] = ph.j("scalar", lambda h, c=c: h.activation(
                self.hn[0:16, c * 512:(c + 1) * 512], self.ps[4 + c][0:16, 0:512],
                Act.Sqrt), waits=[("mm", mm2[(NDC - 1, nkc - 1)])], tick="sc")
        def vkf(h):
            h.reduce_max(self.smal[0:16, 1:2], self.hn[0:16, 0:klen], AX.X)
            return h.drain()
        vk = ph.j("vector", vkf, waits=[("sc", ev2[nkc - 1])], tick="v")
        lastv = vk
        if nk2 is not None:
            s3 = ph.j("scalar", lambda h: h.activation(
                nk2[0:64, :], self.nullk[0:64, :], Act.Square), tick="sc")
            m3 = ph.j("tensor", lambda h: h.matmul(
                self.ps[6][0:16, 0:1], nk2[0:64, 0:16], self.onesc[0:64, 0:1],
                start=True, stop=True), waits=[("sc", s3), ("v", vk)], tick="mm")
            s4 = ph.j("scalar", lambda h: h.activation(
                self.smal[0:16, 2:3], self.ps[6][0:16, 0:1], Act.Sqrt),
                waits=[("mm", m3)], tick="sc")
            def vmx(h):
                h.tensor_max(self.smal[0:16, 1:2], self.smal[0:16, 1:2],
                             self.smal[0:16, 2:3])
                return h.drain()
            lastv = ph.j("vector", vmx, waits=[("sc", s4)], tick="v")

        def fin(h, scale=scale):
            h.tensor_tensor(self.smal[0:16, 3:4], self.smal[0:16, 0:1],
                            self.smal[0:16, 1:2], Alu.mult)
            h.drain()
            h.tensor_scalar_mul(self.smal[0:16, 3:4],
                                self.smal[0:16, 3:4], -scale)
            return h.drain()
        vf = ph.j("vector", fin, waits=[("v", lastv)], tick="v")
        dg = ph.j("gpsimd", lambda h: h.dma_start(
            out=self.smal[0:1, 64:80], in_=self.smal[0:16, 3:4]),
            waits=[("v", vf)], tick="w", n=16)
        bm = ph.j("tensor", lambda h: h.matmul(
            self.ps[7][0:128, 0:16], self.onesF[0:1, 0:128],
            self.smal[0:1, 64:80], start=True, stop=True),
            waits=[("w", dg)], tick="mm")
        ph.j("vector", lambda h: h.tensor_copy(self.negc[:, 0:16],
                                               self.ps[7][0:128, 0:16]),
             waits=[("mm", bm)], tick="v")

    def hnorm_b(self, ph, qT, kT, q2, hnc, knb):
        sq_tk, mm_tk = {}, {}
        for dc in range(NDC):
            sq_tk[dc] = ph.j("scalar", lambda h, dc=dc: h.activation(
                q2[dc % 2][:, 0:T], qT[:, dc * T:(dc + 1) * T], Act.Square),
                waits=[("mm", mm_tk.get((dc - 2, NTC - 1), 0))], tick="sc")
            for c in range(NTC):
                def qmm(h, dc=dc, c=c):
                    h.matmul(self.ps[c][0:16, 0:512],
                             self.hlhs[:, dc * NH:(dc + 1) * NH],
                             q2[dc % 2][:, c * 512:(c + 1) * 512],
                             start=(dc == 0), stop=(dc == NDC - 1),
                             tile_position=(0, 0))
                    return h.matmul(self.ps[4 + c][64:80, 0:512],
                                    self.hlhs[:, dc * NH:(dc + 1) * NH],
                                    q2[dc % 2][:, c * 512:(c + 1) * 512],
                                    start=(dc == 0), stop=(dc == NDC - 1),
                                    tile_position=(0, 64))
                mm_tk[(dc, c)] = ph.j("tensor", qmm, waits=[("sc", sq_tk[dc])],
                                      tick="mm")
        ev = {}
        for c in range(NTC):
            def qev(h, c=c):
                h.activation(hnc[0:16, c * 512:(c + 1) * 512],
                             self.ps[c][0:16, 0:512], Act.Sqrt)
                return h.activation(hnc[64:80, c * 512:(c + 1) * 512],
                                    self.ps[4 + c][64:80, 0:512], Act.Sqrt)
            ev[c] = ph.j("scalar", qev,
                         waits=[("mm", mm_tk[(NDC - 1, NTC - 1)])], tick="sc")
        mm2, s2_tk = {}, {}
        for dc in range(NDC):
            s2_tk[dc] = ph.j("scalar", lambda h, dc=dc: h.activation(
                q2[dc % 2][:, 0:T], kT[:, dc * T:(dc + 1) * T], Act.Square),
                waits=[("mm", mm2.get((dc - 2, NTC - 1), 0)),
                       ("sc", ev[NTC - 1])], tick="sc")
            for c in range(NTC):
                def kmm(h, dc=dc, c=c):
                    h.matmul(self.ps[c][0:16, 0:512],
                             self.hlhs[:, dc * NH:(dc + 1) * NH],
                             q2[dc % 2][:, c * 512:(c + 1) * 512],
                             start=(dc == 0), stop=(dc == NDC - 1),
                             tile_position=(0, 0))
                    return h.matmul(self.ps[4 + c][64:80, 0:512],
                                    self.hlhs[:, dc * NH:(dc + 1) * NH],
                                    q2[dc % 2][:, c * 512:(c + 1) * 512],
                                    start=(dc == 0), stop=(dc == NDC - 1),
                                    tile_position=(0, 64))
                mm2[(dc, c)] = ph.j("tensor", kmm,
                                    waits=[("sc", s2_tk[dc]),
                                           ("sc", ev[c])], tick="mm")
        ev2 = {}
        for c in range(NTC):
            def kev(h, c=c):
                h.activation(knb[0:16, c * 512:(c + 1) * 512],
                             self.ps[c][0:16, 0:512], Act.Sqrt)
                return h.activation(knb[64:80, c * 512:(c + 1) * 512],
                                    self.ps[4 + c][64:80, 0:512], Act.Sqrt)
            ev2[c] = ph.j("scalar", kev,
                          waits=[("mm", mm2[(NDC - 1, NTC - 1)])], tick="sc")
        def bmx(h):
            h.drain()
            h.tensor_reduce(self.smal[0:80, 32:48],
                            knb[0:80, 0:T].rearrange("p (b j) -> p b j", j=128),
                            AX.X, Alu.max, negate=True)
            h.drain()
            for B in range(1, 16):
                h.tensor_tensor(self.smal[0:80, 32 + B:33 + B],
                                self.smal[0:80, 32 + B:33 + B],
                                self.smal[0:80, 31 + B:32 + B], Alu.min)
                h.drain()
            bc = self.smal[0:80, 32:48].unsqueeze(2).broadcast_to([80, 16, 128])
            return h.tensor_tensor(
                hnc[0:80, 0:T].rearrange("p (b j) -> p b j", j=128),
                hnc[0:80, 0:T].rearrange("p (b j) -> p b j", j=128),
                bc, Alu.mult)
        ph.j("vector", bmx, waits=[("sc", ev2[NTC - 1])], tick="v")

    def p_attn_core(self, sq, kind, qT, kT, vna, outT, pb, pn, mask,
                    proj_cb=None):
        scale = (1.0 / 32.0) if kind == "a" else 0.125
        kspan = CK if kind == "a" else T
        ph = Phase(self, f"{kind}{sq}_att")
        rdm = self.rd[0:64, 0:256].bitcast(BF16)
        rnd = 0
        nrm_tk, gb_tk, rcp_tk, proj_tk = {}, {}, {}, {}
        shf_tk = {}
        sb_last = {}
        pb_last = {}
        nul_exp_last = [0]
        nav_last = [0]
        ks_glob = [0]
        for c in range(NTC):
            for h_ in range(NH):
                r = rnd
                rnd += 1
                gl = h_ % 2
                po = 64 * gl
                hc = h_ // 2
                pob = self.ps[2 + r % 2]
                kts = list(range(c + 1)) if kind == "a" else list(range(4 * c + 4))
                av_tk = {}
                first_set = False
                if kind == "a":
                    def nsc(h, h_=h_, po=po, c=c):
                        rq = qT[po:po + 64, hc * T + c * 512:hc * T + c * 512 + 512] \
                            if False else qT[po:po + 64,
                                             (h_ // 2) * T + c * 512:
                                             (h_ // 2) * T + c * 512 + 512]
                        return h.matmul(self.ps[4][0:1, 0:512],
                                        self.nullk[po:po + 64, h_:h_ + 1], rq,
                                        start=True, stop=True,
                                        tile_position=(po, 0))
                    m = ph.j("tensor", nsc,
                             waits=[("sc", nul_exp_last[0])], tick="mm")
                    en = ph.j("scalar", lambda h, h_=h_: h.activation(
                        pn[0:1, 0:512], self.ps[4][0:1, 0:512], Act.Exp,
                        bias=self.negc[0:1, h_:h_ + 1], scale=scale),
                        waits=[("mm", m), ("mm", nav_last[0])], tick="sc")
                    nul_exp_last[0] = en
                    def nav(h, h_=h_, pob=pob):
                        return h.matmul(pob[0:65, 0:512],
                                        self.nullv[0:1, h_ * 65:(h_ + 1) * 65],
                                        pn[0:1, 0:512], start=True, stop=False,
                                        tile_position=(0, 0))
                    nav_last[0] = ph.j("tensor", nav,
                                       waits=[("sc", en),
                                              ("v", nrm_tk.get(r - 2, 0)),
                                              ("mm", shf_tk.get(r - 2, 0))],
                                       tick="mm")
                    first_set = True
                for ki, kt in enumerate(kts):
                    ks = ks_glob[0]
                    ks_glob[0] += 1
                    boundary = (kind == "a" and kt == c) or \
                               (kind == "b" and kt >= 4 * c)
                    sbi = ks % 2
                    sb_ = self.ps[sbi]
                    def smm(h, h_=h_, po=po, kt=kt, c=c, sb_=sb_):
                        lk = kT[po:po + 64, (h_ // 2) * kspan + kt * 128:
                                (h_ // 2) * kspan + (kt + 1) * 128]
                        rq = qT[po:po + 64, (h_ // 2) * T + c * 512:
                                (h_ // 2) * T + c * 512 + 512]
                        mm = h.matmul(sb_[0:128, 0:512], lk, rq,
                                      start=True, stop=(kind == "a"),
                                      tile_position=(po, 0))
                        if kind == "a":
                            return mm
                        return h.matmul(
                            sb_[0:128, 0:512],
                            self.rowsel[po:po + 16, h_ * 128:(h_ + 1) * 128],
                            self.hnc[po:po + 16,
                                     c * 512:c * 512 + 512],
                            start=False, stop=True, tile_position=(po, 0))
                    m = ph.j("tensor", smm,
                             waits=[("sc", sb_last.get(sbi, 0))], tick="mm")
                    wl = [("mm", m)]
                    if boundary:
                        if kind == "a":
                            mk = mask[:, 0:512]
                        else:
                            off = 384 - 128 * (kt - 4 * c)
                            mk = mask[:, off:off + 512]
                        mv = ph.j("vector", lambda h, sb_=sb_, mk=mk:
                                  h.tensor_add(sb_[0:128, 0:512],
                                               sb_[0:128, 0:512], mk),
                                  waits=[("mm", m)], tick="v")
                        wl = [("v", mv)]
                    pbi = ks % 2
                    bias_ap = self.negc[:, h_:h_ + 1] if kind == "a" else 0.0
                    ex = ph.j("scalar", lambda h, sb_=sb_, pbi=pbi, bias_ap=bias_ap:
                              h.activation(pb[pbi][:, 0:512], sb_[0:128, 0:512],
                                           Act.Exp, bias=bias_ap, scale=scale),
                              waits=wl + [("mm", pb_last.get(pbi, 0))], tick="sc")
                    sb_last[sbi] = ex
                    st = (not first_set) and ki == 0
                    def av(h, h_=h_, kt=kt, pob=pob, st=st, pbi=pbi):
                        return h.matmul(
                            pob[0:65, 0:512],
                            vna[:, kt * 1040 + h_ * 65:kt * 1040 + (h_ + 1) * 65],
                            pb[pbi][:, 0:512], start=st, stop=False,
                            tile_position=(0, 0))
                    av_tk[ki] = ph.j("tensor", av,
                                     waits=[("sc", ex),
                                            ("v", nrm_tk.get(r - 2, 0)),
                                            ("mm", shf_tk.get(r - 2, 0))],
                                     tick="mm")
                    pb_last[pbi] = av_tk[ki]
                lastav = av_tk[len(kts) - 1]
                def rcp(h, pob=pob):
                    h.activation(self.rd[64:65, 0:512], pob[64:65, 0:512], Act.Ln)
                    h.drain()
                    return h.activation(self.rd[64:65, 0:256].bitcast(BF16),
                                        self.rd[64:65, 0:512], Act.Exp,
                                        scale=-1.0)
                rcp_tk[r] = ph.j("scalar", rcp, waits=[("mm", lastav)], tick="sc")
                gb_tk[r] = ph.j("tensor", lambda h: h.matmul(
                    self.ps[5][0:64, 0:512], self.onesB[64:65, 0:64],
                    self.rd[64:65, 0:256].bitcast(BF16), start=True, stop=True,
                    tile_position=(64, 0)),
                    waits=[("sc", rcp_tk[r]), ("v", nrm_tk.get(r - 1, 0))],
                    tick="mm")
                if proj_cb is None:
                    dste = outT[0:64, (h_ // 2) * T + c * 512:
                                (h_ // 2) * T + c * 512 + 512]
                    dsto = outT[64:128, (h_ // 2) * T + c * 512:
                                (h_ // 2) * T + c * 512 + 512]
                else:
                    dste = outT[0:64, (h_ // 2) * 512:(h_ // 2) * 512 + 512]
                    dsto = outT[64:128, (h_ // 2) * 512:(h_ // 2) * 512 + 512]
                if gl == 0:
                    def evn(h, dste=dste, pob=pob):
                        h.tensor_copy(self.rdb[0:64, 0:512],
                                      self.ps[5][0:64, 0:512])
                        h.drain()
                        return h.tensor_tensor(dste, pob[0:64, 0:512],
                                               self.rdb[0:64, 0:512], Alu.mult)
                    nrm_tk[r] = ph.j("vector", evn,
                                     waits=[("mm", gb_tk[r]), ("mm", lastav),
                                            ("v", proj_tk.get(c - 1, 0))],
                                     tick="v")
                    shf_tk[r] = 0
                else:
                    def evn2(h, pob=pob):
                        h.tensor_copy(self.rdb[0:64, 0:512],
                                      self.ps[5][0:64, 0:512])
                        h.drain()
                        return h.tensor_tensor(rdm[0:64, 0:512],
                                               pob[0:64, 0:512],
                                               self.rdb[0:64, 0:512], Alu.mult)
                    v1 = ph.j("vector", evn2,
                              waits=[("mm", gb_tk[r]), ("mm", lastav),
                                     ("v", proj_tk.get(c - 1, 0)),
                                     ("v", nrm_tk.get(r - 1, 0))], tick="v")
                    shf_tk[r] = ph.j("tensor", lambda h: h.matmul(
                        self.ps[6][64:128, 0:512], self.ident[0:64, 0:64],
                        rdm[0:64, 0:512], start=True, stop=True,
                        tile_position=(0, 64)),
                        waits=[("v", v1), ("v", nrm_tk.get(r - 2, 0))], tick="mm")
                    nrm_tk[r] = ph.j("vector", lambda h, dsto=dsto: h.tensor_copy(
                        dsto, self.ps[6][64:128, 0:512]),
                        waits=[("mm", shf_tk[r])], tick="v")
            if proj_cb is not None:
                proj_tk[c] = proj_cb(ph, c, nrm_tk[rnd - 1])
        ph.run()

    def p_proj_a(self, sq, outT, wcol, yT):
        ph = Phase(self, f"a{sq}_proj")
        wo3 = self.wout_a.rearrange("(ec p) j -> p ec j", p=128)
        mm_tk, ev_tk, wl = {}, {}, {}
        for jc in range(NDC):
            wl[jc] = ph.j("gpsimd", lambda h, jc=jc: h.dma_start(
                out=wcol[jc % 2].rearrange("p (ec j) -> p ec j", j=128),
                in_=wo3[:, :, jc * 128:(jc + 1) * 128]),
                waits=[("mm", mm_tk.get((jc - 2, NTC - 1), 0))], tick="w", n=16)
            for c in range(NTC):
                i = jc * NTC + c
                bank = self.ps[i % 4]
                def mmf(h, jc=jc, c=c, bank=bank):
                    last = None
                    for ec in range(NDC):
                        last = h.matmul(
                            bank[:, 0:512],
                            wcol[jc % 2][:, ec * 128:(ec + 1) * 128],
                            outT[:, ec * T + c * 512:ec * T + c * 512 + 512],
                            start=(ec == 0), stop=(ec == NDC - 1))
                    return last
                mm_tk[(jc, c)] = ph.j("tensor", mmf,
                                      waits=[("w", wl[jc]),
                                             ("ev", ev_tk.get(i - 4, 0))],
                                      tick="mm")
                ev_tk[i] = ph.j("scalar", lambda h, jc=jc, c=c, bank=bank:
                                h.activation(
                                    yT[:, jc * T + c * 512:jc * T + c * 512 + 512],
                                    bank[:, 0:512], Act.Identity,
                                    bias=self.bouta[:, jc:jc + 1]),
                                waits=[("mm", mm_tk[(jc, c)])], tick="ev")
        ph.run()

    def p_fft(self, sq, yT):
        nc = self.nc
        with ExitStack() as es:
            Aim = es.enter_context(nc.sbuf_tensor(self.uname("Aim"), [128, NT * 513], BF16))
            Bim = es.enter_context(nc.sbuf_tensor(self.uname("Bim"), [128, NT * 513], BF16))
            Bng = es.enter_context(nc.sbuf_tensor(self.uname("Bng"), [128, NT * 513], BF16))
            cosD = es.enter_context(nc.sbuf_tensor(self.uname("cosD"), [128, NDC * 513], BF16))
            sinD = es.enter_context(nc.sbuf_tensor(self.uname("sinD"), [128, NDC * 513], BF16))
            ct = self.dbuf(es, "ct", 4, 128, BF16)
            ph = Phase(self, f"f{sq}_dd")
            ah = Aim[:].bitcast(F32)[:, 0:NDC * 513]
            bh = Bim[:].bitcast(F32)[:, 0:NDC * 513]
            l1 = ph.j("sync", lambda h: h.dma_start(out=ah, in_=self.cosD_hi[:]),
                      tick="io", n=16)
            l2 = ph.j("sync", lambda h: h.dma_start(out=bh, in_=self.cosD_lo[:]),
                      tick="io", n=16)
            v1 = ph.j("vector", lambda h: h.tensor_add(cosD[:], ah, bh),
                      waits=[("io", l2)], tick="v")
            l3 = ph.j("sync", lambda h: h.dma_start(out=ah, in_=self.sinD_hi[:]),
                      waits=[("v", v1)], tick="io", n=16)
            l4 = ph.j("sync", lambda h: h.dma_start(out=bh, in_=self.sinD_lo[:]),
                      tick="io", n=16)
            ph.j("vector", lambda h: h.tensor_add(sinD[:], ah, bh),
                 waits=[("io", l4)], tick="v")
            ph.run()

            ph = Phase(self, f"f{sq}_1")
            ev_tk = {}
            for tt in range(NT):
                g0 = 3 * (tt % 2)
                pA, pB, pC = self.ps[g0], self.ps[g0 + 1], self.ps[g0 + 2]
                def mmf(h, tt=tt, pA=pA, pB=pB, pC=pC):
                    last = None
                    for dc in range(NDC):
                        lt = yT[:, dc * T + tt * 128:dc * T + (tt + 1) * 128]
                        st_ = (dc == 0)
                        sp = (dc == NDC - 1)
                        h.matmul(pA[:, 0:512], lt, cosD[:, dc * 513:dc * 513 + 512],
                                 start=st_, stop=sp)
                        h.matmul(pC[:, 0:1], lt,
                                 cosD[:, dc * 513 + 512:dc * 513 + 513],
                                 start=st_, stop=sp)
                        h.matmul(pB[:, 0:512], lt, sinD[:, dc * 513:dc * 513 + 512],
                                 start=st_, stop=sp)
                        last = h.matmul(pC[:, 1:2],
                                        lt, sinD[:, dc * 513 + 512:dc * 513 + 513],
                                        start=False, stop=sp)
                    return last
                m = ph.j("tensor", mmf, waits=[("ev", ev_tk.get(tt - 2, 0))],
                         tick="mm")
                def evs(h, tt=tt, pA=pA, pB=pB, pC=pC):
                    h.copy(Aim[:, tt * 513:tt * 513 + 512], pA[:, 0:512])
                    h.copy(Aim[:, tt * 513 + 512:tt * 513 + 513], pC[:, 0:1])
                    h.copy(Bim[:, tt * 513:tt * 513 + 512], pB[:, 0:512])
                    return h.copy(Bim[:, tt * 513 + 512:tt * 513 + 513], pC[:, 1:2])
                se = ph.j("scalar", evs, waits=[("mm", m)], tick="sc")
                def evv(h, tt=tt, pB=pB, pC=pC):
                    h.tensor_scalar_mul(Bng[:, tt * 513:tt * 513 + 512],
                                        pB[:, 0:512], -1.0)
                    return h.tensor_scalar_mul(
                        Bng[:, tt * 513 + 512:tt * 513 + 513], pC[:, 1:2], -1.0)
                ev_tk[tt] = ph.j("vector", evv, waits=[("mm", m), ("sc", se)],
                                 tick="ev")
            ph.run()

            ph = Phase(self, f"f{sq}_2")
            io_tk, mm_tk, v_tk = {}, {}, {}
            for stt in range(NT):
                pL = self.ps[6] if stt % 2 == 0 else self.ps[4]
                pH = self.ps[7] if stt % 2 == 0 else self.ps[5]
                for tc in range(NT):
                    i = stt * NT + tc
                    cb = ct[2 * (i % 2)]
                    sb_ = ct[2 * (i % 2) + 1]
                    ph.j("sync", lambda h, stt=stt, tc=tc, cb=cb: h.dma_start(
                        out=cb[:], in_=self.sc_cosT[:, tc * T + stt * 128:
                                                    tc * T + (stt + 1) * 128]),
                        waits=[("mm", mm_tk.get(i - 2, 0))], tick="io", n=16)
                    io_tk[i] = ph.j("sync", lambda h, stt=stt, tc=tc, sb_=sb_:
                                    h.dma_start(out=sb_[:],
                                                in_=self.sc_sinT[:, tc * T + stt * 128:
                                                                 tc * T + (stt + 1) * 128]),
                                    tick="io", n=16)
                    def mmf(h, tc=tc, cb=cb, sb_=sb_, pL=pL, pH=pH):
                        st_ = (tc == 0)
                        sp = (tc == NT - 1)
                        arev = Aim[:, tc * 513 + 512:tc * 513:-1]
                        brev = Bim[:, tc * 513 + 512:tc * 513:-1]
                        h.matmul(pL[:, 0:512], cb[:],
                                 Aim[:, tc * 513:tc * 513 + 512],
                                 start=st_, stop=False)
                        h.matmul(pH[:, 0:512], cb[:], arev, start=st_, stop=False)
                        h.matmul(pL[:, 0:512], sb_[:],
                                 Bng[:, tc * 513:tc * 513 + 512],
                                 start=False, stop=sp)
                        return h.matmul(pH[:, 0:512], sb_[:], brev,
                                        start=False, stop=sp)
                    mm_tk[i] = ph.j("tensor", mmf,
                                    waits=[("io", io_tk[i]),
                                           ("v", v_tk.get(stt - 2, 0))], tick="mm")
                def evx(h, stt=stt, pL=pL, pH=pH):
                    h.tensor_add(self.x_sb[:, stt * D:stt * D + 512],
                                 self.x_sb[:, stt * D:stt * D + 512], pL[:, 0:512])
                    return h.tensor_add(self.x_sb[:, stt * D + 512:(stt + 1) * D],
                                        self.x_sb[:, stt * D + 512:(stt + 1) * D],
                                        pH[:, 0:512])
                v_tk[stt] = ph.j("vector", evx,
                                 waits=[("mm", mm_tk[stt * NT + NT - 1])], tick="v")
            ph.run()

    def p_ffn(self, sq, call):
        nc = self.nc
        with ExitStack() as es:
            wgT = es.enter_context(nc.sbuf_tensor(self.uname("wgT"), [128, NDC * II], BF16))
            wuT = es.enter_context(nc.sbuf_tensor(self.uname("wuT"), [128, NDC * II], BF16))
            with ExitStack() as e2:
                q = self.mk_qtmp(e2, 1024)
                lnt = e2.enter_context(nc.sbuf_tensor(self.uname("f_ln"), [128, D], F32))
                tm2 = e2.enter_context(nc.sbuf_tensor(self.uname("f_tm2"), [128, D], F32))
                sums = e2.enter_context(nc.sbuf_tensor(self.uname("f_sums"), [128, 8], F32))
                xq = self.dbuf(e2, "f_xq", 2, D, BF16)
                xqT = self.dbuf(e2, "f_xqT", 2, D, BF16)
                sg = e2.enter_context(nc.sbuf_tensor(self.uname("f_sg"), [128, 512], F32))
                hb = e2.enter_context(nc.sbuf_tensor(self.uname("f_h"), [128, 512], F32))
                hqt = self.dbuf(e2, "f_hq", 2, II, BF16)
                ph = Phase(self, f"n{sq}_{call}_s1")
                ph.j("sync", lambda h: h.dma_start(out=wgT[:], in_=self.sc_wg[:]),
                     tick="io", n=16)
                lu = ph.j("sync", lambda h: h.dma_start(out=wuT[:],
                                                        in_=self.sc_wu[:]),
                          tick="io", n=16)
                sq_tk, mm_tk, hst = {}, {}, {}
                sil_tk, hq_tk, trst = {}, {}, {}
                sc3h = self.sc_hqT.rearrange("p (ic t) -> p ic t", t=T)
                for tt in range(NT):
                    xs = self.x_sb[:, tt * D:(tt + 1) * D]
                    def st1(h, xs=xs):
                        h.drain()
                        h.tensor_tensor(tm2[:, 0:D], xs, xs, Alu.mult)
                        h.reduce_sum(sums[:, 0:1], xs, AX.X)
                        h.reduce_sum(sums[:, 1:2], tm2[:, 0:D], AX.X)
                        h.drain()
                        h.scalar_tensor_tensor(sums[:, 2:3], sums[:, 0:1],
                                               1.0 / (D * D), sums[:, 0:1],
                                               Alu.mult, Alu.mult)
                        h.tensor_scalar_mul(sums[:, 0:1], sums[:, 0:1], 1.0 / D)
                        h.tensor_scalar_mul(sums[:, 1:2], sums[:, 1:2], 1.0 / D)
                        h.drain()
                        return h.tensor_scalar(sums[:, 1:2], sums[:, 1:2],
                                               sums[:, 2:3], 1e-5,
                                               Alu.subtract, Alu.add)
                    v1 = ph.j("vector", st1, waits=[("sc", sq_tk.get(tt - 1, 0))],
                              tick="v")
                    sq_tk[tt] = ph.j("scalar", lambda h: h.activation(
                        sums[:, 2:3], sums[:, 1:2], Act.Sqrt),
                        waits=[("v", v1)], tick="sc")
                    CX = 768
                    def lnap(h, xs=xs, tt=tt):
                        h.reciprocal(sums[:, 3:4], sums[:, 2:3])
                        h.drain()
                        h.tensor_scalar(lnt[:, 0:D], xs, sums[:, 0:1], sums[:, 3:4],
                                        Alu.subtract, Alu.mult)
                        return self.quant_head(h, lnt[:, 0:D], q, D)
                    vh = ph.j("vector", lnap,
                              waits=[("sc", sq_tk[tt]),
                                     ("io", hst.get(tt - 1, 0)),
                                     ("mm", mm_tk.get((tt - 2, 5), 0))], tick="v")
                    sgq = ph.j("scalar", lambda h: self.quant_sign(
                        h, lnt, q, D, 0, D), waits=[("v", vh)], tick="sc")
                    rcx = ph.j("scalar", lambda h: self.quant_recip(h, q, D),
                               tick="sc")
                    ph.j("vector", lambda h: self.quant_steps(
                        h, lnt[:, 0:D], q, D, 0, D), waits=[("sc", rcx)],
                        tick="v")
                    vq = ph.j("vector", lambda h, tt=tt: self.quant_fin(
                        h, xq[tt % 2], q, D, 0, D), waits=[("sc", sgq)],
                        tick="v")
                    iot = None
                    for dc in range(NDC):
                        iot = ph.j("sync", lambda h, tt=tt, dc=dc:
                                   h.dma_start_transpose(
                                       xqT[tt % 2][:, dc * 128:(dc + 1) * 128],
                                       xq[tt % 2][:, dc * 128:(dc + 1) * 128]),
                                   waits=[("v", vq)], tick="io", n=16)
                    for ic, (i0, iw) in enumerate(ICS):
                        gr = tt * 6 + ic
                        pg = self.ps[2 * (gr % 2)]
                        pu = self.ps[2 * (gr % 2) + 1]
                        def gmm(h, tt=tt, i0=i0, iw=iw, pg=pg, pu=pu):
                            last = None
                            for dc in range(NDC):
                                lt = xqT[tt % 2][:, dc * 128:(dc + 1) * 128]
                                h.matmul(pg[:, 0:iw], lt,
                                         wgT[:, dc * II + i0:dc * II + i0 + iw],
                                         start=(dc == 0), stop=(dc == NDC - 1))
                                last = h.matmul(
                                    pu[:, 0:iw], lt,
                                    wuT[:, dc * II + i0:dc * II + i0 + iw],
                                    start=(dc == 0), stop=(dc == NDC - 1))
                            return last
                        mm_tk[(tt, ic)] = ph.j(
                            "tensor", gmm,
                            waits=[("io", iot), ("io", lu),
                                   ("sc", sil_tk.get(gr - 2, 0)),
                                   ("v", hq_tk.get(gr - 2, 0))], tick="mm")
                        sil_tk[gr] = ph.j("scalar", lambda h, pg=pg, iw=iw:
                                          h.activation(sg[:, 0:iw], pg[:, 0:iw],
                                                       Act.Silu),
                                          waits=[("mm", mm_tk[(tt, ic)]),
                                                 ("v", hq_tk.get(gr - 1, 0))],
                                          tick="sc")
                        def hqf(h, tt=tt, i0=i0, iw=iw, pu=pu):
                            h.tensor_tensor(hb[:, 0:iw], sg[:, 0:iw], pu[:, 0:iw],
                                            Alu.mult)
                            return self.quant_head(h, hb[:, 0:iw], q, iw)
                        vh2 = ph.j("vector", hqf,
                                   waits=[("sc", sil_tk[gr]),
                                          ("io", hst.get(tt - 2, 0))], tick="v")
                        sgh = ph.j("scalar", lambda h, iw=iw: self.quant_sign(
                            h, hb, q, iw, 0, iw), waits=[("v", vh2)], tick="sc")
                        rch = ph.j("scalar", lambda h, iw=iw: self.quant_recip(
                            h, q, iw), tick="sc")
                        ph.j("vector", lambda h, iw=iw: self.quant_steps(
                            h, hb[:, 0:iw], q, iw, 0, iw),
                            waits=[("sc", rch)], tick="v")
                        hq_tk[gr] = ph.j("vector", lambda h, tt=tt, i0=i0,
                                         iw=iw: self.quant_fin(
                                             h, hqt[tt % 2][:, i0:i0 + iw], q,
                                             iw, 0, iw),
                                         waits=[("sc", sgh)], tick="v")
                        ncc = iw // 128
                        slot = xq[tt % 2][:, (gr % 2) * 512:(gr % 2) * 512 + 512]
                        tio2 = None
                        for j in range(ncc):
                            tio2 = ph.j("sync", lambda h, tt=tt, i0=i0, j=j,
                                        slot=slot: h.dma_start_transpose(
                                            slot[:, j * 128:(j + 1) * 128],
                                            hqt[tt % 2][:, i0 + j * 128:
                                                        i0 + (j + 1) * 128]),
                                        waits=[("v", hq_tk[gr]),
                                               ("io", trst.get(gr - 2, 0))],
                                        tick="io", n=16)
                        trst[gr] = ph.j("sync", lambda h, tt=tt, i0=i0, ncc=ncc,
                                        slot=slot: h.dma_start(
                                            out=sc3h[:, i0 // 128:
                                                     i0 // 128 + ncc,
                                                     tt * 128:(tt + 1) * 128],
                                            in_=slot[:, 0:ncc * 128].rearrange(
                                                "p (ic j) -> p ic j", j=128)),
                                        tick="io", n=16)
                    hst[tt] = trst[tt * 6 + 5]
                ph.run()
        with ExitStack() as es:
            wdT = es.enter_context(nc.sbuf_tensor(self.uname("wdT"), [128, NIC * D], BF16))
            hqT = self.dbuf(es, "f_hqT", 2, II, BF16)
            sc3h = self.sc_hqT.rearrange("p (ic t) -> p ic t", t=T)
            ph = Phase(self, f"n{sq}_{call}_s2")
            lw = ph.j("sync", lambda h: h.dma_start(out=wdT[:], in_=self.sc_wd[:]),
                      tick="io", n=16)
            mm_tk, v_tk = {}, {}
            for tt in range(NT):
                tio = ph.j("sync", lambda h, tt=tt: h.dma_start(
                    out=hqT[tt % 2][:].rearrange("p (ic j) -> p ic j", j=128),
                    in_=sc3h[:, 0:NIC, tt * 128:(tt + 1) * 128]),
                    waits=[("mm", mm_tk.get((tt - 2, 1), 0))], tick="io", n=16)
                for half in range(2):
                    bank = self.ps[4 + 2 * (tt % 2) + half]
                    def dmm(h, tt=tt, half=half, bank=bank):
                        last = None
                        for icc in range(NIC):
                            last = h.matmul(
                                bank[:, 0:512],
                                hqT[tt % 2][:, icc * 128:(icc + 1) * 128],
                                wdT[:, icc * D + half * 512:
                                    icc * D + half * 512 + 512],
                                start=(icc == 0), stop=(icc == NIC - 1))
                        return last
                    mm_tk[(tt, half)] = ph.j(
                        "tensor", dmm,
                        waits=[("io", tio), ("io", lw),
                               ("v", v_tk.get((tt - 2, half), 0))], tick="mm")
                    v_tk[(tt, half)] = ph.j(
                        "vector", lambda h, tt=tt, half=half, bank=bank:
                        h.tensor_add(
                            self.x_sb[:, tt * D + half * 512:
                                      tt * D + half * 512 + 512],
                            self.x_sb[:, tt * D + half * 512:
                                      tt * D + half * 512 + 512],
                            bank[:, 0:512]),
                        waits=[("mm", mm_tk[(tt, half)])], tick="v")
            ph.run()

    def p_attn_b(self, sq):
        nc = self.nc
        with ExitStack() as es:
            qT = es.enter_context(nc.sbuf_tensor(self.uname("bqT"), [128, NDC * T], BF16))
            kT = es.enter_context(nc.sbuf_tensor(self.uname("bkT"), [128, NDC * T], BF16))
            vnb = es.enter_context(nc.sbuf_tensor(self.uname("bvn"),
                                                  [128, NT * 1040], BF16))
            self.hnc = es.enter_context(
                nc.sbuf_tensor(self.uname("bhnc"), [80, T], BF16))
            self.rowsel = es.enter_context(
                nc.sbuf_tensor(self.uname("brs"), [128, NH * 128], BF16))
            phr = Phase(self, f"b{sq}_rs")
            phr.j("sync", lambda h: h.dma_start(out=self.rowsel[:],
                                               in_=self.rowsel_i[:]),
                  tick="io", n=16)
            phr.run()
            mkB = es.enter_context(nc.sbuf_tensor(self.uname("mkB"), [128, 896], BF16))
            phv = Phase(self, f"b{sq}_von")
            phv.j("gpsimd", lambda h: h.memset(
                vnb.rearrange("p (k o) -> p k o", o=65)[:, :, 64:65], 1.0),
                tick="g")
            phv.run()
            with ExitStack() as e2:
                xlnT = e2.enter_context(
                    nc.sbuf_tensor(self.uname("bxlT"), [128, NDC * 1024], BF16))
                bwb = e2.enter_context(
                    nc.sbuf_tensor(self.uname("bwb"), [128, 2048], BF16))
                self.wbuf = [bwb[:, 0:1024], bwb[:, 1024:2048]]
                bvt = e2.enter_context(
                    nc.sbuf_tensor(self.uname("bvt"), [128, 2048], BF16))
                vtmp = [bvt[:, 0:1024], bvt[:, 1024:2048]]
                for half in range(2):
                    with ExitStack() as e3:
                        ph = Phase(self, f"b{sq}_ln{half}")
                        if half == 0:
                            ph.j("sync", lambda h: h.dma_start(
                                out=mkB[:], in_=self.maskB_i[:]), tick="io", n=16)
                        self.ln_block(ph, list(range(8 * half, 8 * half + 8)),
                                      xlnT, e3, double=True, span=1024,
                                      tmp2=bvt[:].bitcast(F32),
                                      tmp=bwb[:].bitcast(F32))
                        ph.run()
                    self.p_qkv_b_half(sq, half, xlnT, vtmp, qT, kT, vnb)
            with ExitStack() as e4:
                q2 = self.dbuf(e4, "bq2", 2, T, BF16)
                knb = e4.enter_context(
                    nc.sbuf_tensor(self.uname("bknb"), [80, T], BF16))
                ph = Phase(self, f"b{sq}_hn")
                self.hnorm_b(ph, qT, kT, q2, self.hnc, knb)
                ph.run()
                if sq == 0:
                    self.maybe_probe("bnegc", self.negc[:, 0:16])
                    self.maybe_probe("bqT", qT[:])
                    self.maybe_probe("bkT", kT[:])
                    self.maybe_probe("bvn", vnb[:])
                    self.maybe_probe("bxlT", None)
            with ExitStack() as e5:
                outT = e5.enter_context(nc.sbuf_tensor(self.uname("boT"), [128, NDC * 512],
                                                       BF16))
                pb = self.dbuf(e5, "bpb", 2, 512, BF16)
                self.rd = e5.enter_context(nc.sbuf_tensor(self.uname("brd"), [128, 512], F32))
                self.rdb = e5.enter_context(nc.sbuf_tensor(self.uname("brdb"), [128, 512], BF16))
                wim = e5.enter_context(
                    nc.sbuf_tensor(self.uname("bwim"), [128, NDC * 1024], BF16))
                bb = e5.enter_context(
                    nc.sbuf_tensor(self.uname("bbb"), [128, D], BF16))
                phb = Phase(self, f"b{sq}_bb")
                m1 = phb.j("tensor", lambda h: h.matmul(
                    self.ps[7][0:128, 0:512], self.onesr[0:1, 0:128],
                    self.boutb[0:1, 0:512], start=True, stop=True,
                    tile_position=(0, 0)), tick="mm")
                phb.j("vector", lambda h: h.tensor_copy(bb[:, 0:512],
                                                        self.ps[7][:, 0:512]),
                      waits=[("mm", m1)], tick="v")
                m2 = phb.j("tensor", lambda h: h.matmul(
                    self.ps[7][0:128, 0:512], self.onesr[0:1, 0:128],
                    self.boutb[0:1, 512:1024], start=True, stop=True,
                    tile_position=(0, 0)), waits=[("v", self.sv["v"])], tick="mm")
                phb.j("vector", lambda h: h.tensor_copy(bb[:, 512:1024],
                                                        self.ps[7][:, 0:512]),
                      waits=[("mm", m2)], tick="v")
                phb.run()
                self._projb = {"outT": outT, "wim": wim, "wl": None, "bb": bb}
                self._core_es = e5
                self.p_attn_core(sq, "b", qT, kT, vnb, outT, pb, None, mkB,
                                 proj_cb=self.projb_cb)

    def p_qkv_b_half(self, sq, half, xlnT, vtmp, qT, kT, vnb):
        ph = Phase(self, f"b{sq}_qkv{half}")
        w3 = self.sc_wqkvb.rearrange("p (dc e) -> p dc e", e=3 * D)

        def wload(h, ec, buf):
            return h.dma_start(
                out=self.wbuf[buf].rearrange("p (dc j) -> p dc j", j=128),
                in_=w3[:, :, ec * 128:(ec + 1) * 128])

        def rhs(ec, c, dc):
            return xlnT[:, dc * 1024 + c * 512:dc * 1024 + c * 512 + 512]

        def evac(h, ec, c, bank):
            col = half * 1024 + c * 512
            if ec < 8:
                dst = qT[:, ec * T + col:ec * T + col + 512]
            elif ec < 16:
                dst = kT[:, (ec - 8) * T + col:(ec - 8) * T + col + 512]
            else:
                dst = vtmp[ec % 2][:, c * 512:c * 512 + 512]
            return h.activation(dst, bank[:, 0:512], Act.Identity,
                                bias=self.betW[:, ec:ec + 1])

        vn_ev = {}

        def kv_hook(ph, gi, ec, evlast):
            if ec < 16:
                return
            c2 = ec - 16
            for ktl in range(8):
                i = c2 * 8 + ktl
                ktg = half * 8 + ktl
                bank = self.ps[4 + i % 2]
                m = ph.j("tensor", lambda h, ec=ec, ktl=ktl, bank=bank: h.transpose(
                    bank[:, 0:64].bitcast(BF16),
                    vtmp[ec % 2][:, ktl * 128:(ktl + 1) * 128],
                    self.ident[:]),
                    waits=[("ev", evlast), ("v", vn_ev.get(i - 2, 0))], tick="mm")
                def cpy(h, c2=c2, ktg=ktg, bank=bank):
                    h.tensor_copy(
                        vnb[:, ktg * 1040 + (2 * c2) * 65:
                            ktg * 1040 + (2 * c2) * 65 + 64],
                        bank[:, 0:64].bitcast(BF16)[:, 0:64])
                    return h.tensor_copy(
                        vnb[:, ktg * 1040 + (2 * c2 + 1) * 65:
                            ktg * 1040 + (2 * c2 + 1) * 65 + 64],
                        bank[:, 0:64].bitcast(BF16)[:, 64:128])
                vn_ev[i] = ph.j("vector", cpy, waits=[("mm", m)], tick="v")

        self.qkv_phase(ph, list(range(NEC)), wload, rhs, evac, 1024,
                       weng="sync", wsem="io", kv_hook=kv_hook)
        ph.run()

    def projb_cb(self, ph, c, nrm_last):
        st = self._projb
        outT, wim = st["outT"], st["wim"]
        if st["wl"] is None:
            wo3 = self.woutT_b.rearrange("(ec p) j -> p ec j", p=128)
            st["wl"] = ph.j("gpsimd", lambda h: h.dma_start(
                out=wim.rearrange("p (ec j) -> p ec j", j=1024), in_=wo3),
                tick="w", n=16)
        last_v = 0
        for ttl in range(4):
            tt = c * 4 + ttl
            for half in range(2):
                def mmf(h, ttl=ttl, half=half):
                    last = None
                    for ec in range(NDC):
                        last = h.matmul(self.ps[7][:, 0:512],
                                        outT[:, ec * 512 + ttl * 128:
                                             ec * 512 + (ttl + 1) * 128],
                                        wim[:, ec * 1024 + half * 512:
                                            ec * 1024 + half * 512 + 512],
                                        start=(ec == 0), stop=(ec == NDC - 1))
                    return last
                m = ph.j("tensor", mmf,
                         waits=[("w", st["wl"]), ("v", nrm_last),
                                ("v", last_v)], tick="mm")
                def resadd(h, tt=tt, half=half):
                    xs = self.x_sb[:, tt * D + half * 512:
                                   tt * D + half * 512 + 512]
                    h.tensor_add(xs, xs, self.ps[7][:, 0:512])
                    h.drain()
                    return h.tensor_add(xs, xs,
                                        st["bb"][:, half * 512:half * 512 + 512])
                last_v = ph.j("vector", resadd, waits=[("mm", m)], tick="v")
        return last_v

    def build(self):
        self.declare()
        self.p_init()
        if self.upto >= 2:
            self.p_quant_weights(self.w_gate, II, self.sc_wg, "wg")
            self.p_quant_weights(self.w_up, II, self.sc_wu, "wu")
            self.p_quant_weights(self.w_down, D, self.sc_wd, "wd", cols=II)
        if self.upto >= 3:
            self.p_fold_wqkvb()
        if self.upto >= 1:
            self.p_build_Ttab()
        for sq in range(self.nseq):
            self.p_xload(sq)
            if self.upto >= 1:
                self.p_attn_a(sq)
            if self.upto >= 2:
                self.p_ffn(sq, 0)
            if self.upto >= 3:
                self.p_attn_b(sq)
            if self.upto >= 4:
                self.p_ffn(sq, 1)
            self.p_store(sq)
        ph = Phase(self, "fin")
        ph.j("sync", lambda h: h.nop(),
             waits=[("io", self.sv["io"]), ("w", self.sv["w"]),
                    ("v", self.sv["v"]), ("mm", self.sv["mm"]),
                    ("sc", self.sv["sc"]), ("g", self.sv["g"]),
                    ("ev", self.sv["ev"])])
        ph.run(tail=False)
        self.st.close()
        return self.nc


def _tables():
    f = {}
    d = np.arange(1024, dtype=np.float64)
    fidx = np.arange(513, dtype=np.float64)
    angD = 2.0 * np.pi * np.outer(d, fidx) / 1024.0
    for nm, fn in [("cosD", np.cos), ("sinD", np.sin)]:
        tab = fn(angD)
        if nm == "sinD":
            tab[:, 0] = 0.0
            tab[:, 512] = 0.0
        hi = tab.astype(np.float32)
        lo = (tab - hi.astype(np.float64)).astype(np.float32)
        for sfx, arr in [("_hi", hi), ("_lo", lo)]:
            img = arr.reshape(8, 128, 513).transpose(1, 0, 2).reshape(128, 8 * 513)
            f[nm + sfx] = np.ascontiguousarray(img)
    t = np.arange(2048, dtype=np.float64)
    ang1 = 2.0 * np.pi * np.outer(t, np.arange(16, dtype=np.float64) * 128.0) / 2048.0
    ang2 = 2.0 * np.pi * np.outer(t, np.arange(128, dtype=np.float64)) / 2048.0
    for nm, ang, w in [("c1", ang1, 16), ("s1", ang1, 16),
                       ("c2", ang2, 128), ("s2", ang2, 128)]:
        tab = (np.cos if nm[0] == "c" else np.sin)(ang)
        hi = tab.astype(np.float32)
        lo = (tab - hi.astype(np.float64)).astype(np.float32)
        for sfx, arr in [("_hi", hi), ("_lo", lo)]:
            img = arr.reshape(16, 128, w).transpose(1, 0, 2).reshape(128, 16 * w)
            f[nm + sfx] = np.ascontiguousarray(img)
    r = np.arange(128)
    tq = np.arange(512)
    f["maskA"] = np.where(tq[None, :] < 4 * r[:, None] + 3,
                          np.float32(-1e38), np.float32(0.0)).astype(BF)
    w = np.arange(896)
    f["maskB"] = np.where(w[None, :] < r[:, None] + 384,
                          np.float32(-1e38), np.float32(0.0)).astype(BF)
    f["identc"] = np.eye(128, dtype=BF)
    rs = np.zeros((128, 16, 128), dtype=BF)
    for hh in range(16):
        rs[hh, hh, :] = 1.0
        rs[64 + hh, hh, :] = 1.0
    f["rowsel"] = np.ascontiguousarray(rs.reshape(128, 16 * 128))
    hl = np.zeros((128, 8, 16), dtype=BF)
    for dc in range(8):
        for rr in range(128):
            hl[rr, dc, 2 * dc + rr // 64] = 1.0
    f["hlhs"] = np.ascontiguousarray(hl.reshape(128, 128))
    return f


_NC_CACHE = {}


def _get_nc(upto, nseq):
    key = (upto, nseq)
    if key not in _NC_CACHE:
        _NC_CACHE[key] = Bld(upto, nseq).build()
    return _NC_CACHE[key]


def _host_inputs(inputs):
    conv_w = np.ascontiguousarray(inputs["L0_attn_a_conv_w"])
    cw = conv_w.transpose(2, 1, 0)
    cwi = np.zeros((128, 4 * 1024), dtype=np.float32)
    for kap in range(4):
        cwi[0:64, kap * 1024:(kap + 1) * 1024] = cw[kap]
        cwi[64:128, kap * 1024:(kap + 1) * 1024] = cw[kap]
    nullk = np.asarray(inputs["L0_attn_a_null_k"]).reshape(16, 64)
    nki = np.zeros((128, 16), dtype=np.float32)
    nki[0:64, :] = nullk.T
    nki[64:128, :] = nullk.T
    nullv = np.asarray(inputs["L0_attn_a_null_v"], dtype=np.float32).reshape(16, 64)
    nvi = np.concatenate([nullv, np.ones((16, 1), np.float32)], axis=1).reshape(1, 16 * 65)
    nvi = np.ascontiguousarray(nvi)
    col8 = lambda b: np.ascontiguousarray(
        np.asarray(b, dtype=np.float32).reshape(8, 128).T)
    ins = {
        "wqkv_a": np.ascontiguousarray(inputs["L0_attn_a_Wqkv"], dtype=np.float32),
        "convw": cwi,
        "convb": col8(inputs["L0_attn_a_conv_b"]),
        "nullk": nki,
        "nullv": nvi,
        "wout_a": np.ascontiguousarray(inputs["L0_attn_a_W_out"], dtype=np.float32),
        "bout_a": col8(inputs["L0_attn_a_b_out"]),
        "gam_b": col8(inputs["L0_attn_b_ln_gamma"]),
        "bet_b": col8(inputs["L0_attn_b_ln_beta"]),
        "wqkvT_b": np.ascontiguousarray(
            np.asarray(inputs["L0_attn_b_w_qkv"], dtype=np.float32).T),
        "woutT_b": np.ascontiguousarray(
            np.asarray(inputs["L0_attn_b_w_out"], dtype=np.float32).T),
        "bout_b": np.ascontiguousarray(
            np.asarray(inputs["L0_attn_b_b_out"], dtype=np.float32).reshape(1, 1024)),
        "w_gate": np.ascontiguousarray(inputs["L0_ffn_w_gate"], dtype=np.float32),
        "w_up": np.ascontiguousarray(inputs["L0_ffn_w_up"], dtype=np.float32),
        "w_down": np.ascontiguousarray(inputs["L0_ffn_w_down"], dtype=np.float32),
    }
    ins.update(_tables())
    return ins


def kernel(**inputs):
    n_cores, nseq = 8, 2
    x = np.ascontiguousarray(np.asarray(inputs["L0_attn_a_x"], dtype=np.float32))
    ins = _host_inputs(inputs)
    nc = _get_nc(4, nseq)
    in_maps = []
    for core in range(n_cores):
        m = dict(ins)
        m["x"] = np.ascontiguousarray(x[core * nseq:(core + 1) * nseq])
        in_maps.append(m)
    res = run_bass_kernel_spmd(nc, in_maps, core_ids=list(range(n_cores)))
    out = np.concatenate([r["y"] for r in res.results], axis=0)
    return out.astype(np.float32)


# version 08
# speedup vs baseline: 1.0848x; 1.0848x over previous
import numpy as np
import ml_dtypes
from contextlib import ExitStack
import concourse.bass as bass
import concourse.mybir as mybir
from concourse.bass_utils import run_bass_kernel_spmd

F32 = mybir.dt.float32
BF16 = mybir.dt.bfloat16
Alu = mybir.AluOpType
Act = mybir.ActivationFunctionType
AX = mybir.AxisListType
BF = ml_dtypes.bfloat16

T, D, NH, DH, CF, CK, II = 2048, 1024, 16, 64, 4, 512, 2816
NT, NDC, NTC, NEC, NIC = T // 128, D // 128, T // 512, 3 * D // 128, II // 128
MIDS = [0.25, 0.75, 1.25, 1.75, 2.5, 3.5, 5.0]
CNTS = [1.0, 1.0, 1.0, 1.0, 2.0, 2.0, 4.0]
ICS = [(0, 512), (512, 512), (1024, 512), (1536, 512), (2048, 512), (2560, 256)]
ENG = ("sync", "gpsimd", "tensor", "vector", "scalar")


class Phase:

    def __init__(self, b, name):
        self.b, self.name = b, name
        self.jobs = {e: [] for e in ENG}

    def j(self, eng, fn, waits=(), tick=None, n=1):
        v = None
        if tick is not None:
            self.b.sv[tick] += n
            v = self.b.sv[tick]
        self.jobs[eng].append((fn, tuple(waits), tick, n))
        return v

    def tail(self):
        for e in ("sync", "gpsimd"):
            if self.jobs[e]:
                nm = "io" if e == "sync" else "w"
                self.jobs[e].append(
                    (lambda h: h.nop(), (( nm, self.b.sv[nm]),), None, 1))

    def run(self, tail=True):
        b = self.b
        if tail:
            self.tail()
        with b.nc.Block(self.name) as blk:
            for e in ENG:
                jl = self.jobs[e]
                if not jl:
                    continue

                def body(h, jl=jl):
                    cur = {}
                    for fn, waits, tick, n in jl:
                        for snm, val in waits:
                            if val and cur.get(snm, -1) < val:
                                h.wait_ge(getattr(b, "s_" + snm), val)
                                cur[snm] = val
                        inst = fn(h)
                        if tick is not None:
                            inst.then_inc(getattr(b, "s_" + tick), n)
                getattr(blk, e)(body)


class Bld:
    def __init__(self, upto=4, nseq=2):
        self.upto, self.nseq = upto, nseq
        self.nc = bass.Bass("TRN2", target_bir_lowering=False)
        self.st = ExitStack()

    def declare(self):
        nc, st = self.nc, self.st
        self.sv = {}
        for n in ["w", "io", "mm", "ev", "v", "sc", "g"]:
            setattr(self, "s_" + n, st.enter_context(nc.semaphore("s_" + n)))
            self.sv[n] = 0

        def din(name, shape, dtype=F32):
            return nc.dram_tensor(name, list(shape), dtype, kind="ExternalInput").ap()

        self.x_in = din("x", (self.nseq, T, D))
        self.y_out = nc.dram_tensor("y", [self.nseq, T, D], F32,
                                    kind="ExternalOutput").ap()
        self.dbg_out = nc.dram_tensor("dbg", [128, NT * D], F32,
                                      kind="ExternalOutput").ap()
        self.wqkv_a = din("wqkv_a", (D, 3 * D))
        self.convw_i = din("convw", (128, 4 * D))
        self.convb_i = din("convb", (128, NDC))
        self.nullk_i = din("nullk", (128, NH))
        self.nullv_i = din("nullv", (1, NH * 65))
        self.wout_a = din("wout_a", (D, D))
        self.bouta_i = din("bout_a", (128, NDC))
        self.gam_i = din("gam_b", (128, NDC))
        self.bet_i = din("bet_b", (128, NDC))
        self.wqkvT_b = din("wqkvT_b", (D, 3 * D))
        self.woutT_b = din("woutT_b", (D, D))
        self.boutb_i = din("bout_b", (1, D))
        self.w_gate = din("w_gate", (II, D))
        self.w_up = din("w_up", (II, D))
        self.w_down = din("w_down", (D, II))
        self.maskA_i = din("maskA", (128, 512), BF16)
        self.maskB_i = din("maskB", (128, 896), BF16)
        self.ident_i = din("identc", (128, 128), BF16)
        self.hlhs_i = din("hlhs", (128, NDC * NH), BF16)
        self.rowsel_i = din("rowsel", (128, NH * 128), BF16)
        self.cosD_hi = din("cosD_hi", (128, NDC * 513))
        self.cosD_lo = din("cosD_lo", (128, NDC * 513))
        self.sinD_hi = din("sinD_hi", (128, NDC * 513))
        self.sinD_lo = din("sinD_lo", (128, NDC * 513))
        self.c1_hi = din("c1_hi", (128, NT * 16)); self.c1_lo = din("c1_lo", (128, NT * 16))
        self.s1_hi = din("s1_hi", (128, NT * 16)); self.s1_lo = din("s1_lo", (128, NT * 16))
        self.c2_hi = din("c2_hi", (128, NT * 128)); self.c2_lo = din("c2_lo", (128, NT * 128))
        self.s2_hi = din("s2_hi", (128, NT * 128)); self.s2_lo = din("s2_lo", (128, NT * 128))

        itn = lambda n, s, dt=BF16: nc.dram_tensor(n, list(s), dt, kind="Internal").ap()
        self.sc_cosT = itn("sc_cosT", (128, NT * T))
        self.sc_sinT = itn("sc_sinT", (128, NT * T))
        self.sc_wg = itn("sc_wg", (128, NDC * II))
        self.sc_wu = itn("sc_wu", (128, NDC * II))
        self.sc_wd = itn("sc_wd", (128, NIC * D))
        self.sc_wqkvb = itn("sc_wqkvb", (128, NDC * 3 * D))
        self.sc_hqT = itn("sc_hqT", (128, NIC * T))

        sb = lambda n, s, dt=F32: st.enter_context(
            nc.sbuf_tensor("sb_" + n, list(s), dt))
        self.x_sb = sb("x_sb", (128, NT * D))
        self.ident = sb("ident", (128, 128), BF16)
        self.hlhs = sb("hlhs", (128, NDC * NH), BF16)
        self.convb = sb("convb", (128, NDC))
        self.nullk = sb("nullk", (128, NH), BF16)
        self.nullv = sb("nullv", (1, NH * 65), BF16)
        self.bouta = sb("bouta", (128, NDC))
        self.gam = sb("gam", (128, NDC))
        self.bet = sb("bet", (128, NDC))
        self.betW = sb("betW", (128, NEC))
        self.boutb = sb("boutb", (1, D), BF16)
        self.onesr = sb("onesr", (1, 128), BF16)
        self.onesc = sb("onesc", (128, 1), BF16)
        self.onesF = sb("onesF", (128, 128))
        self.onesB = sb("onesB", (128, 64), BF16)
        self.negc = sb("negc", (128, NH))
        self.smal = sb("smal", (128, 96))
        self.ps = [st.enter_context(nc.psum_tensor(f"ps{i}", [128, 512], F32))
                   for i in range(8)]

    def uname(self, n):
        self._uid = getattr(self, "_uid", 0) + 1
        return f"{n}_u{self._uid}"

    def dbuf(self, es, name, nbuf, cols, dt=F32):
        t = es.enter_context(
            self.nc.sbuf_tensor(self.uname(name), [128, nbuf * cols], dt))
        return [t[:, i * cols:(i + 1) * cols] for i in range(nbuf)]

    def mk_qtmp(self, es, n):
        nc = self.nc
        t = {}
        for nm, sz, dt in [("amax", n // 16, F32), ("trc", n // 16, F32),
                           ("ntb", n // 16, F32), ("ws", n, F32), ("aw", n, F32),
                           ("e1", n, BF16), ("e2", n, BF16), ("e3", n, BF16),
                           ("ms", n, BF16)]:
            t[nm] = es.enter_context(nc.sbuf_tensor(self.uname("q_" + nm), [128, sz], dt))
        t["msn"] = t["ws"]
        return t

    def quant_head(self, v, src_ap, q, n):
        g = n // 16
        amax, trc, ntb = q["amax"], q["trc"], q["ntb"]
        r3 = lambda ap: ap.rearrange("p (g k) -> p g k", k=16)
        v.drain()
        v.tensor_reduce(amax[:, 0:g], r3(src_ap), AX.X, Alu.max,
                        apply_absolute_value=True)
        v.drain()
        v.tensor_scalar(trc[:, 0:g], amax[:, 0:g], 6e-12, 1.0 / 6.0,
                        Alu.max, Alu.mult)
        v.tensor_scalar(ntb[:, 0:g], amax[:, 0:g], 6e-12, -1.0 / 12.0,
                        Alu.max, Alu.mult)
        return v.reciprocal(trc[:, 0:g], trc[:, 0:g])

    def quant_sign(self, e, src_ap, q, n, c0, c1):
        return e.activation(q["ms"][:, c0:c1], src_ap[:, c0:c1], Act.Sign,
                            scale=-1.0)

    def quant_steps(self, e, src_ap, q, n, c0, c1):
        g0, g1 = c0 // 16, c1 // 16
        gg = g1 - g0
        trc = q["trc"]
        ws, aw, e1, e2, e3 = q["ws"], q["aw"], q["e1"], q["e2"], q["e3"]
        sl = lambda ap: ap[:, c0:c1]
        e.drain()
        bc = trc[:, g0:g1].unsqueeze(2).broadcast_to([128, gg, 16])
        e.tensor_tensor(sl(ws).rearrange("p (g k) -> p g k", k=16),
                        src_ap[:, c0:c1].rearrange("p (g k) -> p g k", k=16),
                        bc, Alu.mult)
        e.scalar_tensor_tensor(sl(aw), sl(ws), -1.0, sl(ws), Alu.mult, Alu.max)
        e.tensor_scalar(sl(e1), sl(aw), MIDS[0], CNTS[0], Alu.is_gt, Alu.mult)
        e.tensor_scalar(sl(e2), sl(aw), MIDS[1], CNTS[1], Alu.is_gt, Alu.mult)
        e.tensor_scalar(sl(e3), sl(aw), MIDS[2], CNTS[2], Alu.is_gt, Alu.mult)
        e.tensor_add(sl(e1), sl(e1), sl(e2))
        e.tensor_scalar(sl(e2), sl(aw), MIDS[3], CNTS[3], Alu.is_gt, Alu.mult)
        e.tensor_add(sl(e1), sl(e1), sl(e3))
        e.tensor_scalar(sl(e3), sl(aw), MIDS[4], CNTS[4], Alu.is_gt, Alu.mult)
        e.tensor_add(sl(e1), sl(e1), sl(e2))
        e.tensor_scalar(sl(e2), sl(aw), MIDS[5], CNTS[5], Alu.is_gt, Alu.mult)
        e.tensor_add(sl(e1), sl(e1), sl(e3))
        e.tensor_scalar(sl(e3), sl(aw), MIDS[6], CNTS[6], Alu.is_gt, Alu.mult)
        e.tensor_add(sl(e1), sl(e1), sl(e2))
        return e.tensor_add(sl(e1), sl(e1), sl(e3))

    def quant_fin(self, e, out_ap, q, n, c0, c1):
        g0, g1 = c0 // 16, c1 // 16
        gg = g1 - g0
        ntb, e1, ms, msn = q["ntb"], q["e1"], q["ms"], q["msn"]
        r3 = lambda ap: ap[:, c0:c1].rearrange("p (g k) -> p g k", k=16)
        bn = ntb[:, g0:g1].unsqueeze(2).broadcast_to([128, gg, 16])
        e.tensor_tensor(r3(msn), r3(ms), bn, Alu.mult)
        return e.tensor_tensor(
            out_ap[:, c0:c1].rearrange("p (g k) -> p g k", k=16), r3(e1),
            r3(msn), Alu.mult)

    def quant(self, v, src_ap, out_ap, q, n):
        self.quant_head(v, src_ap, q, n)
        self.quant_steps(v, src_ap, q, n, 0, n)
        return self.quant_fin(v, out_ap, q, n, 0, n)

    def maybe_probe(self, name, ap):
        if getattr(self, "probe", None) != name or ap is None:
            return
        ph = Phase(self, "probe_" + name)
        cols = ap.shape[-1] if len(ap.shape) == 2 else ap.shape[1]
        if ap.dtype == BF16:
            ph.j("gpsimd", lambda h: h.dma_start(
                out=self.dbg_out[0:ap.shape[0], 0:cols], in_=ap), tick="w", n=16)
        else:
            ph.j("sync", lambda h: h.dma_start(
                out=self.dbg_out[0:ap.shape[0], 0:cols], in_=ap), tick="io", n=16)
        ph.run()

    def p_init(self):
        ph = Phase(self, "init")
        for dst, src in [(self.ident, self.ident_i), (self.hlhs, self.hlhs_i),
                         (self.convb, self.convb_i), (self.bouta, self.bouta_i),
                         (self.gam, self.gam_i), (self.bet, self.bet_i)]:
            ph.j("sync", lambda h, d=dst, s=src: h.dma_start(out=d[:], in_=s[:]),
                 tick="io", n=16)
        for dst, src in [(self.nullk, self.nullk_i), (self.nullv, self.nullv_i),
                         (self.boutb, self.boutb_i)]:
            ph.j("gpsimd", lambda h, d=dst, s=src: h.dma_start(out=d[:], in_=s[:]),
                 tick="w", n=16)
        ph.j("gpsimd", lambda h: h.memset(self.onesc[:], 1.0), tick="g")
        ph.j("gpsimd", lambda h: h.memset(self.onesr[:], 1.0), tick="g")
        ph.j("gpsimd", lambda h: h.memset(self.onesF[:], 1.0), tick="g")
        ph.j("gpsimd", lambda h: h.memset(self.onesB[:], 1.0), tick="g")
        ph.run()

    def p_quant_weights(self):
        nc = self.nc
        with ExitStack() as es:
            q = self.mk_qtmp(es, II)
            wt = self.dbuf(es, "qw_wt", 2, II, F32)
            wq = self.dbuf(es, "qw_wq", 2, II, BF16)
            stw = self.dbuf(es, "qw_st", 2, II, BF16)
            ph = Phase(self, "qw_all")
            carry = (0, 0, 0)
            for w_ap, rows, sc_out, cols in [
                    (self.w_gate, II, self.sc_wg, D),
                    (self.w_up, II, self.sc_wu, D),
                    (self.w_down, D, self.sc_wd, II)]:
                carry = self._qw_body(ph, q, wt, wq, stw, w_ap, rows, sc_out,
                                      cols, carry)
            ph.run()

    def _qw_body(self, ph, q, wt, wq, stw, w_ap, rows, sc_out, cols, carry):
        nrt, nct = rows // 128, cols // 128
        sc3 = sc_out.rearrange("p (ct r) -> p ct r", r=rows)
        pv, psn, pio = carry
        vtk = {-2: pv, -1: pv}
        stk = {-2: psn, -1: psn}
        iotk = {-2: pio, -1: pio}
        for rt in range(nrt):
            ld = ph.j("sync", lambda h, rt=rt: h.dma_start(
                out=wt[rt % 2][:, 0:cols],
                in_=w_ap[rt * 128:(rt + 1) * 128, :]),
                waits=[("v", vtk.get(rt - 2, 0)),
                       ("sc", stk.get(rt - 2, 0))], tick="io", n=16)
            vh = ph.j("vector", lambda h, rt=rt: self.quant_head(
                h, wt[rt % 2][:, 0:cols], q, cols),
                waits=[("io", ld), ("io", iotk.get(rt - 2, 0))], tick="v")
            stk[rt] = ph.j("scalar", lambda h, rt=rt: self.quant_sign(
                h, wt[rt % 2], q, cols, 0, cols),
                waits=[("io", ld), ("v", vtk.get(rt - 1, 0))], tick="sc")
            ph.j("vector", lambda h, rt=rt: self.quant_steps(
                h, wt[rt % 2][:, 0:cols], q, cols, 0, cols), tick="v")
            vtk[rt] = ph.j("vector", lambda h, rt=rt: self.quant_fin(
                h, wq[rt % 2][:, 0:cols], q, cols, 0, cols),
                waits=[("sc", stk[rt])], tick="v")
            for ct in range(nct):
                ph.j("sync", lambda h, rt=rt, ct=ct: h.dma_start_transpose(
                    stw[rt % 2][:, ct * 128:(ct + 1) * 128],
                    wq[rt % 2][:, ct * 128:(ct + 1) * 128]),
                    waits=[("v", vtk[rt])], tick="io", n=16)
            iotk[rt] = ph.j("sync", lambda h, rt=rt: h.dma_start(
                out=sc3[:, 0:nct, rt * 128:(rt + 1) * 128],
                in_=stw[rt % 2][:, 0:cols].rearrange(
                    "p (ct j) -> p ct j", j=128)),
                tick="io", n=16)
        return (vtk[nrt - 1], stk[nrt - 1], iotk[nrt - 1])

    def p_fold_wqkvb(self):
        nc = self.nc
        with ExitStack() as es:
            wf = self.dbuf(es, "fb_wf", 2, 3 * D, F32)
            wb = self.dbuf(es, "fb_wb", 2, 3 * D, BF16)
            btc = es.enter_context(nc.sbuf_tensor(self.uname("fb_bt"), [128, NDC], BF16))
            ph = Phase(self, "foldb")
            bt0 = ph.j("vector", lambda h: h.tensor_copy(btc[:], self.bet[:]),
                       tick="v")
            cast_tk, fold_tk, st_tk, mm_tk = {}, {}, {}, {}
            for dc in range(NDC):
                ld = ph.j("sync", lambda h, dc=dc: h.dma_start(
                    out=wf[dc % 2][:], in_=self.wqkvT_b[dc * 128:(dc + 1) * 128, :]),
                    waits=[("v", cast_tk.get(dc - 2, 0))], tick="io", n=16)
                cast_tk[dc] = ph.j("vector", lambda h, dc=dc: h.tensor_copy(
                    wb[dc % 2][:], wf[dc % 2][:]),
                    waits=[("io", ld), ("io", st_tk.get(dc - 2, 0))], tick="v")
                def mmfn(h, dc=dc):
                    last = None
                    for ec in range(NEC):
                        last = h.matmul(self.ps[7][0:128, ec:ec + 1],
                                        wb[dc % 2][:, ec * 128:(ec + 1) * 128],
                                        btc[:, dc:dc + 1],
                                        start=(dc == 0 and ec == 0),
                                        stop=(dc == NDC - 1 and ec == NEC - 1))
                    return last
                mm_tk[dc] = ph.j("tensor", mmfn,
                                 waits=[("v", cast_tk[dc]), ("v", bt0)], tick="mm")
                fold_tk[dc] = ph.j("vector", lambda h, dc=dc: h.tensor_scalar_mul(
                    wb[dc % 2][:], wb[dc % 2][:], self.gam[:, dc:dc + 1]),
                    waits=[("mm", mm_tk[dc])], tick="v")
                st_tk[dc] = ph.j("sync", lambda h, dc=dc: h.dma_start(
                    out=self.sc_wqkvb[:, dc * 3 * D:(dc + 1) * 3 * D],
                    in_=wb[dc % 2][:]), waits=[("v", fold_tk[dc])], tick="io", n=16)
            ph.j("vector", lambda h: h.tensor_copy(self.betW[:, 0:NEC],
                                                   self.ps[7][:, 0:NEC]),
                 waits=[("mm", mm_tk[NDC - 1])], tick="v")
            ph.run()

    def p_build_Ttab(self):
        nc = self.nc
        with ExitStack() as es:
            t1h = es.enter_context(nc.sbuf_tensor(self.uname("t1h"), [128, NT * 16], F32))
            t1l = es.enter_context(nc.sbuf_tensor(self.uname("t1l"), [128, NT * 16], F32))
            u1h = es.enter_context(nc.sbuf_tensor(self.uname("u1h"), [128, NT * 16], F32))
            u1l = es.enter_context(nc.sbuf_tensor(self.uname("u1l"), [128, NT * 16], F32))
            t2h = es.enter_context(nc.sbuf_tensor(self.uname("t2h"), [128, NT * 128], F32))
            t2l = es.enter_context(nc.sbuf_tensor(self.uname("t2l"), [128, NT * 128], F32))
            u2h = es.enter_context(nc.sbuf_tensor(self.uname("u2h"), [128, NT * 128], F32))
            u2l = es.enter_context(nc.sbuf_tensor(self.uname("u2l"), [128, NT * 128], F32))
            stg = self.dbuf(es, "stg", 2, 2 * T, BF16)
            ta = es.enter_context(nc.sbuf_tensor(self.uname("tta"), [128, 128], F32))
            tb = es.enter_context(nc.sbuf_tensor(self.uname("ttb"), [128, 128], F32))
            ph = Phase(self, "ttab")
            lv = 0
            for src, dst in [(self.c1_hi, t1h), (self.c1_lo, t1l),
                             (self.s1_hi, u1h), (self.s1_lo, u1l),
                             (self.c2_hi, t2h), (self.c2_lo, t2l),
                             (self.s2_hi, u2h), (self.s2_lo, u2l)]:
                lv = ph.j("sync", lambda h, d=dst, s=src:
                          h.dma_start(out=d[:], in_=s[:]), tick="io", n=16)

            def dd(h):
                h.tensor_add(t1h[:], t1h[:], t1l[:])
                h.tensor_add(u1h[:], u1h[:], u1l[:])
                h.tensor_add(t2h[:], t2h[:], t2l[:])
                return h.tensor_add(u2h[:], u2h[:], u2l[:])
            ph.j("vector", dd, waits=[("io", lv)], tick="v")
            vtk, iotk = {}, {}
            for tt in range(NT):
                def bt(h, tt=tt):
                    b = stg[tt % 2]
                    c1 = t1h[:, tt * 16:(tt + 1) * 16]
                    s1 = u1h[:, tt * 16:(tt + 1) * 16]
                    c2 = t2h[:, tt * 128:(tt + 1) * 128]
                    s2 = u2h[:, tt * 128:(tt + 1) * 128]
                    last = None
                    for sc in range(16):
                        h.tensor_scalar_mul(ta[:], c2[:], c1[:, sc:sc + 1])
                        h.tensor_scalar_mul(tb[:], s2[:], s1[:, sc:sc + 1])
                        h.tensor_sub(b[:, sc * 128:(sc + 1) * 128], ta[:], tb[:])
                        h.tensor_scalar_mul(ta[:], c2[:], s1[:, sc:sc + 1])
                        h.tensor_scalar_mul(tb[:], s2[:], c1[:, sc:sc + 1])
                        last = h.tensor_add(b[:, T + sc * 128:T + (sc + 1) * 128],
                                            ta[:], tb[:])
                    return last
                vtk[tt] = ph.j("vector", bt, waits=[("io", iotk.get(tt - 2, 0))],
                               tick="v")
                ph.j("sync", lambda h, tt=tt: h.dma_start(
                    out=self.sc_cosT[:, tt * T:(tt + 1) * T],
                    in_=stg[tt % 2][:, 0:T]), waits=[("v", vtk[tt])],
                    tick="io", n=16)
                iotk[tt] = ph.j("sync", lambda h, tt=tt: h.dma_start(
                    out=self.sc_sinT[:, tt * T:(tt + 1) * T],
                    in_=stg[tt % 2][:, T:2 * T]), tick="io", n=16)
            ph.run()

    def p_xload(self, sq):
        ph = Phase(self, f"xl{sq}")
        ph.j("sync", lambda h: h.dma_start(
            out=self.x_sb.rearrange("p (tt d) -> p tt d", d=D),
            in_=self.x_in[sq].rearrange("(tt p) d -> p tt d", p=128)),
            tick="io", n=16)
        ph.run()

    def p_store(self, sq):
        ph = Phase(self, f"st{sq}")
        ph.j("sync", lambda h: h.dma_start(
            out=self.y_out[sq].rearrange("(tt p) d -> p tt d", p=128),
            in_=self.x_sb.rearrange("p (tt d) -> p tt d", d=D)),
            tick="io", n=16)
        ph.run()

    def ln_block(self, ph, tiles, xlnT, es, double=False, span=None, tmp2=None,
                 tmp=None):
        nc = self.nc
        span = span or len(tiles) * 128
        if double and tmp is None:
            tmp = es.enter_context(
                nc.sbuf_tensor(self.uname("ln_tmp"), [128, D], F32))
        if tmp2 is None:
            tmp2 = es.enter_context(nc.sbuf_tensor(self.uname("ln_tm2"), [128, D], F32))
        sums = es.enter_context(nc.sbuf_tensor(self.uname("ln_sums"), [128, 8], F32))
        xb = self.dbuf(es, "ln_xb", 2, D, BF16)

        def stats(h, src, c0):
            h.drain()
            h.tensor_tensor(tmp2[:, 0:D], src, src, Alu.mult)
            h.reduce_sum(sums[:, c0:c0 + 1], src, AX.X)
            h.reduce_sum(sums[:, c0 + 1:c0 + 2], tmp2[:, 0:D], AX.X)
            h.drain()
            h.scalar_tensor_tensor(sums[:, c0 + 2:c0 + 3], sums[:, c0:c0 + 1],
                                   1.0 / (D * D), sums[:, c0:c0 + 1],
                                   Alu.mult, Alu.mult)
            h.tensor_scalar_mul(sums[:, c0:c0 + 1], sums[:, c0:c0 + 1], 1.0 / D)
            h.tensor_scalar_mul(sums[:, c0 + 1:c0 + 2], sums[:, c0 + 1:c0 + 2],
                                1.0 / D)
            h.drain()
            return h.tensor_scalar(sums[:, c0 + 1:c0 + 2], sums[:, c0 + 1:c0 + 2],
                                   sums[:, c0 + 2:c0 + 3], 1e-5,
                                   Alu.subtract, Alu.add)

        sc_last, io_tk = {}, {}
        for i, tt in enumerate(tiles):
            xs = self.x_sb[:, tt * D:(tt + 1) * D]
            v1 = ph.j("vector", lambda h, xs=xs: stats(h, xs, 0),
                      waits=[("sc", sc_last.get(i - 1, 0)),
                             ("io", io_tk.get(i - 2, 0))], tick="v")
            s1 = ph.j("scalar", lambda h: h.activation(sums[:, 2:3], sums[:, 1:2],
                                                       Act.Sqrt),
                      waits=[("v", v1)], tick="sc")
            if not double:
                sc_last[i] = s1
                def ap1(h, i=i, xs=xs):
                    h.reciprocal(sums[:, 3:4], sums[:, 2:3])
                    h.drain()
                    return h.tensor_scalar(xb[i % 2][:], xs, sums[:, 0:1],
                                           sums[:, 3:4], Alu.subtract, Alu.mult)
                vfin = ph.j("vector", ap1, waits=[("sc", s1)], tick="v")
            else:
                def ap1(h, i=i, xs=xs):
                    h.reciprocal(sums[:, 3:4], sums[:, 2:3])
                    h.drain()
                    h.tensor_scalar(tmp[:, 0:D], xs, sums[:, 0:1], sums[:, 3:4],
                                    Alu.subtract, Alu.mult)
                    return stats(h, tmp[:, 0:D], 4)
                v2 = ph.j("vector", ap1, waits=[("sc", s1)], tick="v")
                s2 = ph.j("scalar", lambda h: h.activation(
                    sums[:, 6:7], sums[:, 5:6], Act.Sqrt),
                    waits=[("v", v2)], tick="sc")
                sc_last[i] = s2
                def ap2(h, i=i):
                    h.reciprocal(sums[:, 7:8], sums[:, 6:7])
                    h.drain()
                    return h.tensor_scalar(xb[i % 2][:], tmp[:, 0:D], sums[:, 4:5],
                                           sums[:, 7:8], Alu.subtract, Alu.mult)
                vfin = ph.j("vector", ap2, waits=[("sc", s2)], tick="v")
            for dc in range(NDC):
                io_tk[i] = ph.j("sync", lambda h, i=i, dc=dc: h.dma_start_transpose(
                    xlnT[:, dc * span + i * 128:dc * span + (i + 1) * 128],
                    xb[i % 2][:, dc * 128:(dc + 1) * 128]),
                    waits=[("v", vfin)], tick="io", n=16)

    def qkv_phase(self, ph, nec_list, wload_fn, rhs_fn, evac_fn, toks,
                  weng="gpsimd", wsem="w", kv_hook=None):
        ntc = toks // 512
        mm_tk, ev_tk, w_tk = {}, {}, {}
        for gi, ec in enumerate(nec_list):
            w_tk[gi] = ph.j(weng, lambda h, ec=ec, gi=gi: wload_fn(h, ec, gi % 2),
                            waits=[("mm", mm_tk.get((gi - 2, ntc - 1), 0))],
                            tick=wsem, n=16)
            for c in range(ntc):
                j = gi * ntc + c
                bank = self.ps[j % 4]
                def mmf(h, ec=ec, gi=gi, c=c, bank=bank):
                    last = None
                    for dc in range(NDC):
                        last = h.matmul(bank[:, 0:512],
                                        self.wbuf[gi % 2][:, dc * 128:(dc + 1) * 128],
                                        rhs_fn(ec, c, dc),
                                        start=(dc == 0), stop=(dc == NDC - 1))
                    return last
                mm_tk[(gi, c)] = ph.j("tensor", mmf,
                                      waits=[(wsem, w_tk[gi]),
                                             ("ev", ev_tk.get(j - 4, 0))],
                                      tick="mm")
                ev_tk[j] = ph.j("scalar", lambda h, ec=ec, c=c, bank=bank:
                                evac_fn(h, ec, c, bank),
                                waits=[("mm", mm_tk[(gi, c)])], tick="ev")
            if kv_hook is not None:
                kv_hook(ph, gi, ec, ev_tk[gi * ntc + ntc - 1])
        return mm_tk, ev_tk

    def p_attn_a(self, sq):
        nc = self.nc
        with ExitStack() as esy:
            big = esy.enter_context(nc.sbuf_tensor(self.uname("big0"), [128, NDC * T], BF16))
            yT = big
            with ExitStack() as esa:
                qT = esa.enter_context(nc.sbuf_tensor(self.uname("qT"), [128, NDC * T], BF16))
                kcT = esa.enter_context(nc.sbuf_tensor(self.uname("kcT"), [128, NDC * CK], BF16))
                vna = esa.enter_context(nc.sbuf_tensor(self.uname("vna"),
                                                       [128, 4 * 1040], BF16))
                cw = esa.enter_context(nc.sbuf_tensor(self.uname("cw"), [128, 4 * D], BF16))
                mkA = esa.enter_context(nc.sbuf_tensor(self.uname("mkA"), [128, 512], BF16))
                with ExitStack() as esb:
                    xlnT = big
                    vcT = esb.enter_context(
                        nc.sbuf_tensor(self.uname("vcT"), [128, NDC * CK], BF16))
                    self.wbuf = self.dbuf(esb, "wbuf", 2, 1024, BF16)
                    kvt = self.dbuf(esb, "kvt", 2, T, BF16)
                    with ExitStack() as esl:
                        ph = Phase(self, f"a{sq}_ln1")
                        ph.j("gpsimd", lambda h: h.dma_start(out=cw[:],
                                                             in_=self.convw_i[:]),
                             tick="w", n=16)
                        ph.j("sync", lambda h: h.dma_start(out=mkA[:],
                                                           in_=self.maskA_i[:]),
                             tick="io", n=16)
                        self.ln_block(ph, list(range(NT)), xlnT, esl,
                                      tmp2=kvt[0].bitcast(F32)[:, 0:D])
                        ph.run()
                    if sq == 0:
                        self.maybe_probe("xlnT", xlnT[:])
                    self.p_qkv_a_body(sq, xlnT, kvt, qT, kcT, vcT, cw)
                    if sq == 0:
                        self.maybe_probe("qT", qT[:])
                        self.maybe_probe("kcT", kcT[:])
                        self.maybe_probe("vcT", vcT[:])
                    self.p_vnat_a(sq, vcT, vna)
                    if sq == 0:
                        self.maybe_probe("vna", vna[:])
                with ExitStack() as esn:
                    q2 = self.dbuf(esn, "q2", 2, T, BF16)
                    nk2 = esn.enter_context(nc.sbuf_tensor(self.uname("nk2"), [128, NH], BF16))
                    self.hn = esn.enter_context(nc.sbuf_tensor(self.uname("hn"), [16, T], BF16))
                    ph = Phase(self, f"a{sq}_hn")
                    self.hnorm_common(ph, qT, T, 1.0 / 32.0, q2, kcT, CK, nk2=nk2)
                    ph.run()
                    if sq == 0:
                        self.maybe_probe("negc", self.negc[:, 0:16])
                        self.maybe_probe("smal", self.smal[0:128, 0:96])
                        self.maybe_probe("hn", self.hn[0:16, 0:T])
                with ExitStack() as esc:
                    outT = esc.enter_context(
                        nc.sbuf_tensor(self.uname("outT"), [128, NDC * T], BF16))
                    pb = self.dbuf(esc, "pb", 2, 512, BF16)
                    pn = esc.enter_context(nc.sbuf_tensor(self.uname("pn"), [128, 512], BF16))
                    self.rd = esc.enter_context(nc.sbuf_tensor(self.uname("rd"), [128, 512], F32))
                    self.rdb = esc.enter_context(nc.sbuf_tensor(self.uname("rdb"), [128, 512], F32))
                    self._core_es = esc
                    self.p_attn_core(sq, "a", qT, kcT, vna, outT, pb, pn, mkA)
                    if sq == 0:
                        self.maybe_probe("outT", outT[:])
                    with ExitStack() as esd:
                        wcol = self.dbuf(esd, "wcol", 2, 1024, BF16)
                        self.p_proj_a(sq, outT, wcol, yT)
            if sq == 0:
                self.maybe_probe("yT", yT[:])
            self.p_fft(sq, yT)

    def p_qkv_a_body(self, sq, xlnT, kvt, qT, kcT, vcT, cw):
        ph = Phase(self, f"a{sq}_qkv")
        wq3 = self.wqkv_a.rearrange("(dc p) e -> p dc e", p=128)

        def wload(h, ec, buf):
            return h.dma_start(
                out=self.wbuf[buf].rearrange("p (dc j) -> p dc j", j=128),
                in_=wq3[:, :, ec * 128:(ec + 1) * 128])

        def rhs(ec, c, dc):
            return xlnT[:, dc * T + c * 512:dc * T + c * 512 + 512]

        def evac(h, ec, c, bank):
            if ec < 8:
                return h.copy(qT[:, ec * T + c * 512:ec * T + c * 512 + 512],
                              bank[:, 0:512])
            return h.copy(kvt[ec % 2][:, c * 512:c * 512 + 512], bank[:, 0:512])

        conv_ev = {}
        conv_n = [0]

        def kv_hook(ph, gi, ec, evlast):
            if ec < 8:
                return
            is_v = ec >= 16
            c2 = ec - 16 if is_v else ec - 8
            for gl in range(2):
                gs = conv_n[0]
                conv_n[0] += 1
                grp = 2 * c2 + gl
                po = 64 * gl
                bank = self.ps[4 + gs % 2]
                def cmm(h, ec=ec, grp=grp, po=po, bank=bank):
                    last = None
                    for kap in range(CF):
                        last = h.matmul(
                            bank[po:po + 64, 0:512],
                            cw[po:po + 64,
                               kap * D + grp * 64:kap * D + (grp + 1) * 64],
                            kvt[ec % 2][po:po + 64, 0:T].rearrange(
                                "p (j k) -> p j k", k=CF)[:, :, kap],
                            start=(kap == 0), stop=(kap == CF - 1),
                            tile_position=(po, po))
                    return last
                cm = ph.j("tensor", cmm,
                          waits=[("ev", evlast), ("v", conv_ev.get(gs - 2, 0)),
                                 ("w", self.sv["w"])],
                          tick="mm")
                img = vcT if is_v else kcT
                def cev(h, c2=c2, po=po, bank=bank, img=img):
                    dst = img[po:po + 64, c2 * CK:(c2 + 1) * CK]
                    return h.tensor_scalar_add(dst, bank[po:po + 64, 0:512],
                                               self.convb[po:po + 64, c2:c2 + 1])
                conv_ev[gs] = ph.j("vector", cev, waits=[("mm", cm)], tick="v")

        self.qkv_phase(ph, list(range(NEC)), wload, rhs, evac, T, kv_hook=kv_hook)
        ph.run()

    def p_vnat_a(self, sq, vcT, vna):
        ph = Phase(self, f"a{sq}_vn")
        on = ph.j("gpsimd", lambda h: h.memset(
            vna.rearrange("p (k o) -> p k o", o=65)[:, :, 64:65], 1.0), tick="g")
        tk = {}
        for c2 in range(NDC):
            for kt in range(4):
                i = c2 * 4 + kt
                bank = self.ps[i % 2]
                m = ph.j("tensor", lambda h, c2=c2, kt=kt, bank=bank: h.transpose(
                    bank[:, 0:64].bitcast(BF16),
                    vcT[:, c2 * CK + kt * 128:c2 * CK + (kt + 1) * 128],
                    self.ident[:]),
                    waits=[("v", tk.get(i - 2, 0)), ("g", on)], tick="mm")
                def cpy(h, c2=c2, kt=kt, bank=bank):
                    h.tensor_copy(
                        vna[:, kt * 1040 + (2 * c2) * 65:
                            kt * 1040 + (2 * c2) * 65 + 64],
                        bank[:, 0:64].bitcast(BF16)[:, 0:64])
                    return h.tensor_copy(
                        vna[:, kt * 1040 + (2 * c2 + 1) * 65:
                            kt * 1040 + (2 * c2 + 1) * 65 + 64],
                        bank[:, 0:64].bitcast(BF16)[:, 64:128])
                tk[i] = ph.j("vector", cpy, waits=[("mm", m)], tick="v")
        ph.run()

    def hnorm_common(self, ph, qT, qlen, scale, q2, kimg, klen, nk2=None):
        nqc, nkc = qlen // 512, klen // 512
        sq_tk, mm_tk = {}, {}
        for dc in range(NDC):
            sq_tk[dc] = ph.j("scalar", lambda h, dc=dc: h.activation(
                q2[dc % 2][:, 0:qlen], qT[:, dc * qlen:(dc + 1) * qlen], Act.Square),
                waits=[("mm", mm_tk.get((dc - 2, nqc - 1), 0))], tick="sc")
            for c in range(nqc):
                mm_tk[(dc, c)] = ph.j("tensor", lambda h, dc=dc, c=c: h.matmul(
                    self.ps[c][0:16, 0:512], self.hlhs[:, dc * NH:(dc + 1) * NH],
                    q2[dc % 2][:, c * 512:(c + 1) * 512],
                    start=(dc == 0), stop=(dc == NDC - 1)),
                    waits=[("sc", sq_tk[dc])], tick="mm")
        ev = {}
        for c in range(nqc):
            ev[c] = ph.j("scalar", lambda h, c=c: h.activation(
                self.hn[0:16, c * 512:(c + 1) * 512], self.ps[c][0:16, 0:512],
                Act.Sqrt), waits=[("mm", mm_tk[(NDC - 1, nqc - 1)])], tick="sc")
        def vqf(h):
            h.reduce_max(self.smal[0:16, 0:1], self.hn[0:16, 0:qlen], AX.X)
            return h.drain()
        vq = ph.j("vector", vqf, waits=[("sc", ev[nqc - 1])], tick="v")
        mm2, s2_tk = {}, {}
        for dc in range(NDC):
            s2_tk[dc] = ph.j("scalar", lambda h, dc=dc: h.activation(
                q2[dc % 2][:, 0:klen], kimg[:, dc * klen:(dc + 1) * klen],
                Act.Square),
                waits=[("v", vq), ("mm", mm2.get((dc - 2, nkc - 1), 0))], tick="sc")
            for c in range(nkc):
                mm2[(dc, c)] = ph.j("tensor", lambda h, dc=dc, c=c: h.matmul(
                    self.ps[4 + c][0:16, 0:512], self.hlhs[:, dc * NH:(dc + 1) * NH],
                    q2[dc % 2][:, c * 512:(c + 1) * 512],
                    start=(dc == 0), stop=(dc == NDC - 1)),
                    waits=[("sc", s2_tk[dc])], tick="mm")
        ev2 = {}
        for c in range(nkc):
            ev2[c] = ph.j("scalar", lambda h, c=c: h.activation(
                self.hn[0:16, c * 512:(c + 1) * 512], self.ps[4 + c][0:16, 0:512],
                Act.Sqrt), waits=[("mm", mm2[(NDC - 1, nkc - 1)])], tick="sc")
        def vkf(h):
            h.reduce_max(self.smal[0:16, 1:2], self.hn[0:16, 0:klen], AX.X)
            return h.drain()
        vk = ph.j("vector", vkf, waits=[("sc", ev2[nkc - 1])], tick="v")
        lastv = vk
        if nk2 is not None:
            s3 = ph.j("scalar", lambda h: h.activation(
                nk2[0:64, :], self.nullk[0:64, :], Act.Square), tick="sc")
            m3 = ph.j("tensor", lambda h: h.matmul(
                self.ps[6][0:16, 0:1], nk2[0:64, 0:16], self.onesc[0:64, 0:1],
                start=True, stop=True), waits=[("sc", s3), ("v", vk)], tick="mm")
            s4 = ph.j("scalar", lambda h: h.activation(
                self.smal[0:16, 2:3], self.ps[6][0:16, 0:1], Act.Sqrt),
                waits=[("mm", m3)], tick="sc")
            def vmx(h):
                h.tensor_max(self.smal[0:16, 1:2], self.smal[0:16, 1:2],
                             self.smal[0:16, 2:3])
                return h.drain()
            lastv = ph.j("vector", vmx, waits=[("sc", s4)], tick="v")

        def fin(h, scale=scale):
            h.tensor_tensor(self.smal[0:16, 3:4], self.smal[0:16, 0:1],
                            self.smal[0:16, 1:2], Alu.mult)
            h.drain()
            h.tensor_scalar_mul(self.smal[0:16, 3:4],
                                self.smal[0:16, 3:4], -scale)
            return h.drain()
        vf = ph.j("vector", fin, waits=[("v", lastv)], tick="v")
        dg = ph.j("gpsimd", lambda h: h.dma_start(
            out=self.smal[0:1, 64:80], in_=self.smal[0:16, 3:4]),
            waits=[("v", vf)], tick="w", n=16)
        bm = ph.j("tensor", lambda h: h.matmul(
            self.ps[7][0:128, 0:16], self.onesF[0:1, 0:128],
            self.smal[0:1, 64:80], start=True, stop=True),
            waits=[("w", dg)], tick="mm")
        ph.j("vector", lambda h: h.tensor_copy(self.negc[:, 0:16],
                                               self.ps[7][0:128, 0:16]),
             waits=[("mm", bm)], tick="v")

    def hnorm_b(self, ph, qT, kT, q2, hnc, knb):
        sq_tk, mm_tk = {}, {}
        for dc in range(NDC):
            sq_tk[dc] = ph.j("scalar", lambda h, dc=dc: h.activation(
                q2[dc % 2][:, 0:T], qT[:, dc * T:(dc + 1) * T], Act.Square),
                waits=[("mm", mm_tk.get((dc - 2, NTC - 1), 0))], tick="sc")
            for c in range(NTC):
                def qmm(h, dc=dc, c=c):
                    h.matmul(self.ps[c][0:16, 0:512],
                             self.hlhs[:, dc * NH:(dc + 1) * NH],
                             q2[dc % 2][:, c * 512:(c + 1) * 512],
                             start=(dc == 0), stop=(dc == NDC - 1),
                             tile_position=(0, 0))
                    return h.matmul(self.ps[4 + c][64:80, 0:512],
                                    self.hlhs[:, dc * NH:(dc + 1) * NH],
                                    q2[dc % 2][:, c * 512:(c + 1) * 512],
                                    start=(dc == 0), stop=(dc == NDC - 1),
                                    tile_position=(0, 64))
                mm_tk[(dc, c)] = ph.j("tensor", qmm, waits=[("sc", sq_tk[dc])],
                                      tick="mm")
        ev = {}
        for c in range(NTC):
            def qev(h, c=c):
                h.activation(hnc[0:16, c * 512:(c + 1) * 512],
                             self.ps[c][0:16, 0:512], Act.Sqrt)
                return h.activation(hnc[64:80, c * 512:(c + 1) * 512],
                                    self.ps[4 + c][64:80, 0:512], Act.Sqrt)
            ev[c] = ph.j("scalar", qev,
                         waits=[("mm", mm_tk[(NDC - 1, NTC - 1)])], tick="sc")
        mm2, s2_tk = {}, {}
        for dc in range(NDC):
            s2_tk[dc] = ph.j("scalar", lambda h, dc=dc: h.activation(
                q2[dc % 2][:, 0:T], kT[:, dc * T:(dc + 1) * T], Act.Square),
                waits=[("mm", mm2.get((dc - 2, NTC - 1), 0)),
                       ("sc", ev[NTC - 1])], tick="sc")
            for c in range(NTC):
                def kmm(h, dc=dc, c=c):
                    h.matmul(self.ps[c][0:16, 0:512],
                             self.hlhs[:, dc * NH:(dc + 1) * NH],
                             q2[dc % 2][:, c * 512:(c + 1) * 512],
                             start=(dc == 0), stop=(dc == NDC - 1),
                             tile_position=(0, 0))
                    return h.matmul(self.ps[4 + c][64:80, 0:512],
                                    self.hlhs[:, dc * NH:(dc + 1) * NH],
                                    q2[dc % 2][:, c * 512:(c + 1) * 512],
                                    start=(dc == 0), stop=(dc == NDC - 1),
                                    tile_position=(0, 64))
                mm2[(dc, c)] = ph.j("tensor", kmm,
                                    waits=[("sc", s2_tk[dc]),
                                           ("sc", ev[c])], tick="mm")
        ev2 = {}
        for c in range(NTC):
            def kev(h, c=c):
                h.activation(knb[0:16, c * 512:(c + 1) * 512],
                             self.ps[c][0:16, 0:512], Act.Sqrt)
                return h.activation(knb[64:80, c * 512:(c + 1) * 512],
                                    self.ps[4 + c][64:80, 0:512], Act.Sqrt)
            ev2[c] = ph.j("scalar", kev,
                          waits=[("mm", mm2[(NDC - 1, NTC - 1)])], tick="sc")
        def bmx(h):
            h.drain()
            h.tensor_reduce(self.smal[0:80, 32:48],
                            knb[0:80, 0:T].rearrange("p (b j) -> p b j", j=128),
                            AX.X, Alu.max, negate=True)
            h.drain()
            for B in range(1, 16):
                h.tensor_tensor(self.smal[0:80, 32 + B:33 + B],
                                self.smal[0:80, 32 + B:33 + B],
                                self.smal[0:80, 31 + B:32 + B], Alu.min)
                h.drain()
            bc = self.smal[0:80, 32:48].unsqueeze(2).broadcast_to([80, 16, 128])
            return h.tensor_tensor(
                hnc[0:80, 0:T].rearrange("p (b j) -> p b j", j=128),
                hnc[0:80, 0:T].rearrange("p (b j) -> p b j", j=128),
                bc, Alu.mult)
        ph.j("vector", bmx, waits=[("sc", ev2[NTC - 1])], tick="v")

    def p_attn_core(self, sq, kind, qT, kT, vna, outT, pb, pn, mask,
                    proj_cb=None):
        scale = (1.0 / 32.0) if kind == "a" else 0.125
        kspan = CK if kind == "a" else T
        ph = Phase(self, f"{kind}{sq}_att")
        rdm = self.rd[0:64, 0:256].bitcast(BF16)
        rnd = 0
        nrm_tk, gb_tk, rcp_tk, proj_tk = {}, {}, {}, {}
        shf_tk = {}
        sb_last = {}
        pb_last = {}
        nul_exp_last = [0]
        nav_last = [0]
        ks_glob = [0]
        for c in range(NTC):
            for h_ in range(NH):
                r = rnd
                rnd += 1
                gl = h_ % 2
                po = 64 * gl
                hc = h_ // 2
                pob = self.ps[2 + r % 2]
                kts = list(range(c + 1)) if kind == "a" else list(range(4 * c + 4))
                av_tk = {}
                first_set = False
                if kind == "a":
                    def nsc(h, h_=h_, po=po, c=c):
                        rq = qT[po:po + 64, hc * T + c * 512:hc * T + c * 512 + 512] \
                            if False else qT[po:po + 64,
                                             (h_ // 2) * T + c * 512:
                                             (h_ // 2) * T + c * 512 + 512]
                        return h.matmul(self.ps[4][0:1, 0:512],
                                        self.nullk[po:po + 64, h_:h_ + 1], rq,
                                        start=True, stop=True,
                                        tile_position=(po, 0))
                    m = ph.j("tensor", nsc,
                             waits=[("sc", nul_exp_last[0])], tick="mm")
                    en = ph.j("scalar", lambda h, h_=h_: h.activation(
                        pn[0:1, 0:512], self.ps[4][0:1, 0:512], Act.Exp,
                        bias=self.negc[0:1, h_:h_ + 1], scale=scale),
                        waits=[("mm", m), ("mm", nav_last[0])], tick="sc")
                    nul_exp_last[0] = en
                    def nav(h, h_=h_, pob=pob):
                        return h.matmul(pob[0:65, 0:512],
                                        self.nullv[0:1, h_ * 65:(h_ + 1) * 65],
                                        pn[0:1, 0:512], start=True, stop=False,
                                        tile_position=(0, 0))
                    nav_last[0] = ph.j("tensor", nav,
                                       waits=[("sc", en),
                                              ("v", nrm_tk.get(r - 2, 0)),
                                              ("mm", shf_tk.get(r - 2, 0))],
                                       tick="mm")
                    first_set = True
                for ki, kt in enumerate(kts):
                    ks = ks_glob[0]
                    ks_glob[0] += 1
                    boundary = (kind == "a" and kt == c) or \
                               (kind == "b" and kt >= 4 * c)
                    sbi = ks % 2
                    sb_ = self.ps[sbi]
                    def smm(h, h_=h_, po=po, kt=kt, c=c, sb_=sb_):
                        lk = kT[po:po + 64, (h_ // 2) * kspan + kt * 128:
                                (h_ // 2) * kspan + (kt + 1) * 128]
                        rq = qT[po:po + 64, (h_ // 2) * T + c * 512:
                                (h_ // 2) * T + c * 512 + 512]
                        mm = h.matmul(sb_[0:128, 0:512], lk, rq,
                                      start=True, stop=(kind == "a"),
                                      tile_position=(po, 0))
                        if kind == "a":
                            return mm
                        return h.matmul(
                            sb_[0:128, 0:512],
                            self.rowsel[po:po + 16, h_ * 128:(h_ + 1) * 128],
                            self.hnc[po:po + 16,
                                     c * 512:c * 512 + 512],
                            start=False, stop=True, tile_position=(po, 0))
                    m = ph.j("tensor", smm,
                             waits=[("sc", sb_last.get(sbi, 0))], tick="mm")
                    wl = [("mm", m)]
                    if boundary:
                        if kind == "a":
                            mk = mask[:, 0:512]
                        else:
                            off = 384 - 128 * (kt - 4 * c)
                            mk = mask[:, off:off + 512]
                        mv = ph.j("vector", lambda h, sb_=sb_, mk=mk:
                                  h.tensor_add(sb_[0:128, 0:512],
                                               sb_[0:128, 0:512], mk),
                                  waits=[("mm", m)], tick="v")
                        wl = [("v", mv)]
                    pbi = ks % 2
                    bias_ap = self.negc[:, h_:h_ + 1] if kind == "a" else 0.0
                    ex = ph.j("scalar", lambda h, sb_=sb_, pbi=pbi, bias_ap=bias_ap:
                              h.activation(pb[pbi][:, 0:512], sb_[0:128, 0:512],
                                           Act.Exp, bias=bias_ap, scale=scale),
                              waits=wl + [("mm", pb_last.get(pbi, 0))], tick="sc")
                    sb_last[sbi] = ex
                    st = (not first_set) and ki == 0
                    def av(h, h_=h_, kt=kt, pob=pob, st=st, pbi=pbi):
                        return h.matmul(
                            pob[0:65, 0:512],
                            vna[:, kt * 1040 + h_ * 65:kt * 1040 + (h_ + 1) * 65],
                            pb[pbi][:, 0:512], start=st, stop=False,
                            tile_position=(0, 0))
                    av_tk[ki] = ph.j("tensor", av,
                                     waits=[("sc", ex),
                                            ("v", nrm_tk.get(r - 2, 0)),
                                            ("mm", shf_tk.get(r - 2, 0))],
                                     tick="mm")
                    pb_last[pbi] = av_tk[ki]
                lastav = av_tk[len(kts) - 1]
                def rcp(h, pob=pob):
                    h.activation(self.rd[64:65, 0:512], pob[64:65, 0:512], Act.Ln)
                    h.drain()
                    return h.activation(self.rd[64:65, 0:256].bitcast(BF16),
                                        self.rd[64:65, 0:512], Act.Exp,
                                        scale=-1.0)
                rcp_tk[r] = ph.j("scalar", rcp, waits=[("mm", lastav)], tick="sc")
                gb_tk[r] = ph.j("tensor", lambda h: h.matmul(
                    self.ps[5][0:64, 0:512], self.onesB[64:65, 0:64],
                    self.rd[64:65, 0:256].bitcast(BF16), start=True, stop=True,
                    tile_position=(64, 0)),
                    waits=[("sc", rcp_tk[r]), ("v", nrm_tk.get(r - 1, 0))],
                    tick="mm")
                if proj_cb is None:
                    dste = outT[0:64, (h_ // 2) * T + c * 512:
                                (h_ // 2) * T + c * 512 + 512]
                    dsto = outT[64:128, (h_ // 2) * T + c * 512:
                                (h_ // 2) * T + c * 512 + 512]
                else:
                    dste = outT[0:64, (h_ // 2) * 512:(h_ // 2) * 512 + 512]
                    dsto = outT[64:128, (h_ // 2) * 512:(h_ // 2) * 512 + 512]
                if gl == 0:
                    def evn(h, dste=dste, pob=pob):
                        h.tensor_copy(self.rdb[0:64, 0:512],
                                      self.ps[5][0:64, 0:512])
                        h.drain()
                        return h.tensor_tensor(dste, pob[0:64, 0:512],
                                               self.rdb[0:64, 0:512], Alu.mult)
                    nrm_tk[r] = ph.j("vector", evn,
                                     waits=[("mm", gb_tk[r]), ("mm", lastav),
                                            ("v", proj_tk.get(c - 1, 0))],
                                     tick="v")
                    shf_tk[r] = 0
                else:
                    def evn2(h, pob=pob):
                        h.tensor_copy(self.rdb[0:64, 0:512],
                                      self.ps[5][0:64, 0:512])
                        h.drain()
                        return h.tensor_tensor(rdm[0:64, 0:512],
                                               pob[0:64, 0:512],
                                               self.rdb[0:64, 0:512], Alu.mult)
                    v1 = ph.j("vector", evn2,
                              waits=[("mm", gb_tk[r]), ("mm", lastav),
                                     ("v", proj_tk.get(c - 1, 0)),
                                     ("v", nrm_tk.get(r - 1, 0))], tick="v")
                    shf_tk[r] = ph.j("tensor", lambda h: h.matmul(
                        self.ps[6][64:128, 0:512], self.ident[0:64, 0:64],
                        rdm[0:64, 0:512], start=True, stop=True,
                        tile_position=(0, 64)),
                        waits=[("v", v1), ("v", nrm_tk.get(r - 2, 0))], tick="mm")
                    nrm_tk[r] = ph.j("vector", lambda h, dsto=dsto: h.tensor_copy(
                        dsto, self.ps[6][64:128, 0:512]),
                        waits=[("mm", shf_tk[r])], tick="v")
            if proj_cb is not None:
                proj_tk[c] = proj_cb(ph, c, nrm_tk[rnd - 1])
        ph.run()

    def p_proj_a(self, sq, outT, wcol, yT):
        ph = Phase(self, f"a{sq}_proj")
        wo3 = self.wout_a.rearrange("(ec p) j -> p ec j", p=128)
        mm_tk, ev_tk, wl = {}, {}, {}
        for jc in range(NDC):
            wl[jc] = ph.j("gpsimd", lambda h, jc=jc: h.dma_start(
                out=wcol[jc % 2].rearrange("p (ec j) -> p ec j", j=128),
                in_=wo3[:, :, jc * 128:(jc + 1) * 128]),
                waits=[("mm", mm_tk.get((jc - 2, NTC - 1), 0))], tick="w", n=16)
            for c in range(NTC):
                i = jc * NTC + c
                bank = self.ps[i % 4]
                def mmf(h, jc=jc, c=c, bank=bank):
                    last = None
                    for ec in range(NDC):
                        last = h.matmul(
                            bank[:, 0:512],
                            wcol[jc % 2][:, ec * 128:(ec + 1) * 128],
                            outT[:, ec * T + c * 512:ec * T + c * 512 + 512],
                            start=(ec == 0), stop=(ec == NDC - 1))
                    return last
                mm_tk[(jc, c)] = ph.j("tensor", mmf,
                                      waits=[("w", wl[jc]),
                                             ("ev", ev_tk.get(i - 4, 0))],
                                      tick="mm")
                ev_tk[i] = ph.j("scalar", lambda h, jc=jc, c=c, bank=bank:
                                h.activation(
                                    yT[:, jc * T + c * 512:jc * T + c * 512 + 512],
                                    bank[:, 0:512], Act.Identity,
                                    bias=self.bouta[:, jc:jc + 1]),
                                waits=[("mm", mm_tk[(jc, c)])], tick="ev")
        ph.run()

    def p_fft(self, sq, yT):
        nc = self.nc
        with ExitStack() as es:
            Aim = es.enter_context(nc.sbuf_tensor(self.uname("Aim"), [128, NT * 513], BF16))
            Bim = es.enter_context(nc.sbuf_tensor(self.uname("Bim"), [128, NT * 513], BF16))
            Bng = es.enter_context(nc.sbuf_tensor(self.uname("Bng"), [128, NT * 513], BF16))
            cosD = es.enter_context(nc.sbuf_tensor(self.uname("cosD"), [128, NDC * 513], BF16))
            sinD = es.enter_context(nc.sbuf_tensor(self.uname("sinD"), [128, NDC * 513], BF16))
            ct = self.dbuf(es, "ct", 4, 128, BF16)
            ph = Phase(self, f"f{sq}_dd")
            ah = Aim[:].bitcast(F32)[:, 0:NDC * 513]
            bh = Bim[:].bitcast(F32)[:, 0:NDC * 513]
            l1 = ph.j("sync", lambda h: h.dma_start(out=ah, in_=self.cosD_hi[:]),
                      tick="io", n=16)
            l2 = ph.j("sync", lambda h: h.dma_start(out=bh, in_=self.cosD_lo[:]),
                      tick="io", n=16)
            v1 = ph.j("vector", lambda h: h.tensor_add(cosD[:], ah, bh),
                      waits=[("io", l2)], tick="v")
            l3 = ph.j("sync", lambda h: h.dma_start(out=ah, in_=self.sinD_hi[:]),
                      waits=[("v", v1)], tick="io", n=16)
            l4 = ph.j("sync", lambda h: h.dma_start(out=bh, in_=self.sinD_lo[:]),
                      tick="io", n=16)
            ph.j("vector", lambda h: h.tensor_add(sinD[:], ah, bh),
                 waits=[("io", l4)], tick="v")
            ph.run()

            ph = Phase(self, f"f{sq}_1")
            ev_tk = {}
            for tt in range(NT):
                g0 = 3 * (tt % 2)
                pA, pB, pC = self.ps[g0], self.ps[g0 + 1], self.ps[g0 + 2]
                def mmf(h, tt=tt, pA=pA, pB=pB, pC=pC):
                    last = None
                    for dc in range(NDC):
                        lt = yT[:, dc * T + tt * 128:dc * T + (tt + 1) * 128]
                        st_ = (dc == 0)
                        sp = (dc == NDC - 1)
                        h.matmul(pA[:, 0:512], lt, cosD[:, dc * 513:dc * 513 + 512],
                                 start=st_, stop=sp)
                        h.matmul(pC[:, 0:1], lt,
                                 cosD[:, dc * 513 + 512:dc * 513 + 513],
                                 start=st_, stop=sp)
                        h.matmul(pB[:, 0:512], lt, sinD[:, dc * 513:dc * 513 + 512],
                                 start=st_, stop=sp)
                        last = h.matmul(pC[:, 1:2],
                                        lt, sinD[:, dc * 513 + 512:dc * 513 + 513],
                                        start=False, stop=sp)
                    return last
                m = ph.j("tensor", mmf, waits=[("ev", ev_tk.get(tt - 2, 0))],
                         tick="mm")
                def evs(h, tt=tt, pA=pA, pB=pB, pC=pC):
                    h.copy(Aim[:, tt * 513:tt * 513 + 512], pA[:, 0:512])
                    h.copy(Aim[:, tt * 513 + 512:tt * 513 + 513], pC[:, 0:1])
                    h.copy(Bim[:, tt * 513:tt * 513 + 512], pB[:, 0:512])
                    return h.copy(Bim[:, tt * 513 + 512:tt * 513 + 513], pC[:, 1:2])
                se = ph.j("scalar", evs, waits=[("mm", m)], tick="sc")
                def evv(h, tt=tt, pB=pB, pC=pC):
                    h.tensor_scalar_mul(Bng[:, tt * 513:tt * 513 + 512],
                                        pB[:, 0:512], -1.0)
                    return h.tensor_scalar_mul(
                        Bng[:, tt * 513 + 512:tt * 513 + 513], pC[:, 1:2], -1.0)
                ev_tk[tt] = ph.j("vector", evv, waits=[("mm", m), ("sc", se)],
                                 tick="ev")
            ph.run()

            ph = Phase(self, f"f{sq}_2")
            io_tk, mm_tk, v_tk = {}, {}, {}
            for stt in range(NT):
                pL = self.ps[6] if stt % 2 == 0 else self.ps[4]
                pH = self.ps[7] if stt % 2 == 0 else self.ps[5]
                for tc in range(NT):
                    i = stt * NT + tc
                    cb = ct[2 * (i % 2)]
                    sb_ = ct[2 * (i % 2) + 1]
                    ph.j("sync", lambda h, stt=stt, tc=tc, cb=cb: h.dma_start(
                        out=cb[:], in_=self.sc_cosT[:, tc * T + stt * 128:
                                                    tc * T + (stt + 1) * 128]),
                        waits=[("mm", mm_tk.get(i - 2, 0))], tick="io", n=16)
                    io_tk[i] = ph.j("sync", lambda h, stt=stt, tc=tc, sb_=sb_:
                                    h.dma_start(out=sb_[:],
                                                in_=self.sc_sinT[:, tc * T + stt * 128:
                                                                 tc * T + (stt + 1) * 128]),
                                    tick="io", n=16)
                    def mmf(h, tc=tc, cb=cb, sb_=sb_, pL=pL, pH=pH):
                        st_ = (tc == 0)
                        sp = (tc == NT - 1)
                        arev = Aim[:, tc * 513 + 512:tc * 513:-1]
                        brev = Bim[:, tc * 513 + 512:tc * 513:-1]
                        h.matmul(pL[:, 0:512], cb[:],
                                 Aim[:, tc * 513:tc * 513 + 512],
                                 start=st_, stop=False)
                        h.matmul(pH[:, 0:512], cb[:], arev, start=st_, stop=False)
                        h.matmul(pL[:, 0:512], sb_[:],
                                 Bng[:, tc * 513:tc * 513 + 512],
                                 start=False, stop=sp)
                        return h.matmul(pH[:, 0:512], sb_[:], brev,
                                        start=False, stop=sp)
                    mm_tk[i] = ph.j("tensor", mmf,
                                    waits=[("io", io_tk[i]),
                                           ("v", v_tk.get(stt - 2, 0))], tick="mm")
                def evx(h, stt=stt, pL=pL, pH=pH):
                    h.tensor_add(self.x_sb[:, stt * D:stt * D + 512],
                                 self.x_sb[:, stt * D:stt * D + 512], pL[:, 0:512])
                    return h.tensor_add(self.x_sb[:, stt * D + 512:(stt + 1) * D],
                                        self.x_sb[:, stt * D + 512:(stt + 1) * D],
                                        pH[:, 0:512])
                v_tk[stt] = ph.j("vector", evx,
                                 waits=[("mm", mm_tk[stt * NT + NT - 1])], tick="v")
            ph.run()

    def p_ffn(self, sq, call):
        nc = self.nc
        with ExitStack() as es:
            wgT = es.enter_context(nc.sbuf_tensor(self.uname("wgT"), [128, NDC * II], BF16))
            wuT = es.enter_context(nc.sbuf_tensor(self.uname("wuT"), [128, NDC * II], BF16))
            with ExitStack() as e2:
                q = self.mk_qtmp(e2, 1024)
                lnt = e2.enter_context(nc.sbuf_tensor(self.uname("f_ln"), [128, D], F32))
                tm2 = e2.enter_context(nc.sbuf_tensor(self.uname("f_tm2"), [128, D], F32))
                sums = e2.enter_context(nc.sbuf_tensor(self.uname("f_sums"), [128, 8], F32))
                xq = self.dbuf(e2, "f_xq", 2, D, BF16)
                xqT = self.dbuf(e2, "f_xqT", 2, D, BF16)
                sg = e2.enter_context(nc.sbuf_tensor(self.uname("f_sg"), [128, 512], F32))
                hb = e2.enter_context(nc.sbuf_tensor(self.uname("f_h"), [128, 512], F32))
                hqt = self.dbuf(e2, "f_hq", 2, II, BF16)
                ph = Phase(self, f"n{sq}_{call}_s1")
                ph.j("sync", lambda h: h.dma_start(out=wgT[:], in_=self.sc_wg[:]),
                     tick="io", n=16)
                lu = ph.j("sync", lambda h: h.dma_start(out=wuT[:],
                                                        in_=self.sc_wu[:]),
                          tick="io", n=16)
                sq_tk, mm_tk, hst = {}, {}, {}
                sil_tk, hq_tk, trst = {}, {}, {}
                sc3h = self.sc_hqT.rearrange("p (ic t) -> p ic t", t=T)
                for tt in range(NT):
                    xs = self.x_sb[:, tt * D:(tt + 1) * D]
                    def st1(h, xs=xs):
                        h.drain()
                        h.tensor_tensor(tm2[:, 0:D], xs, xs, Alu.mult)
                        h.reduce_sum(sums[:, 0:1], xs, AX.X)
                        h.reduce_sum(sums[:, 1:2], tm2[:, 0:D], AX.X)
                        h.drain()
                        h.scalar_tensor_tensor(sums[:, 2:3], sums[:, 0:1],
                                               1.0 / (D * D), sums[:, 0:1],
                                               Alu.mult, Alu.mult)
                        h.tensor_scalar_mul(sums[:, 0:1], sums[:, 0:1], 1.0 / D)
                        h.tensor_scalar_mul(sums[:, 1:2], sums[:, 1:2], 1.0 / D)
                        h.drain()
                        return h.tensor_scalar(sums[:, 1:2], sums[:, 1:2],
                                               sums[:, 2:3], 1e-5,
                                               Alu.subtract, Alu.add)
                    v1 = ph.j("vector", st1, waits=[("sc", sq_tk.get(tt - 1, 0))],
                              tick="v")
                    sq_tk[tt] = ph.j("scalar", lambda h: h.activation(
                        sums[:, 2:3], sums[:, 1:2], Act.Sqrt),
                        waits=[("v", v1)], tick="sc")
                    CX = 768
                    def lnap(h, xs=xs, tt=tt):
                        h.reciprocal(sums[:, 3:4], sums[:, 2:3])
                        h.drain()
                        h.tensor_scalar(lnt[:, 0:D], xs, sums[:, 0:1], sums[:, 3:4],
                                        Alu.subtract, Alu.mult)
                        return self.quant_head(h, lnt[:, 0:D], q, D)
                    vh = ph.j("vector", lnap,
                              waits=[("sc", sq_tk[tt]),
                                     ("io", hst.get(tt - 1, 0)),
                                     ("mm", mm_tk.get((tt - 2, 5), 0))], tick="v")
                    sgq = ph.j("scalar", lambda h: self.quant_sign(
                        h, lnt, q, D, 0, D), waits=[("v", vh)], tick="sc")
                    ph.j("vector", lambda h: self.quant_steps(
                        h, lnt[:, 0:D], q, D, 0, D), tick="v")
                    vq = ph.j("vector", lambda h, tt=tt: self.quant_fin(
                        h, xq[tt % 2], q, D, 0, D), waits=[("sc", sgq)],
                        tick="v")
                    iot = None
                    for dc in range(NDC):
                        iot = ph.j("sync", lambda h, tt=tt, dc=dc:
                                   h.dma_start_transpose(
                                       xqT[tt % 2][:, dc * 128:(dc + 1) * 128],
                                       xq[tt % 2][:, dc * 128:(dc + 1) * 128]),
                                   waits=[("v", vq)], tick="io", n=16)
                    for ic, (i0, iw) in enumerate(ICS):
                        gr = tt * 6 + ic
                        pg = self.ps[2 * (gr % 2)]
                        pu = self.ps[2 * (gr % 2) + 1]
                        def gmm(h, tt=tt, i0=i0, iw=iw, pg=pg, pu=pu):
                            last = None
                            for dc in range(NDC):
                                lt = xqT[tt % 2][:, dc * 128:(dc + 1) * 128]
                                h.matmul(pg[:, 0:iw], lt,
                                         wgT[:, dc * II + i0:dc * II + i0 + iw],
                                         start=(dc == 0), stop=(dc == NDC - 1))
                                last = h.matmul(
                                    pu[:, 0:iw], lt,
                                    wuT[:, dc * II + i0:dc * II + i0 + iw],
                                    start=(dc == 0), stop=(dc == NDC - 1))
                            return last
                        mm_tk[(tt, ic)] = ph.j(
                            "tensor", gmm,
                            waits=[("io", iot), ("io", lu),
                                   ("sc", sil_tk.get(gr - 2, 0)),
                                   ("v", hq_tk.get(gr - 2, 0))], tick="mm")
                        sil_tk[gr] = ph.j("scalar", lambda h, pg=pg, iw=iw:
                                          h.activation(sg[:, 0:iw], pg[:, 0:iw],
                                                       Act.Silu),
                                          waits=[("mm", mm_tk[(tt, ic)]),
                                                 ("v", hq_tk.get(gr - 1, 0))],
                                          tick="sc")
                        def hqf(h, tt=tt, i0=i0, iw=iw, pu=pu):
                            h.tensor_tensor(hb[:, 0:iw], sg[:, 0:iw], pu[:, 0:iw],
                                            Alu.mult)
                            return self.quant_head(h, hb[:, 0:iw], q, iw)
                        vh2 = ph.j("vector", hqf,
                                   waits=[("sc", sil_tk[gr]),
                                          ("io", hst.get(tt - 2, 0))], tick="v")
                        sgh = ph.j("scalar", lambda h, iw=iw: self.quant_sign(
                            h, hb, q, iw, 0, iw), waits=[("v", vh2)], tick="sc")
                        ph.j("vector", lambda h, iw=iw: self.quant_steps(
                            h, hb[:, 0:iw], q, iw, 0, iw), tick="v")
                        hq_tk[gr] = ph.j("vector", lambda h, tt=tt, i0=i0,
                                         iw=iw: self.quant_fin(
                                             h, hqt[tt % 2][:, i0:i0 + iw], q,
                                             iw, 0, iw),
                                         waits=[("sc", sgh)], tick="v")
                        ncc = iw // 128
                        slot = xq[tt % 2][:, (gr % 2) * 512:(gr % 2) * 512 + 512]
                        tio2 = None
                        for j in range(ncc):
                            tio2 = ph.j("sync", lambda h, tt=tt, i0=i0, j=j,
                                        slot=slot: h.dma_start_transpose(
                                            slot[:, j * 128:(j + 1) * 128],
                                            hqt[tt % 2][:, i0 + j * 128:
                                                        i0 + (j + 1) * 128]),
                                        waits=[("v", hq_tk[gr]),
                                               ("io", trst.get(gr - 2, 0))],
                                        tick="io", n=16)
                        trst[gr] = ph.j("sync", lambda h, tt=tt, i0=i0, ncc=ncc,
                                        slot=slot: h.dma_start(
                                            out=sc3h[:, i0 // 128:
                                                     i0 // 128 + ncc,
                                                     tt * 128:(tt + 1) * 128],
                                            in_=slot[:, 0:ncc * 128].rearrange(
                                                "p (ic j) -> p ic j", j=128)),
                                        tick="io", n=16)
                    hst[tt] = trst[tt * 6 + 5]
                ph.run()
        with ExitStack() as es:
            wdT = es.enter_context(nc.sbuf_tensor(self.uname("wdT"), [128, NIC * D], BF16))
            hqT = self.dbuf(es, "f_hqT", 2, II, BF16)
            sc3h = self.sc_hqT.rearrange("p (ic t) -> p ic t", t=T)
            ph = Phase(self, f"n{sq}_{call}_s2")
            lw = ph.j("sync", lambda h: h.dma_start(out=wdT[:], in_=self.sc_wd[:]),
                      tick="io", n=16)
            mm_tk, v_tk = {}, {}
            for tt in range(NT):
                tio = ph.j("sync", lambda h, tt=tt: h.dma_start(
                    out=hqT[tt % 2][:].rearrange("p (ic j) -> p ic j", j=128),
                    in_=sc3h[:, 0:NIC, tt * 128:(tt + 1) * 128]),
                    waits=[("mm", mm_tk.get((tt - 2, 1), 0))], tick="io", n=16)
                for half in range(2):
                    bank = self.ps[4 + 2 * (tt % 2) + half]
                    def dmm(h, tt=tt, half=half, bank=bank):
                        last = None
                        for icc in range(NIC):
                            last = h.matmul(
                                bank[:, 0:512],
                                hqT[tt % 2][:, icc * 128:(icc + 1) * 128],
                                wdT[:, icc * D + half * 512:
                                    icc * D + half * 512 + 512],
                                start=(icc == 0), stop=(icc == NIC - 1))
                        return last
                    mm_tk[(tt, half)] = ph.j(
                        "tensor", dmm,
                        waits=[("io", tio), ("io", lw),
                               ("v", v_tk.get((tt - 2, half), 0))], tick="mm")
                    v_tk[(tt, half)] = ph.j(
                        "vector", lambda h, tt=tt, half=half, bank=bank:
                        h.tensor_add(
                            self.x_sb[:, tt * D + half * 512:
                                      tt * D + half * 512 + 512],
                            self.x_sb[:, tt * D + half * 512:
                                      tt * D + half * 512 + 512],
                            bank[:, 0:512]),
                        waits=[("mm", mm_tk[(tt, half)])], tick="v")
            ph.run()

    def p_attn_b(self, sq):
        nc = self.nc
        with ExitStack() as es:
            qT = es.enter_context(nc.sbuf_tensor(self.uname("bqT"), [128, NDC * T], BF16))
            kT = es.enter_context(nc.sbuf_tensor(self.uname("bkT"), [128, NDC * T], BF16))
            vnb = es.enter_context(nc.sbuf_tensor(self.uname("bvn"),
                                                  [128, NT * 1040], BF16))
            self.hnc = es.enter_context(
                nc.sbuf_tensor(self.uname("bhnc"), [80, T], BF16))
            self.rowsel = es.enter_context(
                nc.sbuf_tensor(self.uname("brs"), [128, NH * 128], BF16))
            phr = Phase(self, f"b{sq}_rs")
            phr.j("sync", lambda h: h.dma_start(out=self.rowsel[:],
                                               in_=self.rowsel_i[:]),
                  tick="io", n=16)
            phr.run()
            mkB = es.enter_context(nc.sbuf_tensor(self.uname("mkB"), [128, 896], BF16))
            phv = Phase(self, f"b{sq}_von")
            phv.j("gpsimd", lambda h: h.memset(
                vnb.rearrange("p (k o) -> p k o", o=65)[:, :, 64:65], 1.0),
                tick="g")
            phv.run()
            with ExitStack() as e2:
                xlnT = e2.enter_context(
                    nc.sbuf_tensor(self.uname("bxlT"), [128, NDC * 1024], BF16))
                bwb = e2.enter_context(
                    nc.sbuf_tensor(self.uname("bwb"), [128, 2048], BF16))
                self.wbuf = [bwb[:, 0:1024], bwb[:, 1024:2048]]
                bvt = e2.enter_context(
                    nc.sbuf_tensor(self.uname("bvt"), [128, 2048], BF16))
                vtmp = [bvt[:, 0:1024], bvt[:, 1024:2048]]
                for half in range(2):
                    with ExitStack() as e3:
                        ph = Phase(self, f"b{sq}_ln{half}")
                        if half == 0:
                            ph.j("sync", lambda h: h.dma_start(
                                out=mkB[:], in_=self.maskB_i[:]), tick="io", n=16)
                        self.ln_block(ph, list(range(8 * half, 8 * half + 8)),
                                      xlnT, e3, double=True, span=1024,
                                      tmp2=bvt[:].bitcast(F32),
                                      tmp=bwb[:].bitcast(F32))
                        ph.run()
                    self.p_qkv_b_half(sq, half, xlnT, vtmp, qT, kT, vnb)
            with ExitStack() as e4:
                q2 = self.dbuf(e4, "bq2", 2, T, BF16)
                knb = e4.enter_context(
                    nc.sbuf_tensor(self.uname("bknb"), [80, T], BF16))
                ph = Phase(self, f"b{sq}_hn")
                self.hnorm_b(ph, qT, kT, q2, self.hnc, knb)
                ph.run()
                if sq == 0:
                    self.maybe_probe("bnegc", self.negc[:, 0:16])
                    self.maybe_probe("bqT", qT[:])
                    self.maybe_probe("bkT", kT[:])
                    self.maybe_probe("bvn", vnb[:])
                    self.maybe_probe("bxlT", None)
            with ExitStack() as e5:
                outT = e5.enter_context(nc.sbuf_tensor(self.uname("boT"), [128, NDC * 512],
                                                       BF16))
                pb = self.dbuf(e5, "bpb", 2, 512, BF16)
                self.rd = e5.enter_context(nc.sbuf_tensor(self.uname("brd"), [128, 512], F32))
                self.rdb = e5.enter_context(nc.sbuf_tensor(self.uname("brdb"), [128, 512], BF16))
                wim = e5.enter_context(
                    nc.sbuf_tensor(self.uname("bwim"), [128, NDC * 1024], BF16))
                bb = e5.enter_context(
                    nc.sbuf_tensor(self.uname("bbb"), [128, D], BF16))
                phb = Phase(self, f"b{sq}_bb")
                m1 = phb.j("tensor", lambda h: h.matmul(
                    self.ps[7][0:128, 0:512], self.onesr[0:1, 0:128],
                    self.boutb[0:1, 0:512], start=True, stop=True,
                    tile_position=(0, 0)), tick="mm")
                phb.j("vector", lambda h: h.tensor_copy(bb[:, 0:512],
                                                        self.ps[7][:, 0:512]),
                      waits=[("mm", m1)], tick="v")
                m2 = phb.j("tensor", lambda h: h.matmul(
                    self.ps[7][0:128, 0:512], self.onesr[0:1, 0:128],
                    self.boutb[0:1, 512:1024], start=True, stop=True,
                    tile_position=(0, 0)), waits=[("v", self.sv["v"])], tick="mm")
                phb.j("vector", lambda h: h.tensor_copy(bb[:, 512:1024],
                                                        self.ps[7][:, 0:512]),
                      waits=[("mm", m2)], tick="v")
                phb.run()
                self._projb = {"outT": outT, "wim": wim, "wl": None, "bb": bb}
                self._core_es = e5
                self.p_attn_core(sq, "b", qT, kT, vnb, outT, pb, None, mkB,
                                 proj_cb=self.projb_cb)

    def p_qkv_b_half(self, sq, half, xlnT, vtmp, qT, kT, vnb):
        ph = Phase(self, f"b{sq}_qkv{half}")
        w3 = self.sc_wqkvb.rearrange("p (dc e) -> p dc e", e=3 * D)

        def wload(h, ec, buf):
            return h.dma_start(
                out=self.wbuf[buf].rearrange("p (dc j) -> p dc j", j=128),
                in_=w3[:, :, ec * 128:(ec + 1) * 128])

        def rhs(ec, c, dc):
            return xlnT[:, dc * 1024 + c * 512:dc * 1024 + c * 512 + 512]

        def evac(h, ec, c, bank):
            col = half * 1024 + c * 512
            if ec < 8:
                dst = qT[:, ec * T + col:ec * T + col + 512]
            elif ec < 16:
                dst = kT[:, (ec - 8) * T + col:(ec - 8) * T + col + 512]
            else:
                dst = vtmp[ec % 2][:, c * 512:c * 512 + 512]
            return h.activation(dst, bank[:, 0:512], Act.Identity,
                                bias=self.betW[:, ec:ec + 1])

        vn_ev = {}

        def kv_hook(ph, gi, ec, evlast):
            if ec < 16:
                return
            c2 = ec - 16
            for ktl in range(8):
                i = c2 * 8 + ktl
                ktg = half * 8 + ktl
                bank = self.ps[4 + i % 2]
                m = ph.j("tensor", lambda h, ec=ec, ktl=ktl, bank=bank: h.transpose(
                    bank[:, 0:64].bitcast(BF16),
                    vtmp[ec % 2][:, ktl * 128:(ktl + 1) * 128],
                    self.ident[:]),
                    waits=[("ev", evlast), ("v", vn_ev.get(i - 2, 0))], tick="mm")
                def cpy(h, c2=c2, ktg=ktg, bank=bank):
                    h.tensor_copy(
                        vnb[:, ktg * 1040 + (2 * c2) * 65:
                            ktg * 1040 + (2 * c2) * 65 + 64],
                        bank[:, 0:64].bitcast(BF16)[:, 0:64])
                    return h.tensor_copy(
                        vnb[:, ktg * 1040 + (2 * c2 + 1) * 65:
                            ktg * 1040 + (2 * c2 + 1) * 65 + 64],
                        bank[:, 0:64].bitcast(BF16)[:, 64:128])
                vn_ev[i] = ph.j("vector", cpy, waits=[("mm", m)], tick="v")

        self.qkv_phase(ph, list(range(NEC)), wload, rhs, evac, 1024,
                       weng="sync", wsem="io", kv_hook=kv_hook)
        ph.run()

    def projb_cb(self, ph, c, nrm_last):
        st = self._projb
        outT, wim = st["outT"], st["wim"]
        if st["wl"] is None:
            wo3 = self.woutT_b.rearrange("(ec p) j -> p ec j", p=128)
            st["wl"] = ph.j("gpsimd", lambda h: h.dma_start(
                out=wim.rearrange("p (ec j) -> p ec j", j=1024), in_=wo3),
                tick="w", n=16)
        last_v = 0
        for ttl in range(4):
            tt = c * 4 + ttl
            for half in range(2):
                def mmf(h, ttl=ttl, half=half):
                    last = None
                    for ec in range(NDC):
                        last = h.matmul(self.ps[7][:, 0:512],
                                        outT[:, ec * 512 + ttl * 128:
                                             ec * 512 + (ttl + 1) * 128],
                                        wim[:, ec * 1024 + half * 512:
                                            ec * 1024 + half * 512 + 512],
                                        start=(ec == 0), stop=(ec == NDC - 1))
                    return last
                m = ph.j("tensor", mmf,
                         waits=[("w", st["wl"]), ("v", nrm_last),
                                ("v", last_v)], tick="mm")
                def resadd(h, tt=tt, half=half):
                    xs = self.x_sb[:, tt * D + half * 512:
                                   tt * D + half * 512 + 512]
                    h.tensor_add(xs, xs, self.ps[7][:, 0:512])
                    h.drain()
                    return h.tensor_add(xs, xs,
                                        st["bb"][:, half * 512:half * 512 + 512])
                last_v = ph.j("vector", resadd, waits=[("mm", m)], tick="v")
        return last_v

    def build(self):
        self.declare()
        self.p_init()
        if self.upto >= 2:
            self.p_quant_weights()
        if self.upto >= 3:
            self.p_fold_wqkvb()
        if self.upto >= 1:
            self.p_build_Ttab()
        for sq in range(self.nseq):
            self.p_xload(sq)
            if self.upto >= 1:
                self.p_attn_a(sq)
            if self.upto >= 2:
                self.p_ffn(sq, 0)
            if self.upto >= 3:
                self.p_attn_b(sq)
            if self.upto >= 4:
                self.p_ffn(sq, 1)
            self.p_store(sq)
        ph = Phase(self, "fin")
        ph.j("sync", lambda h: h.nop(),
             waits=[("io", self.sv["io"]), ("w", self.sv["w"]),
                    ("v", self.sv["v"]), ("mm", self.sv["mm"]),
                    ("sc", self.sv["sc"]), ("g", self.sv["g"]),
                    ("ev", self.sv["ev"])])
        ph.run(tail=False)
        self.st.close()
        return self.nc


def _tables():
    f = {}
    d = np.arange(1024, dtype=np.float64)
    fidx = np.arange(513, dtype=np.float64)
    angD = 2.0 * np.pi * np.outer(d, fidx) / 1024.0
    for nm, fn in [("cosD", np.cos), ("sinD", np.sin)]:
        tab = fn(angD)
        if nm == "sinD":
            tab[:, 0] = 0.0
            tab[:, 512] = 0.0
        hi = tab.astype(np.float32)
        lo = (tab - hi.astype(np.float64)).astype(np.float32)
        for sfx, arr in [("_hi", hi), ("_lo", lo)]:
            img = arr.reshape(8, 128, 513).transpose(1, 0, 2).reshape(128, 8 * 513)
            f[nm + sfx] = np.ascontiguousarray(img)
    t = np.arange(2048, dtype=np.float64)
    ang1 = 2.0 * np.pi * np.outer(t, np.arange(16, dtype=np.float64) * 128.0) / 2048.0
    ang2 = 2.0 * np.pi * np.outer(t, np.arange(128, dtype=np.float64)) / 2048.0
    for nm, ang, w in [("c1", ang1, 16), ("s1", ang1, 16),
                       ("c2", ang2, 128), ("s2", ang2, 128)]:
        tab = (np.cos if nm[0] == "c" else np.sin)(ang)
        hi = tab.astype(np.float32)
        lo = (tab - hi.astype(np.float64)).astype(np.float32)
        for sfx, arr in [("_hi", hi), ("_lo", lo)]:
            img = arr.reshape(16, 128, w).transpose(1, 0, 2).reshape(128, 16 * w)
            f[nm + sfx] = np.ascontiguousarray(img)
    r = np.arange(128)
    tq = np.arange(512)
    f["maskA"] = np.where(tq[None, :] < 4 * r[:, None] + 3,
                          np.float32(-1e38), np.float32(0.0)).astype(BF)
    w = np.arange(896)
    f["maskB"] = np.where(w[None, :] < r[:, None] + 384,
                          np.float32(-1e38), np.float32(0.0)).astype(BF)
    f["identc"] = np.eye(128, dtype=BF)
    rs = np.zeros((128, 16, 128), dtype=BF)
    for hh in range(16):
        rs[hh, hh, :] = 1.0
        rs[64 + hh, hh, :] = 1.0
    f["rowsel"] = np.ascontiguousarray(rs.reshape(128, 16 * 128))
    hl = np.zeros((128, 8, 16), dtype=BF)
    for dc in range(8):
        for rr in range(128):
            hl[rr, dc, 2 * dc + rr // 64] = 1.0
    f["hlhs"] = np.ascontiguousarray(hl.reshape(128, 128))
    return f


_NC_CACHE = {}


def _get_nc(upto, nseq):
    key = (upto, nseq)
    if key not in _NC_CACHE:
        _NC_CACHE[key] = Bld(upto, nseq).build()
    return _NC_CACHE[key]


def _host_inputs(inputs):
    conv_w = np.ascontiguousarray(inputs["L0_attn_a_conv_w"])
    cw = conv_w.transpose(2, 1, 0)
    cwi = np.zeros((128, 4 * 1024), dtype=np.float32)
    for kap in range(4):
        cwi[0:64, kap * 1024:(kap + 1) * 1024] = cw[kap]
        cwi[64:128, kap * 1024:(kap + 1) * 1024] = cw[kap]
    nullk = np.asarray(inputs["L0_attn_a_null_k"]).reshape(16, 64)
    nki = np.zeros((128, 16), dtype=np.float32)
    nki[0:64, :] = nullk.T
    nki[64:128, :] = nullk.T
    nullv = np.asarray(inputs["L0_attn_a_null_v"], dtype=np.float32).reshape(16, 64)
    nvi = np.concatenate([nullv, np.ones((16, 1), np.float32)], axis=1).reshape(1, 16 * 65)
    nvi = np.ascontiguousarray(nvi)
    col8 = lambda b: np.ascontiguousarray(
        np.asarray(b, dtype=np.float32).reshape(8, 128).T)
    ins = {
        "wqkv_a": np.ascontiguousarray(inputs["L0_attn_a_Wqkv"], dtype=np.float32),
        "convw": cwi,
        "convb": col8(inputs["L0_attn_a_conv_b"]),
        "nullk": nki,
        "nullv": nvi,
        "wout_a": np.ascontiguousarray(inputs["L0_attn_a_W_out"], dtype=np.float32),
        "bout_a": col8(inputs["L0_attn_a_b_out"]),
        "gam_b": col8(inputs["L0_attn_b_ln_gamma"]),
        "bet_b": col8(inputs["L0_attn_b_ln_beta"]),
        "wqkvT_b": np.ascontiguousarray(
            np.asarray(inputs["L0_attn_b_w_qkv"], dtype=np.float32).T),
        "woutT_b": np.ascontiguousarray(
            np.asarray(inputs["L0_attn_b_w_out"], dtype=np.float32).T),
        "bout_b": np.ascontiguousarray(
            np.asarray(inputs["L0_attn_b_b_out"], dtype=np.float32).reshape(1, 1024)),
        "w_gate": np.ascontiguousarray(inputs["L0_ffn_w_gate"], dtype=np.float32),
        "w_up": np.ascontiguousarray(inputs["L0_ffn_w_up"], dtype=np.float32),
        "w_down": np.ascontiguousarray(inputs["L0_ffn_w_down"], dtype=np.float32),
    }
    ins.update(_tables())
    return ins


def kernel(**inputs):
    n_cores, nseq = 8, 2
    x = np.ascontiguousarray(np.asarray(inputs["L0_attn_a_x"], dtype=np.float32))
    ins = _host_inputs(inputs)
    nc = _get_nc(4, nseq)
    in_maps = []
    for core in range(n_cores):
        m = dict(ins)
        m["x"] = np.ascontiguousarray(x[core * nseq:(core + 1) * nseq])
        in_maps.append(m)
    res = run_bass_kernel_spmd(nc, in_maps, core_ids=list(range(n_cores)))
    out = np.concatenate([r["y"] for r in res.results], axis=0)
    return out.astype(np.float32)


# version 10 (final)
# speedup vs baseline: 1.1047x; 1.0183x over previous
import numpy as np
import ml_dtypes
from contextlib import ExitStack
import concourse.bass as bass
import concourse.mybir as mybir
from concourse.bass_utils import run_bass_kernel_spmd

F32 = mybir.dt.float32
BF16 = mybir.dt.bfloat16
Alu = mybir.AluOpType
Act = mybir.ActivationFunctionType
AX = mybir.AxisListType
BF = ml_dtypes.bfloat16

T, D, NH, DH, CF, CK, II = 2048, 1024, 16, 64, 4, 512, 2816
NT, NDC, NTC, NEC, NIC = T // 128, D // 128, T // 512, 3 * D // 128, II // 128
MIDS = [0.25, 0.75, 1.25, 1.75, 2.5, 3.5, 5.0]
CNTS = [1.0, 1.0, 1.0, 1.0, 2.0, 2.0, 4.0]
ICS = [(0, 512), (512, 512), (1024, 512), (1536, 512), (2048, 512), (2560, 256)]
ENG = ("sync", "gpsimd", "tensor", "vector", "scalar")


class Phase:

    def __init__(self, b, name):
        self.b, self.name = b, name
        self.jobs = {e: [] for e in ENG}

    def j(self, eng, fn, waits=(), tick=None, n=1):
        v = None
        if tick is not None:
            self.b.sv[tick] += n
            v = self.b.sv[tick]
        self.jobs[eng].append((fn, tuple(waits), tick, n))
        return v

    def tail(self):
        for e in ("sync", "gpsimd"):
            if self.jobs[e]:
                nm = "io" if e == "sync" else "w"
                self.jobs[e].append(
                    (lambda h: h.nop(), (( nm, self.b.sv[nm]),), None, 1))

    def run(self, tail=True):
        b = self.b
        if tail:
            self.tail()
        with b.nc.Block(self.name) as blk:
            for e in ENG:
                jl = self.jobs[e]
                if not jl:
                    continue

                def body(h, jl=jl):
                    cur = {}
                    for fn, waits, tick, n in jl:
                        for snm, val in waits:
                            if val and cur.get(snm, -1) < val:
                                h.wait_ge(getattr(b, "s_" + snm), val)
                                cur[snm] = val
                        inst = fn(h)
                        if tick is not None:
                            inst.then_inc(getattr(b, "s_" + tick), n)
                getattr(blk, e)(body)


class Bld:
    def __init__(self, upto=4, nseq=2):
        self.upto, self.nseq = upto, nseq
        self.nc = bass.Bass("TRN2", target_bir_lowering=False)
        self.st = ExitStack()

    def declare(self):
        nc, st = self.nc, self.st
        self.sv = {}
        for n in ["w", "io", "mm", "ev", "v", "sc", "g"]:
            setattr(self, "s_" + n, st.enter_context(nc.semaphore("s_" + n)))
            self.sv[n] = 0

        def din(name, shape, dtype=F32):
            return nc.dram_tensor(name, list(shape), dtype, kind="ExternalInput").ap()

        self.x_in = din("x", (self.nseq, T, D))
        self.y_out = nc.dram_tensor("y", [self.nseq, T, D], F32,
                                    kind="ExternalOutput").ap()
        self.dbg_out = nc.dram_tensor("dbg", [128, NT * D], F32,
                                      kind="ExternalOutput").ap()
        self.wqkv_a = din("wqkv_a", (D, 3 * D))
        self.convw_i = din("convw", (128, 4 * D))
        self.convb_i = din("convb", (128, NDC))
        self.nullk_i = din("nullk", (128, NH))
        self.nullv_i = din("nullv", (1, NH * 65))
        self.wout_a = din("wout_a", (D, D))
        self.bouta_i = din("bout_a", (128, NDC))
        self.gam_i = din("gam_b", (128, NDC))
        self.bet_i = din("bet_b", (128, NDC))
        self.wqkvT_b = din("wqkvT_b", (D, 3 * D))
        self.woutT_b = din("woutT_b", (D, D))
        self.boutb_i = din("bout_b", (1, D))
        self.w_gate = din("w_gate", (II, D))
        self.w_up = din("w_up", (II, D))
        self.w_down = din("w_down", (D, II))
        self.maskA_i = din("maskA", (128, 512), BF16)
        self.maskB_i = din("maskB", (128, 896), BF16)
        self.ident_i = din("identc", (128, 128), BF16)
        self.hlhs_i = din("hlhs", (128, NDC * NH), BF16)
        self.rowsel_i = din("rowsel", (128, NH * 128), BF16)
        self.cosD_hi = din("cosD_hi", (128, NDC * 513))
        self.cosD_lo = din("cosD_lo", (128, NDC * 513))
        self.sinD_hi = din("sinD_hi", (128, NDC * 513))
        self.sinD_lo = din("sinD_lo", (128, NDC * 513))
        self.c1_hi = din("c1_hi", (128, NT * 16)); self.c1_lo = din("c1_lo", (128, NT * 16))
        self.s1_hi = din("s1_hi", (128, NT * 16)); self.s1_lo = din("s1_lo", (128, NT * 16))
        self.c2_hi = din("c2_hi", (128, NT * 128)); self.c2_lo = din("c2_lo", (128, NT * 128))
        self.s2_hi = din("s2_hi", (128, NT * 128)); self.s2_lo = din("s2_lo", (128, NT * 128))

        itn = lambda n, s, dt=BF16: nc.dram_tensor(n, list(s), dt, kind="Internal").ap()
        self.sc_cosT = itn("sc_cosT", (128, NT * T))
        self.sc_sinT = itn("sc_sinT", (128, NT * T))
        self.sc_wg = itn("sc_wg", (128, NDC * II))
        self.sc_wu = itn("sc_wu", (128, NDC * II))
        self.sc_wd = itn("sc_wd", (128, NIC * D))
        self.sc_wqkvb = itn("sc_wqkvb", (128, NDC * 3 * D))
        self.sc_hqT = itn("sc_hqT", (128, NIC * T))

        sb = lambda n, s, dt=F32: st.enter_context(
            nc.sbuf_tensor("sb_" + n, list(s), dt))
        self.x_sb = sb("x_sb", (128, NT * D))
        self.ident = sb("ident", (128, 128), BF16)
        self.hlhs = sb("hlhs", (128, NDC * NH), BF16)
        self.convb = sb("convb", (128, NDC))
        self.nullk = sb("nullk", (128, NH), BF16)
        self.nullv = sb("nullv", (1, NH * 65), BF16)
        self.bouta = sb("bouta", (128, NDC))
        self.gam = sb("gam", (128, NDC))
        self.bet = sb("bet", (128, NDC))
        self.betW = sb("betW", (128, NEC))
        self.boutb = sb("boutb", (1, D), BF16)
        self.onesr = sb("onesr", (1, 128), BF16)
        self.onesc = sb("onesc", (128, 1), BF16)
        self.onesF = sb("onesF", (128, 128))
        self.onesB = sb("onesB", (128, 64), BF16)
        self.negc = sb("negc", (128, NH))
        self.smal = sb("smal", (128, 96))
        self.ps = [st.enter_context(nc.psum_tensor(f"ps{i}", [128, 512], F32))
                   for i in range(8)]

    def uname(self, n):
        self._uid = getattr(self, "_uid", 0) + 1
        return f"{n}_u{self._uid}"

    def dbuf(self, es, name, nbuf, cols, dt=F32):
        t = es.enter_context(
            self.nc.sbuf_tensor(self.uname(name), [128, nbuf * cols], dt))
        return [t[:, i * cols:(i + 1) * cols] for i in range(nbuf)]

    def mk_qtmp(self, es, n):
        nc = self.nc
        t = {}
        for nm, sz, dt in [("amax", n // 16, F32), ("trc", n // 16, F32),
                           ("ntb", n // 16, F32), ("ws", n, F32), ("aw", n, F32),
                           ("e1", n, BF16), ("e2", n, BF16), ("e3", n, BF16),
                           ("ms", n, BF16)]:
            t[nm] = es.enter_context(nc.sbuf_tensor(self.uname("q_" + nm), [128, sz], dt))
        t["msn"] = t["ws"]
        return t

    def quant_head(self, v, src_ap, q, n):
        g = n // 16
        amax, trc, ntb = q["amax"], q["trc"], q["ntb"]
        r3 = lambda ap: ap.rearrange("p (g k) -> p g k", k=16)
        v.drain()
        v.tensor_reduce(amax[:, 0:g], r3(src_ap), AX.X, Alu.max,
                        apply_absolute_value=True)
        v.drain()
        v.tensor_scalar(trc[:, 0:g], amax[:, 0:g], 6e-12, 1.0 / 6.0,
                        Alu.max, Alu.mult)
        v.tensor_scalar(ntb[:, 0:g], amax[:, 0:g], 6e-12, -1.0 / 12.0,
                        Alu.max, Alu.mult)
        return v.reciprocal(trc[:, 0:g], trc[:, 0:g])

    def quant_sign(self, e, src_ap, q, n, c0, c1):
        return e.activation(q["ms"][:, c0:c1], src_ap[:, c0:c1], Act.Sign,
                            scale=-1.0)

    def quant_steps(self, e, src_ap, q, n, c0, c1):
        g0, g1 = c0 // 16, c1 // 16
        gg = g1 - g0
        trc = q["trc"]
        ws, aw, e1, e2, e3 = q["ws"], q["aw"], q["e1"], q["e2"], q["e3"]
        sl = lambda ap: ap[:, c0:c1]
        e.drain()
        bc = trc[:, g0:g1].unsqueeze(2).broadcast_to([128, gg, 16])
        e.tensor_tensor(sl(ws).rearrange("p (g k) -> p g k", k=16),
                        src_ap[:, c0:c1].rearrange("p (g k) -> p g k", k=16),
                        bc, Alu.mult)
        e.scalar_tensor_tensor(sl(aw), sl(ws), -1.0, sl(ws), Alu.mult, Alu.max)
        e.tensor_scalar(sl(e1), sl(aw), MIDS[0], CNTS[0], Alu.is_gt, Alu.mult)
        e.tensor_scalar(sl(e2), sl(aw), MIDS[1], CNTS[1], Alu.is_gt, Alu.mult)
        e.tensor_scalar(sl(e3), sl(aw), MIDS[2], CNTS[2], Alu.is_gt, Alu.mult)
        e.tensor_add(sl(e1), sl(e1), sl(e2))
        e.tensor_scalar(sl(e2), sl(aw), MIDS[3], CNTS[3], Alu.is_gt, Alu.mult)
        e.tensor_add(sl(e1), sl(e1), sl(e3))
        e.tensor_scalar(sl(e3), sl(aw), MIDS[4], CNTS[4], Alu.is_gt, Alu.mult)
        e.tensor_add(sl(e1), sl(e1), sl(e2))
        e.tensor_scalar(sl(e2), sl(aw), MIDS[5], CNTS[5], Alu.is_gt, Alu.mult)
        e.tensor_add(sl(e1), sl(e1), sl(e3))
        e.tensor_scalar(sl(e3), sl(aw), MIDS[6], CNTS[6], Alu.is_gt, Alu.mult)
        e.tensor_add(sl(e1), sl(e1), sl(e2))
        return e.tensor_add(sl(e1), sl(e1), sl(e3))

    def quant_fin(self, e, out_ap, q, n, c0, c1):
        g0, g1 = c0 // 16, c1 // 16
        gg = g1 - g0
        ntb, e1, ms, msn = q["ntb"], q["e1"], q["ms"], q["msn"]
        r3 = lambda ap: ap[:, c0:c1].rearrange("p (g k) -> p g k", k=16)
        bn = ntb[:, g0:g1].unsqueeze(2).broadcast_to([128, gg, 16])
        e.tensor_tensor(r3(msn), r3(ms), bn, Alu.mult)
        return e.tensor_tensor(
            out_ap[:, c0:c1].rearrange("p (g k) -> p g k", k=16), r3(e1),
            r3(msn), Alu.mult)

    def quant(self, v, src_ap, out_ap, q, n):
        self.quant_head(v, src_ap, q, n)
        self.quant_steps(v, src_ap, q, n, 0, n)
        return self.quant_fin(v, out_ap, q, n, 0, n)

    def maybe_probe(self, name, ap):
        if getattr(self, "probe", None) != name or ap is None:
            return
        ph = Phase(self, "probe_" + name)
        cols = ap.shape[-1] if len(ap.shape) == 2 else ap.shape[1]
        if ap.dtype == BF16:
            ph.j("gpsimd", lambda h: h.dma_start(
                out=self.dbg_out[0:ap.shape[0], 0:cols], in_=ap), tick="w", n=16)
        else:
            ph.j("sync", lambda h: h.dma_start(
                out=self.dbg_out[0:ap.shape[0], 0:cols], in_=ap), tick="io", n=16)
        ph.run()

    def p_init(self):
        ph = Phase(self, "init")
        for dst, src in [(self.ident, self.ident_i), (self.hlhs, self.hlhs_i),
                         (self.convb, self.convb_i), (self.bouta, self.bouta_i),
                         (self.gam, self.gam_i), (self.bet, self.bet_i)]:
            ph.j("sync", lambda h, d=dst, s=src: h.dma_start(out=d[:], in_=s[:]),
                 tick="io", n=16)
        for dst, src in [(self.nullk, self.nullk_i), (self.nullv, self.nullv_i),
                         (self.boutb, self.boutb_i)]:
            ph.j("gpsimd", lambda h, d=dst, s=src: h.dma_start(out=d[:], in_=s[:]),
                 tick="w", n=16)
        ph.j("gpsimd", lambda h: h.memset(self.onesc[:], 1.0), tick="g")
        ph.j("gpsimd", lambda h: h.memset(self.onesr[:], 1.0), tick="g")
        ph.j("gpsimd", lambda h: h.memset(self.onesF[:], 1.0), tick="g")
        ph.j("gpsimd", lambda h: h.memset(self.onesB[:], 1.0), tick="g")
        ph.run()

    def p_quant_weights(self):
        nc = self.nc
        with ExitStack() as es:
            q = self.mk_qtmp(es, II)
            wt = self.dbuf(es, "qw_wt", 2, II, F32)
            wq = self.dbuf(es, "qw_wq", 2, II, BF16)
            stw = self.dbuf(es, "qw_st", 2, II, BF16)
            ph = Phase(self, "qw_all")
            carry = (0, 0, 0)
            for w_ap, rows, sc_out, cols in [
                    (self.w_gate, II, self.sc_wg, D),
                    (self.w_up, II, self.sc_wu, D),
                    (self.w_down, D, self.sc_wd, II)]:
                carry = self._qw_body(ph, q, wt, wq, stw, w_ap, rows, sc_out,
                                      cols, carry)
            ph.run()

    def _qw_body(self, ph, q, wt, wq, stw, w_ap, rows, sc_out, cols, carry):
        nrt, nct = rows // 128, cols // 128
        sc3 = sc_out.rearrange("p (ct r) -> p ct r", r=rows)
        pv, psn, pio = carry
        vtk = {-2: pv, -1: pv}
        stk = {-2: psn, -1: psn}
        iotk = {-2: pio, -1: pio}
        for rt in range(nrt):
            ld = ph.j("sync", lambda h, rt=rt: h.dma_start(
                out=wt[rt % 2][:, 0:cols],
                in_=w_ap[rt * 128:(rt + 1) * 128, :]),
                waits=[("v", vtk.get(rt - 2, 0)),
                       ("sc", stk.get(rt - 2, 0))], tick="io", n=16)
            vh = ph.j("vector", lambda h, rt=rt: self.quant_head(
                h, wt[rt % 2][:, 0:cols], q, cols),
                waits=[("io", ld), ("io", iotk.get(rt - 2, 0))], tick="v")
            stk[rt] = ph.j("scalar", lambda h, rt=rt: self.quant_sign(
                h, wt[rt % 2], q, cols, 0, cols),
                waits=[("io", ld), ("v", vtk.get(rt - 1, 0))], tick="sc")
            ph.j("vector", lambda h, rt=rt: self.quant_steps(
                h, wt[rt % 2][:, 0:cols], q, cols, 0, cols), tick="v")
            vtk[rt] = ph.j("vector", lambda h, rt=rt: self.quant_fin(
                h, wq[rt % 2][:, 0:cols], q, cols, 0, cols),
                waits=[("sc", stk[rt])], tick="v")
            for ct in range(nct):
                ph.j("sync", lambda h, rt=rt, ct=ct: h.dma_start_transpose(
                    stw[rt % 2][:, ct * 128:(ct + 1) * 128],
                    wq[rt % 2][:, ct * 128:(ct + 1) * 128]),
                    waits=[("v", vtk[rt])], tick="io", n=16)
            iotk[rt] = ph.j("sync", lambda h, rt=rt: h.dma_start(
                out=sc3[:, 0:nct, rt * 128:(rt + 1) * 128],
                in_=stw[rt % 2][:, 0:cols].rearrange(
                    "p (ct j) -> p ct j", j=128)),
                tick="io", n=16)
        return (vtk[nrt - 1], stk[nrt - 1], iotk[nrt - 1])

    def p_fold_wqkvb(self):
        nc = self.nc
        with ExitStack() as es:
            wf = self.dbuf(es, "fb_wf", 2, 3 * D, F32)
            wb = self.dbuf(es, "fb_wb", 2, 3 * D, BF16)
            btc = es.enter_context(nc.sbuf_tensor(self.uname("fb_bt"), [128, NDC], BF16))
            ph = Phase(self, "foldb")
            bt0 = ph.j("vector", lambda h: h.tensor_copy(btc[:], self.bet[:]),
                       tick="v")
            cast_tk, fold_tk, st_tk, mm_tk = {}, {}, {}, {}
            for dc in range(NDC):
                ld = ph.j("sync", lambda h, dc=dc: h.dma_start(
                    out=wf[dc % 2][:], in_=self.wqkvT_b[dc * 128:(dc + 1) * 128, :]),
                    waits=[("v", cast_tk.get(dc - 2, 0))], tick="io", n=16)
                cast_tk[dc] = ph.j("vector", lambda h, dc=dc: h.tensor_copy(
                    wb[dc % 2][:], wf[dc % 2][:]),
                    waits=[("io", ld), ("io", st_tk.get(dc - 2, 0))], tick="v")
                def mmfn(h, dc=dc):
                    last = None
                    for ec in range(NEC):
                        last = h.matmul(self.ps[7][0:128, ec:ec + 1],
                                        wb[dc % 2][:, ec * 128:(ec + 1) * 128],
                                        btc[:, dc:dc + 1],
                                        start=(dc == 0 and ec == 0),
                                        stop=(dc == NDC - 1 and ec == NEC - 1))
                    return last
                mm_tk[dc] = ph.j("tensor", mmfn,
                                 waits=[("v", cast_tk[dc]), ("v", bt0)], tick="mm")
                fold_tk[dc] = ph.j("vector", lambda h, dc=dc: h.tensor_scalar_mul(
                    wb[dc % 2][:], wb[dc % 2][:], self.gam[:, dc:dc + 1]),
                    waits=[("mm", mm_tk[dc])], tick="v")
                st_tk[dc] = ph.j("sync", lambda h, dc=dc: h.dma_start(
                    out=self.sc_wqkvb[:, dc * 3 * D:(dc + 1) * 3 * D],
                    in_=wb[dc % 2][:]), waits=[("v", fold_tk[dc])], tick="io", n=16)
            ph.j("vector", lambda h: h.tensor_copy(self.betW[:, 0:NEC],
                                                   self.ps[7][:, 0:NEC]),
                 waits=[("mm", mm_tk[NDC - 1])], tick="v")
            ph.run()

    def p_build_Ttab(self):
        nc = self.nc
        with ExitStack() as es:
            t1h = es.enter_context(nc.sbuf_tensor(self.uname("t1h"), [128, NT * 16], F32))
            t1l = es.enter_context(nc.sbuf_tensor(self.uname("t1l"), [128, NT * 16], F32))
            u1h = es.enter_context(nc.sbuf_tensor(self.uname("u1h"), [128, NT * 16], F32))
            u1l = es.enter_context(nc.sbuf_tensor(self.uname("u1l"), [128, NT * 16], F32))
            t2h = es.enter_context(nc.sbuf_tensor(self.uname("t2h"), [128, NT * 128], F32))
            t2l = es.enter_context(nc.sbuf_tensor(self.uname("t2l"), [128, NT * 128], F32))
            u2h = es.enter_context(nc.sbuf_tensor(self.uname("u2h"), [128, NT * 128], F32))
            u2l = es.enter_context(nc.sbuf_tensor(self.uname("u2l"), [128, NT * 128], F32))
            stg = self.dbuf(es, "stg", 2, 2 * T, BF16)
            ta = es.enter_context(nc.sbuf_tensor(self.uname("tta"), [128, 128], F32))
            tb = es.enter_context(nc.sbuf_tensor(self.uname("ttb"), [128, 128], F32))
            ph = Phase(self, "ttab")
            lv = 0
            for src, dst in [(self.c1_hi, t1h), (self.c1_lo, t1l),
                             (self.s1_hi, u1h), (self.s1_lo, u1l),
                             (self.c2_hi, t2h), (self.c2_lo, t2l),
                             (self.s2_hi, u2h), (self.s2_lo, u2l)]:
                lv = ph.j("sync", lambda h, d=dst, s=src:
                          h.dma_start(out=d[:], in_=s[:]), tick="io", n=16)

            def dd(h):
                h.tensor_add(t1h[:], t1h[:], t1l[:])
                h.tensor_add(u1h[:], u1h[:], u1l[:])
                h.tensor_add(t2h[:], t2h[:], t2l[:])
                return h.tensor_add(u2h[:], u2h[:], u2l[:])
            ph.j("vector", dd, waits=[("io", lv)], tick="v")
            vtk, iotk = {}, {}
            for tt in range(NT):
                def bt(h, tt=tt):
                    b = stg[tt % 2]
                    c1 = t1h[:, tt * 16:(tt + 1) * 16]
                    s1 = u1h[:, tt * 16:(tt + 1) * 16]
                    c2 = t2h[:, tt * 128:(tt + 1) * 128]
                    s2 = u2h[:, tt * 128:(tt + 1) * 128]
                    last = None
                    for sc in range(16):
                        h.tensor_scalar_mul(ta[:], c2[:], c1[:, sc:sc + 1])
                        h.tensor_scalar_mul(tb[:], s2[:], s1[:, sc:sc + 1])
                        h.tensor_sub(b[:, sc * 128:(sc + 1) * 128], ta[:], tb[:])
                        h.tensor_scalar_mul(ta[:], c2[:], s1[:, sc:sc + 1])
                        h.tensor_scalar_mul(tb[:], s2[:], c1[:, sc:sc + 1])
                        last = h.tensor_add(b[:, T + sc * 128:T + (sc + 1) * 128],
                                            ta[:], tb[:])
                    return last
                vtk[tt] = ph.j("vector", bt, waits=[("io", iotk.get(tt - 2, 0))],
                               tick="v")
                ph.j("sync", lambda h, tt=tt: h.dma_start(
                    out=self.sc_cosT[:, tt * T:(tt + 1) * T],
                    in_=stg[tt % 2][:, 0:T]), waits=[("v", vtk[tt])],
                    tick="io", n=16)
                iotk[tt] = ph.j("sync", lambda h, tt=tt: h.dma_start(
                    out=self.sc_sinT[:, tt * T:(tt + 1) * T],
                    in_=stg[tt % 2][:, T:2 * T]), tick="io", n=16)
            ph.run()

    def p_xload(self, sq):
        ph = Phase(self, f"xl{sq}")
        ph.j("sync", lambda h: h.dma_start(
            out=self.x_sb.rearrange("p (tt d) -> p tt d", d=D),
            in_=self.x_in[sq].rearrange("(tt p) d -> p tt d", p=128)),
            tick="io", n=16)
        ph.run()

    def p_store(self, sq):
        ph = Phase(self, f"st{sq}")
        ph.j("sync", lambda h: h.dma_start(
            out=self.y_out[sq].rearrange("(tt p) d -> p tt d", p=128),
            in_=self.x_sb.rearrange("p (tt d) -> p tt d", d=D)),
            tick="io", n=16)
        ph.run()

    def ln_block(self, ph, tiles, xlnT, es, double=False, span=None, tmp2=None,
                 tmp=None):
        nc = self.nc
        span = span or len(tiles) * 128
        if double and tmp is None:
            tmp = es.enter_context(
                nc.sbuf_tensor(self.uname("ln_tmp"), [128, D], F32))
        if tmp2 is None:
            tmp2 = es.enter_context(nc.sbuf_tensor(self.uname("ln_tm2"), [128, D], F32))
        sums = es.enter_context(nc.sbuf_tensor(self.uname("ln_sums"), [128, 8], F32))
        xb = self.dbuf(es, "ln_xb", 2, D, BF16)

        def stats(h, src, c0):
            h.drain()
            h.tensor_tensor(tmp2[:, 0:D], src, src, Alu.mult)
            h.reduce_sum(sums[:, c0:c0 + 1], src, AX.X)
            h.reduce_sum(sums[:, c0 + 1:c0 + 2], tmp2[:, 0:D], AX.X)
            h.drain()
            h.scalar_tensor_tensor(sums[:, c0 + 2:c0 + 3], sums[:, c0:c0 + 1],
                                   1.0 / (D * D), sums[:, c0:c0 + 1],
                                   Alu.mult, Alu.mult)
            h.tensor_scalar_mul(sums[:, c0:c0 + 1], sums[:, c0:c0 + 1], 1.0 / D)
            h.tensor_scalar_mul(sums[:, c0 + 1:c0 + 2], sums[:, c0 + 1:c0 + 2],
                                1.0 / D)
            h.drain()
            return h.tensor_scalar(sums[:, c0 + 1:c0 + 2], sums[:, c0 + 1:c0 + 2],
                                   sums[:, c0 + 2:c0 + 3], 1e-5,
                                   Alu.subtract, Alu.add)

        sc_last, io_tk = {}, {}
        for i, tt in enumerate(tiles):
            xs = self.x_sb[:, tt * D:(tt + 1) * D]
            v1 = ph.j("vector", lambda h, xs=xs: stats(h, xs, 0),
                      waits=[("sc", sc_last.get(i - 1, 0)),
                             ("io", io_tk.get(i - 2, 0))], tick="v")
            s1 = ph.j("scalar", lambda h: h.activation(sums[:, 2:3], sums[:, 1:2],
                                                       Act.Sqrt),
                      waits=[("v", v1)], tick="sc")
            if not double:
                sc_last[i] = s1
                def ap1(h, i=i, xs=xs):
                    h.reciprocal(sums[:, 3:4], sums[:, 2:3])
                    h.drain()
                    return h.tensor_scalar(xb[i % 2][:], xs, sums[:, 0:1],
                                           sums[:, 3:4], Alu.subtract, Alu.mult)
                vfin = ph.j("vector", ap1, waits=[("sc", s1)], tick="v")
            else:
                def ap1(h, i=i, xs=xs):
                    h.reciprocal(sums[:, 3:4], sums[:, 2:3])
                    h.drain()
                    h.tensor_scalar(tmp[:, 0:D], xs, sums[:, 0:1], sums[:, 3:4],
                                    Alu.subtract, Alu.mult)
                    return stats(h, tmp[:, 0:D], 4)
                v2 = ph.j("vector", ap1, waits=[("sc", s1)], tick="v")
                s2 = ph.j("scalar", lambda h: h.activation(
                    sums[:, 6:7], sums[:, 5:6], Act.Sqrt),
                    waits=[("v", v2)], tick="sc")
                sc_last[i] = s2
                def ap2(h, i=i):
                    h.reciprocal(sums[:, 7:8], sums[:, 6:7])
                    h.drain()
                    return h.tensor_scalar(xb[i % 2][:], tmp[:, 0:D], sums[:, 4:5],
                                           sums[:, 7:8], Alu.subtract, Alu.mult)
                vfin = ph.j("vector", ap2, waits=[("sc", s2)], tick="v")
            for dc in range(NDC):
                io_tk[i] = ph.j("sync", lambda h, i=i, dc=dc: h.dma_start_transpose(
                    xlnT[:, dc * span + i * 128:dc * span + (i + 1) * 128],
                    xb[i % 2][:, dc * 128:(dc + 1) * 128]),
                    waits=[("v", vfin)], tick="io", n=16)

    def qkv_phase(self, ph, nec_list, wload_fn, rhs_fn, evac_fn, toks,
                  weng="gpsimd", wsem="w", kv_hook=None):
        ntc = toks // 512
        mm_tk, ev_tk, w_tk = {}, {}, {}
        for gi, ec in enumerate(nec_list):
            w_tk[gi] = ph.j(weng, lambda h, ec=ec, gi=gi: wload_fn(h, ec, gi % 2),
                            waits=[("mm", mm_tk.get((gi - 2, ntc - 1), 0))],
                            tick=wsem, n=16)
            for c in range(ntc):
                j = gi * ntc + c
                bank = self.ps[j % 4]
                def mmf(h, ec=ec, gi=gi, c=c, bank=bank):
                    last = None
                    for dc in range(NDC):
                        last = h.matmul(bank[:, 0:512],
                                        self.wbuf[gi % 2][:, dc * 128:(dc + 1) * 128],
                                        rhs_fn(ec, c, dc),
                                        start=(dc == 0), stop=(dc == NDC - 1))
                    return last
                mm_tk[(gi, c)] = ph.j("tensor", mmf,
                                      waits=[(wsem, w_tk[gi]),
                                             ("ev", ev_tk.get(j - 4, 0))],
                                      tick="mm")
                ev_tk[j] = ph.j("scalar", lambda h, ec=ec, c=c, bank=bank:
                                evac_fn(h, ec, c, bank),
                                waits=[("mm", mm_tk[(gi, c)])], tick="ev")
            if kv_hook is not None:
                kv_hook(ph, gi, ec, ev_tk[gi * ntc + ntc - 1])
        return mm_tk, ev_tk

    def p_attn_a(self, sq):
        nc = self.nc
        with ExitStack() as esy:
            big = esy.enter_context(nc.sbuf_tensor(self.uname("big0"), [128, NDC * T], BF16))
            yT = big
            with ExitStack() as esa:
                qT = esa.enter_context(nc.sbuf_tensor(self.uname("qT"), [128, NDC * T], BF16))
                kcT = esa.enter_context(nc.sbuf_tensor(self.uname("kcT"), [128, NDC * CK], BF16))
                vna = esa.enter_context(nc.sbuf_tensor(self.uname("vna"),
                                                       [128, 4 * 1040], BF16))
                cw = esa.enter_context(nc.sbuf_tensor(self.uname("cw"), [128, 4 * D], BF16))
                mkA = esa.enter_context(nc.sbuf_tensor(self.uname("mkA"), [128, 512], BF16))
                with ExitStack() as esb:
                    xlnT = big
                    vcT = esb.enter_context(
                        nc.sbuf_tensor(self.uname("vcT"), [128, NDC * CK], BF16))
                    self.wbuf = self.dbuf(esb, "wbuf", 2, 1024, BF16)
                    kvt = self.dbuf(esb, "kvt", 2, T, BF16)
                    with ExitStack() as esl:
                        ph = Phase(self, f"a{sq}_ln1")
                        ph.j("gpsimd", lambda h: h.dma_start(out=cw[:],
                                                             in_=self.convw_i[:]),
                             tick="w", n=16)
                        ph.j("sync", lambda h: h.dma_start(out=mkA[:],
                                                           in_=self.maskA_i[:]),
                             tick="io", n=16)
                        self.ln_block(ph, list(range(NT)), xlnT, esl,
                                      tmp2=kvt[0].bitcast(F32)[:, 0:D])
                        ph.run()
                    if sq == 0:
                        self.maybe_probe("xlnT", xlnT[:])
                    self.p_qkv_a_body(sq, xlnT, kvt, qT, kcT, vcT, cw)
                    if sq == 0:
                        self.maybe_probe("qT", qT[:])
                        self.maybe_probe("kcT", kcT[:])
                        self.maybe_probe("vcT", vcT[:])
                    self.p_vnat_a(sq, vcT, vna)
                    if sq == 0:
                        self.maybe_probe("vna", vna[:])
                with ExitStack() as esn:
                    q2 = self.dbuf(esn, "q2", 2, T, BF16)
                    nk2 = esn.enter_context(nc.sbuf_tensor(self.uname("nk2"), [128, NH], BF16))
                    self.hn = esn.enter_context(nc.sbuf_tensor(self.uname("hn"), [16, T], BF16))
                    ph = Phase(self, f"a{sq}_hn")
                    self.hnorm_common(ph, qT, T, 1.0 / 32.0, q2, kcT, CK, nk2=nk2)
                    ph.run()
                    if sq == 0:
                        self.maybe_probe("negc", self.negc[:, 0:16])
                        self.maybe_probe("smal", self.smal[0:128, 0:96])
                        self.maybe_probe("hn", self.hn[0:16, 0:T])
                with ExitStack() as esc:
                    outT = esc.enter_context(
                        nc.sbuf_tensor(self.uname("outT"), [128, NDC * T], BF16))
                    pb = self.dbuf(esc, "pb", 2, 512, BF16)
                    pn = esc.enter_context(nc.sbuf_tensor(self.uname("pn"), [128, 512], BF16))
                    self.rd = esc.enter_context(nc.sbuf_tensor(self.uname("rd"), [128, 512], F32))
                    self.rdb = esc.enter_context(nc.sbuf_tensor(self.uname("rdb"), [128, 512], F32))
                    self._core_es = esc
                    self.p_attn_core(sq, "a", qT, kcT, vna, outT, pb, pn, mkA)
                    if sq == 0:
                        self.maybe_probe("outT", outT[:])
                    with ExitStack() as esd:
                        wcol = self.dbuf(esd, "wcol", 2, 1024, BF16)
                        self.p_proj_a(sq, outT, wcol, yT)
            if sq == 0:
                self.maybe_probe("yT", yT[:])
            self.p_fft(sq, yT)

    def p_qkv_a_body(self, sq, xlnT, kvt, qT, kcT, vcT, cw):
        ph = Phase(self, f"a{sq}_qkv")
        wq3 = self.wqkv_a.rearrange("(dc p) e -> p dc e", p=128)

        def wload(h, ec, buf):
            return h.dma_start(
                out=self.wbuf[buf].rearrange("p (dc j) -> p dc j", j=128),
                in_=wq3[:, :, ec * 128:(ec + 1) * 128])

        def rhs(ec, c, dc):
            return xlnT[:, dc * T + c * 512:dc * T + c * 512 + 512]

        def evac(h, ec, c, bank):
            if ec < 8:
                return h.copy(qT[:, ec * T + c * 512:ec * T + c * 512 + 512],
                              bank[:, 0:512])
            return h.copy(kvt[ec % 2][:, c * 512:c * 512 + 512], bank[:, 0:512])

        conv_ev = {}
        conv_n = [0]

        def kv_hook(ph, gi, ec, evlast):
            if ec < 8:
                return
            is_v = ec >= 16
            c2 = ec - 16 if is_v else ec - 8
            for gl in range(2):
                gs = conv_n[0]
                conv_n[0] += 1
                grp = 2 * c2 + gl
                po = 64 * gl
                bank = self.ps[4 + gs % 2]
                def cmm(h, ec=ec, grp=grp, po=po, bank=bank):
                    last = None
                    for kap in range(CF):
                        last = h.matmul(
                            bank[po:po + 64, 0:512],
                            cw[po:po + 64,
                               kap * D + grp * 64:kap * D + (grp + 1) * 64],
                            kvt[ec % 2][po:po + 64, 0:T].rearrange(
                                "p (j k) -> p j k", k=CF)[:, :, kap],
                            start=(kap == 0), stop=(kap == CF - 1),
                            tile_position=(po, po))
                    return last
                cm = ph.j("tensor", cmm,
                          waits=[("ev", evlast), ("v", conv_ev.get(gs - 2, 0)),
                                 ("w", self.sv["w"])],
                          tick="mm")
                img = vcT if is_v else kcT
                def cev(h, c2=c2, po=po, bank=bank, img=img):
                    dst = img[po:po + 64, c2 * CK:(c2 + 1) * CK]
                    return h.tensor_scalar_add(dst, bank[po:po + 64, 0:512],
                                               self.convb[po:po + 64, c2:c2 + 1])
                conv_ev[gs] = ph.j("vector", cev, waits=[("mm", cm)], tick="v")

        self.qkv_phase(ph, list(range(NEC)), wload, rhs, evac, T, kv_hook=kv_hook)
        ph.run()

    def p_vnat_a(self, sq, vcT, vna):
        ph = Phase(self, f"a{sq}_vn")
        on = ph.j("gpsimd", lambda h: h.memset(
            vna.rearrange("p (k o) -> p k o", o=65)[:, :, 64:65], 1.0), tick="g")
        tk = {}
        for c2 in range(NDC):
            for kt in range(4):
                i = c2 * 4 + kt
                bank = self.ps[i % 2]
                m = ph.j("tensor", lambda h, c2=c2, kt=kt, bank=bank: h.transpose(
                    bank[:, 0:64].bitcast(BF16),
                    vcT[:, c2 * CK + kt * 128:c2 * CK + (kt + 1) * 128],
                    self.ident[:]),
                    waits=[("v", tk.get(i - 2, 0)), ("g", on)], tick="mm")
                def cpy(h, c2=c2, kt=kt, bank=bank):
                    h.tensor_copy(
                        vna[:, kt * 1040 + (2 * c2) * 65:
                            kt * 1040 + (2 * c2) * 65 + 64],
                        bank[:, 0:64].bitcast(BF16)[:, 0:64])
                    return h.tensor_copy(
                        vna[:, kt * 1040 + (2 * c2 + 1) * 65:
                            kt * 1040 + (2 * c2 + 1) * 65 + 64],
                        bank[:, 0:64].bitcast(BF16)[:, 64:128])
                tk[i] = ph.j("vector", cpy, waits=[("mm", m)], tick="v")
        ph.run()

    def hnorm_common(self, ph, qT, qlen, scale, q2, kimg, klen, nk2=None):
        nqc, nkc = qlen // 512, klen // 512
        sq_tk, mm_tk = {}, {}
        for dc in range(NDC):
            sq_tk[dc] = ph.j("scalar", lambda h, dc=dc: h.activation(
                q2[dc % 2][:, 0:qlen], qT[:, dc * qlen:(dc + 1) * qlen], Act.Square),
                waits=[("mm", mm_tk.get((dc - 2, nqc - 1), 0))], tick="sc")
            for c in range(nqc):
                mm_tk[(dc, c)] = ph.j("tensor", lambda h, dc=dc, c=c: h.matmul(
                    self.ps[c][0:16, 0:512], self.hlhs[:, dc * NH:(dc + 1) * NH],
                    q2[dc % 2][:, c * 512:(c + 1) * 512],
                    start=(dc == 0), stop=(dc == NDC - 1)),
                    waits=[("sc", sq_tk[dc])], tick="mm")
        ev = {}
        for c in range(nqc):
            ev[c] = ph.j("scalar", lambda h, c=c: h.activation(
                self.hn[0:16, c * 512:(c + 1) * 512], self.ps[c][0:16, 0:512],
                Act.Sqrt), waits=[("mm", mm_tk[(NDC - 1, nqc - 1)])], tick="sc")
        def vqf(h):
            h.reduce_max(self.smal[0:16, 0:1], self.hn[0:16, 0:qlen], AX.X)
            return h.drain()
        vq = ph.j("vector", vqf, waits=[("sc", ev[nqc - 1])], tick="v")
        mm2, s2_tk = {}, {}
        for dc in range(NDC):
            s2_tk[dc] = ph.j("scalar", lambda h, dc=dc: h.activation(
                q2[dc % 2][:, 0:klen], kimg[:, dc * klen:(dc + 1) * klen],
                Act.Square),
                waits=[("v", vq), ("mm", mm2.get((dc - 2, nkc - 1), 0))], tick="sc")
            for c in range(nkc):
                mm2[(dc, c)] = ph.j("tensor", lambda h, dc=dc, c=c: h.matmul(
                    self.ps[4 + c][0:16, 0:512], self.hlhs[:, dc * NH:(dc + 1) * NH],
                    q2[dc % 2][:, c * 512:(c + 1) * 512],
                    start=(dc == 0), stop=(dc == NDC - 1)),
                    waits=[("sc", s2_tk[dc])], tick="mm")
        ev2 = {}
        for c in range(nkc):
            ev2[c] = ph.j("scalar", lambda h, c=c: h.activation(
                self.hn[0:16, c * 512:(c + 1) * 512], self.ps[4 + c][0:16, 0:512],
                Act.Sqrt), waits=[("mm", mm2[(NDC - 1, nkc - 1)])], tick="sc")
        def vkf(h):
            h.reduce_max(self.smal[0:16, 1:2], self.hn[0:16, 0:klen], AX.X)
            return h.drain()
        vk = ph.j("vector", vkf, waits=[("sc", ev2[nkc - 1])], tick="v")
        lastv = vk
        if nk2 is not None:
            s3 = ph.j("scalar", lambda h: h.activation(
                nk2[0:64, :], self.nullk[0:64, :], Act.Square), tick="sc")
            m3 = ph.j("tensor", lambda h: h.matmul(
                self.ps[6][0:16, 0:1], nk2[0:64, 0:16], self.onesc[0:64, 0:1],
                start=True, stop=True), waits=[("sc", s3), ("v", vk)], tick="mm")
            s4 = ph.j("scalar", lambda h: h.activation(
                self.smal[0:16, 2:3], self.ps[6][0:16, 0:1], Act.Sqrt),
                waits=[("mm", m3)], tick="sc")
            def vmx(h):
                h.tensor_max(self.smal[0:16, 1:2], self.smal[0:16, 1:2],
                             self.smal[0:16, 2:3])
                return h.drain()
            lastv = ph.j("vector", vmx, waits=[("sc", s4)], tick="v")

        def fin(h, scale=scale):
            h.tensor_tensor(self.smal[0:16, 3:4], self.smal[0:16, 0:1],
                            self.smal[0:16, 1:2], Alu.mult)
            h.drain()
            h.tensor_scalar_mul(self.smal[0:16, 3:4],
                                self.smal[0:16, 3:4], -scale)
            return h.drain()
        vf = ph.j("vector", fin, waits=[("v", lastv)], tick="v")
        dg = ph.j("gpsimd", lambda h: h.dma_start(
            out=self.smal[0:1, 64:80], in_=self.smal[0:16, 3:4]),
            waits=[("v", vf)], tick="w", n=16)
        bm = ph.j("tensor", lambda h: h.matmul(
            self.ps[7][0:128, 0:16], self.onesF[0:1, 0:128],
            self.smal[0:1, 64:80], start=True, stop=True),
            waits=[("w", dg)], tick="mm")
        ph.j("vector", lambda h: h.tensor_copy(self.negc[:, 0:16],
                                               self.ps[7][0:128, 0:16]),
             waits=[("mm", bm)], tick="v")

    def hnorm_b(self, ph, qT, kT, q2, hnc, knb):
        sq_tk, mm_tk = {}, {}
        for dc in range(NDC):
            sq_tk[dc] = ph.j("scalar", lambda h, dc=dc: h.activation(
                q2[dc % 2][:, 0:T], qT[:, dc * T:(dc + 1) * T], Act.Square),
                waits=[("mm", mm_tk.get((dc - 2, NTC - 1), 0))], tick="sc")
            for c in range(NTC):
                def qmm(h, dc=dc, c=c):
                    h.matmul(self.ps[c][0:16, 0:512],
                             self.hlhs[:, dc * NH:(dc + 1) * NH],
                             q2[dc % 2][:, c * 512:(c + 1) * 512],
                             start=(dc == 0), stop=(dc == NDC - 1),
                             tile_position=(0, 0))
                    return h.matmul(self.ps[4 + c][64:80, 0:512],
                                    self.hlhs[:, dc * NH:(dc + 1) * NH],
                                    q2[dc % 2][:, c * 512:(c + 1) * 512],
                                    start=(dc == 0), stop=(dc == NDC - 1),
                                    tile_position=(0, 64))
                mm_tk[(dc, c)] = ph.j("tensor", qmm, waits=[("sc", sq_tk[dc])],
                                      tick="mm")
        ev = {}
        for c in range(NTC):
            def qev(h, c=c):
                h.activation(hnc[0:16, c * 512:(c + 1) * 512],
                             self.ps[c][0:16, 0:512], Act.Sqrt)
                return h.activation(hnc[64:80, c * 512:(c + 1) * 512],
                                    self.ps[4 + c][64:80, 0:512], Act.Sqrt)
            ev[c] = ph.j("scalar", qev,
                         waits=[("mm", mm_tk[(NDC - 1, NTC - 1)])], tick="sc")
        mm2, s2_tk = {}, {}
        for dc in range(NDC):
            s2_tk[dc] = ph.j("scalar", lambda h, dc=dc: h.activation(
                q2[dc % 2][:, 0:T], kT[:, dc * T:(dc + 1) * T], Act.Square),
                waits=[("mm", mm2.get((dc - 2, NTC - 1), 0)),
                       ("sc", ev[NTC - 1])], tick="sc")
            for c in range(NTC):
                def kmm(h, dc=dc, c=c):
                    h.matmul(self.ps[c][0:16, 0:512],
                             self.hlhs[:, dc * NH:(dc + 1) * NH],
                             q2[dc % 2][:, c * 512:(c + 1) * 512],
                             start=(dc == 0), stop=(dc == NDC - 1),
                             tile_position=(0, 0))
                    return h.matmul(self.ps[4 + c][64:80, 0:512],
                                    self.hlhs[:, dc * NH:(dc + 1) * NH],
                                    q2[dc % 2][:, c * 512:(c + 1) * 512],
                                    start=(dc == 0), stop=(dc == NDC - 1),
                                    tile_position=(0, 64))
                mm2[(dc, c)] = ph.j("tensor", kmm,
                                    waits=[("sc", s2_tk[dc]),
                                           ("sc", ev[c])], tick="mm")
        ev2 = {}
        for c in range(NTC):
            def kev(h, c=c):
                h.activation(knb[0:16, c * 512:(c + 1) * 512],
                             self.ps[c][0:16, 0:512], Act.Sqrt)
                return h.activation(knb[64:80, c * 512:(c + 1) * 512],
                                    self.ps[4 + c][64:80, 0:512], Act.Sqrt)
            ev2[c] = ph.j("scalar", kev,
                          waits=[("mm", mm2[(NDC - 1, NTC - 1)])], tick="sc")
        def bmx(h):
            h.drain()
            h.tensor_reduce(self.smal[0:80, 32:48],
                            knb[0:80, 0:T].rearrange("p (b j) -> p b j", j=128),
                            AX.X, Alu.max, negate=True)
            h.drain()
            for B in range(1, 16):
                h.tensor_tensor(self.smal[0:80, 32 + B:33 + B],
                                self.smal[0:80, 32 + B:33 + B],
                                self.smal[0:80, 31 + B:32 + B], Alu.min)
                h.drain()
            bc = self.smal[0:80, 32:48].unsqueeze(2).broadcast_to([80, 16, 128])
            return h.tensor_tensor(
                hnc[0:80, 0:T].rearrange("p (b j) -> p b j", j=128),
                hnc[0:80, 0:T].rearrange("p (b j) -> p b j", j=128),
                bc, Alu.mult)
        ph.j("vector", bmx, waits=[("sc", ev2[NTC - 1])], tick="v")

    def p_attn_core(self, sq, kind, qT, kT, vna, outT, pb, pn, mask,
                    proj_cb=None):
        scale = (1.0 / 32.0) if kind == "a" else 0.125
        kspan = CK if kind == "a" else T
        ph = Phase(self, f"{kind}{sq}_att")
        rdm = self.rd[0:64, 0:256].bitcast(BF16)
        rnd = 0
        nrm_tk, gb_tk, rcp_tk, proj_tk = {}, {}, {}, {}
        shf_tk = {}
        sb_last = {}
        pb_last = {}
        nul_exp_last = [0]
        nav_last = [0]
        ks_glob = [0]
        for c in range(NTC):
            for h_ in range(NH):
                r = rnd
                rnd += 1
                gl = h_ % 2
                po = 64 * gl
                hc = h_ // 2
                pob = self.ps[2 + r % 2]
                kts = list(range(c + 1)) if kind == "a" else list(range(4 * c + 4))
                av_tk = {}
                first_set = False
                if kind == "a":
                    def nsc(h, h_=h_, po=po, c=c):
                        rq = qT[po:po + 64, hc * T + c * 512:hc * T + c * 512 + 512] \
                            if False else qT[po:po + 64,
                                             (h_ // 2) * T + c * 512:
                                             (h_ // 2) * T + c * 512 + 512]
                        return h.matmul(self.ps[4][0:1, 0:512],
                                        self.nullk[po:po + 64, h_:h_ + 1], rq,
                                        start=True, stop=True,
                                        tile_position=(po, 0))
                    m = ph.j("tensor", nsc,
                             waits=[("sc", nul_exp_last[0])], tick="mm")
                    en = ph.j("scalar", lambda h, h_=h_: h.activation(
                        pn[0:1, 0:512], self.ps[4][0:1, 0:512], Act.Exp,
                        bias=self.negc[0:1, h_:h_ + 1], scale=scale),
                        waits=[("mm", m), ("mm", nav_last[0])], tick="sc")
                    nul_exp_last[0] = en
                    def nav(h, h_=h_, pob=pob):
                        return h.matmul(pob[0:65, 0:512],
                                        self.nullv[0:1, h_ * 65:(h_ + 1) * 65],
                                        pn[0:1, 0:512], start=True, stop=False,
                                        tile_position=(0, 0))
                    nav_last[0] = ph.j("tensor", nav,
                                       waits=[("sc", en),
                                              ("v", nrm_tk.get(r - 2, 0)),
                                              ("mm", shf_tk.get(r - 2, 0))],
                                       tick="mm")
                    first_set = True
                for ki, kt in enumerate(kts):
                    ks = ks_glob[0]
                    ks_glob[0] += 1
                    boundary = (kind == "a" and kt == c) or \
                               (kind == "b" and kt >= 4 * c)
                    sbi = ks % 2
                    sb_ = self.ps[sbi]
                    def smm(h, h_=h_, po=po, kt=kt, c=c, sb_=sb_):
                        lk = kT[po:po + 64, (h_ // 2) * kspan + kt * 128:
                                (h_ // 2) * kspan + (kt + 1) * 128]
                        rq = qT[po:po + 64, (h_ // 2) * T + c * 512:
                                (h_ // 2) * T + c * 512 + 512]
                        mm = h.matmul(sb_[0:128, 0:512], lk, rq,
                                      start=True, stop=(kind == "a"),
                                      tile_position=(po, 0))
                        if kind == "a":
                            return mm
                        return h.matmul(
                            sb_[0:128, 0:512],
                            self.rowsel[po:po + 16, h_ * 128:(h_ + 1) * 128],
                            self.hnc[po:po + 16,
                                     c * 512:c * 512 + 512],
                            start=False, stop=True, tile_position=(po, 0))
                    m = ph.j("tensor", smm,
                             waits=[("sc", sb_last.get(sbi, 0))], tick="mm")
                    wl = [("mm", m)]
                    if boundary:
                        if kind == "a":
                            mk = mask[:, 0:512]
                        else:
                            off = 384 - 128 * (kt - 4 * c)
                            mk = mask[:, off:off + 512]
                        mv = ph.j("vector", lambda h, sb_=sb_, mk=mk:
                                  h.tensor_add(sb_[0:128, 0:512],
                                               sb_[0:128, 0:512], mk),
                                  waits=[("mm", m)], tick="v")
                        wl = [("v", mv)]
                    pbi = ks % 2
                    bias_ap = self.negc[:, h_:h_ + 1] if kind == "a" else 0.0
                    ex = ph.j("scalar", lambda h, sb_=sb_, pbi=pbi, bias_ap=bias_ap:
                              h.activation(pb[pbi][:, 0:512], sb_[0:128, 0:512],
                                           Act.Exp, bias=bias_ap, scale=scale),
                              waits=wl + [("mm", pb_last.get(pbi, 0))], tick="sc")
                    sb_last[sbi] = ex
                    st = (not first_set) and ki == 0
                    def av(h, h_=h_, kt=kt, pob=pob, st=st, pbi=pbi):
                        return h.matmul(
                            pob[0:65, 0:512],
                            vna[:, kt * 1040 + h_ * 65:kt * 1040 + (h_ + 1) * 65],
                            pb[pbi][:, 0:512], start=st, stop=False,
                            tile_position=(0, 0))
                    av_tk[ki] = ph.j("tensor", av,
                                     waits=[("sc", ex),
                                            ("v", nrm_tk.get(r - 2, 0)),
                                            ("mm", shf_tk.get(r - 2, 0))],
                                     tick="mm")
                    pb_last[pbi] = av_tk[ki]
                lastav = av_tk[len(kts) - 1]
                def rcp(h, pob=pob):
                    h.activation(self.rd[64:65, 0:512], pob[64:65, 0:512], Act.Ln)
                    h.drain()
                    return h.activation(self.rd[64:65, 0:256].bitcast(BF16),
                                        self.rd[64:65, 0:512], Act.Exp,
                                        scale=-1.0)
                rcp_tk[r] = ph.j("scalar", rcp, waits=[("mm", lastav)], tick="sc")
                gb_tk[r] = ph.j("tensor", lambda h: h.matmul(
                    self.ps[5][0:64, 0:512], self.onesB[64:65, 0:64],
                    self.rd[64:65, 0:256].bitcast(BF16), start=True, stop=True,
                    tile_position=(64, 0)),
                    waits=[("sc", rcp_tk[r]), ("v", nrm_tk.get(r - 1, 0))],
                    tick="mm")
                if proj_cb is None:
                    dste = outT[0:64, (h_ // 2) * T + c * 512:
                                (h_ // 2) * T + c * 512 + 512]
                    dsto = outT[64:128, (h_ // 2) * T + c * 512:
                                (h_ // 2) * T + c * 512 + 512]
                else:
                    dste = outT[0:64, (h_ // 2) * 512:(h_ // 2) * 512 + 512]
                    dsto = outT[64:128, (h_ // 2) * 512:(h_ // 2) * 512 + 512]
                if gl == 0:
                    def evn(h, dste=dste, pob=pob):
                        h.tensor_copy(self.rdb[0:64, 0:512],
                                      self.ps[5][0:64, 0:512])
                        h.drain()
                        return h.tensor_tensor(dste, pob[0:64, 0:512],
                                               self.rdb[0:64, 0:512], Alu.mult)
                    nrm_tk[r] = ph.j("vector", evn,
                                     waits=[("mm", gb_tk[r]), ("mm", lastav),
                                            ("v", proj_tk.get(c - 1, 0))],
                                     tick="v")
                    shf_tk[r] = 0
                else:
                    def evn2(h, pob=pob):
                        h.tensor_copy(self.rdb[0:64, 0:512],
                                      self.ps[5][0:64, 0:512])
                        h.drain()
                        return h.tensor_tensor(rdm[0:64, 0:512],
                                               pob[0:64, 0:512],
                                               self.rdb[0:64, 0:512], Alu.mult)
                    v1 = ph.j("vector", evn2,
                              waits=[("mm", gb_tk[r]), ("mm", lastav),
                                     ("v", proj_tk.get(c - 1, 0)),
                                     ("v", nrm_tk.get(r - 1, 0))], tick="v")
                    shf_tk[r] = ph.j("tensor", lambda h: h.matmul(
                        self.ps[6][64:128, 0:512], self.ident[0:64, 0:64],
                        rdm[0:64, 0:512], start=True, stop=True,
                        tile_position=(0, 64)),
                        waits=[("v", v1), ("v", nrm_tk.get(r - 2, 0))], tick="mm")
                    nrm_tk[r] = ph.j("vector", lambda h, dsto=dsto: h.tensor_copy(
                        dsto, self.ps[6][64:128, 0:512]),
                        waits=[("mm", shf_tk[r])], tick="v")
            if proj_cb is not None:
                proj_tk[c] = proj_cb(ph, c, nrm_tk[rnd - 1])
        ph.run()

    def p_proj_a(self, sq, outT, wcol, yT):
        ph = Phase(self, f"a{sq}_proj")
        wo3 = self.wout_a.rearrange("(ec p) j -> p ec j", p=128)
        mm_tk, ev_tk, wl = {}, {}, {}
        for jc in range(NDC):
            wl[jc] = ph.j("gpsimd", lambda h, jc=jc: h.dma_start(
                out=wcol[jc % 2].rearrange("p (ec j) -> p ec j", j=128),
                in_=wo3[:, :, jc * 128:(jc + 1) * 128]),
                waits=[("mm", mm_tk.get((jc - 2, NTC - 1), 0))], tick="w", n=16)
            for c in range(NTC):
                i = jc * NTC + c
                bank = self.ps[i % 4]
                def mmf(h, jc=jc, c=c, bank=bank):
                    last = None
                    for ec in range(NDC):
                        last = h.matmul(
                            bank[:, 0:512],
                            wcol[jc % 2][:, ec * 128:(ec + 1) * 128],
                            outT[:, ec * T + c * 512:ec * T + c * 512 + 512],
                            start=(ec == 0), stop=(ec == NDC - 1))
                    return last
                mm_tk[(jc, c)] = ph.j("tensor", mmf,
                                      waits=[("w", wl[jc]),
                                             ("ev", ev_tk.get(i - 4, 0))],
                                      tick="mm")
                ev_tk[i] = ph.j("scalar", lambda h, jc=jc, c=c, bank=bank:
                                h.activation(
                                    yT[:, jc * T + c * 512:jc * T + c * 512 + 512],
                                    bank[:, 0:512], Act.Identity,
                                    bias=self.bouta[:, jc:jc + 1]),
                                waits=[("mm", mm_tk[(jc, c)])], tick="ev")
        ph.run()

    def p_fft(self, sq, yT):
        nc = self.nc
        with ExitStack() as es:
            Aim = es.enter_context(nc.sbuf_tensor(self.uname("Aim"), [128, NT * 513], BF16))
            Bim = es.enter_context(nc.sbuf_tensor(self.uname("Bim"), [128, NT * 513], BF16))
            Bng = es.enter_context(nc.sbuf_tensor(self.uname("Bng"), [128, NT * 513], BF16))
            cosD = es.enter_context(nc.sbuf_tensor(self.uname("cosD"), [128, NDC * 513], BF16))
            sinD = es.enter_context(nc.sbuf_tensor(self.uname("sinD"), [128, NDC * 513], BF16))
            ct = self.dbuf(es, "ct", 4, 128, BF16)
            ph = Phase(self, f"f{sq}_dd")
            ah = Aim[:].bitcast(F32)[:, 0:NDC * 513]
            bh = Bim[:].bitcast(F32)[:, 0:NDC * 513]
            l1 = ph.j("sync", lambda h: h.dma_start(out=ah, in_=self.cosD_hi[:]),
                      tick="io", n=16)
            l2 = ph.j("sync", lambda h: h.dma_start(out=bh, in_=self.cosD_lo[:]),
                      tick="io", n=16)
            v1 = ph.j("vector", lambda h: h.tensor_add(cosD[:], ah, bh),
                      waits=[("io", l2)], tick="v")
            l3 = ph.j("sync", lambda h: h.dma_start(out=ah, in_=self.sinD_hi[:]),
                      waits=[("v", v1)], tick="io", n=16)
            l4 = ph.j("sync", lambda h: h.dma_start(out=bh, in_=self.sinD_lo[:]),
                      tick="io", n=16)
            ph.j("vector", lambda h: h.tensor_add(sinD[:], ah, bh),
                 waits=[("io", l4)], tick="v")
            ph.run()

            ph = Phase(self, f"f{sq}_1")
            ev_tk = {}
            for tt in range(NT):
                g0 = 3 * (tt % 2)
                pA, pB, pC = self.ps[g0], self.ps[g0 + 1], self.ps[g0 + 2]
                def mmf(h, tt=tt, pA=pA, pB=pB, pC=pC):
                    last = None
                    for dc in range(NDC):
                        lt = yT[:, dc * T + tt * 128:dc * T + (tt + 1) * 128]
                        st_ = (dc == 0)
                        sp = (dc == NDC - 1)
                        h.matmul(pA[:, 0:512], lt, cosD[:, dc * 513:dc * 513 + 512],
                                 start=st_, stop=sp)
                        h.matmul(pC[:, 0:1], lt,
                                 cosD[:, dc * 513 + 512:dc * 513 + 513],
                                 start=st_, stop=sp)
                        h.matmul(pB[:, 0:512], lt, sinD[:, dc * 513:dc * 513 + 512],
                                 start=st_, stop=sp)
                        last = h.matmul(pC[:, 1:2],
                                        lt, sinD[:, dc * 513 + 512:dc * 513 + 513],
                                        start=False, stop=sp)
                    return last
                m = ph.j("tensor", mmf, waits=[("ev", ev_tk.get(tt - 2, 0))],
                         tick="mm")
                def evs(h, tt=tt, pA=pA, pB=pB, pC=pC):
                    h.copy(Aim[:, tt * 513:tt * 513 + 512], pA[:, 0:512])
                    h.copy(Aim[:, tt * 513 + 512:tt * 513 + 513], pC[:, 0:1])
                    h.copy(Bim[:, tt * 513:tt * 513 + 512], pB[:, 0:512])
                    return h.copy(Bim[:, tt * 513 + 512:tt * 513 + 513], pC[:, 1:2])
                se = ph.j("scalar", evs, waits=[("mm", m)], tick="sc")
                def evv(h, tt=tt, pB=pB, pC=pC):
                    h.tensor_scalar_mul(Bng[:, tt * 513:tt * 513 + 512],
                                        pB[:, 0:512], -1.0)
                    return h.tensor_scalar_mul(
                        Bng[:, tt * 513 + 512:tt * 513 + 513], pC[:, 1:2], -1.0)
                ev_tk[tt] = ph.j("vector", evv, waits=[("mm", m), ("sc", se)],
                                 tick="ev")
            ph.run()

            ph = Phase(self, f"f{sq}_2")
            io_tk, mm_tk, v_tk = {}, {}, {}
            for stt in range(NT):
                pL = self.ps[6] if stt % 2 == 0 else self.ps[4]
                pH = self.ps[7] if stt % 2 == 0 else self.ps[5]
                for tc in range(NT):
                    i = stt * NT + tc
                    cb = ct[2 * (i % 2)]
                    sb_ = ct[2 * (i % 2) + 1]
                    ph.j("sync", lambda h, stt=stt, tc=tc, cb=cb: h.dma_start(
                        out=cb[:], in_=self.sc_cosT[:, tc * T + stt * 128:
                                                    tc * T + (stt + 1) * 128]),
                        waits=[("mm", mm_tk.get(i - 2, 0))], tick="io", n=16)
                    io_tk[i] = ph.j("sync", lambda h, stt=stt, tc=tc, sb_=sb_:
                                    h.dma_start(out=sb_[:],
                                                in_=self.sc_sinT[:, tc * T + stt * 128:
                                                                 tc * T + (stt + 1) * 128]),
                                    tick="io", n=16)
                    def mmf(h, tc=tc, cb=cb, sb_=sb_, pL=pL, pH=pH):
                        st_ = (tc == 0)
                        sp = (tc == NT - 1)
                        arev = Aim[:, tc * 513 + 512:tc * 513:-1]
                        brev = Bim[:, tc * 513 + 512:tc * 513:-1]
                        h.matmul(pL[:, 0:512], cb[:],
                                 Aim[:, tc * 513:tc * 513 + 512],
                                 start=st_, stop=False)
                        h.matmul(pH[:, 0:512], cb[:], arev, start=st_, stop=False)
                        h.matmul(pL[:, 0:512], sb_[:],
                                 Bng[:, tc * 513:tc * 513 + 512],
                                 start=False, stop=sp)
                        return h.matmul(pH[:, 0:512], sb_[:], brev,
                                        start=False, stop=sp)
                    mm_tk[i] = ph.j("tensor", mmf,
                                    waits=[("io", io_tk[i]),
                                           ("v", v_tk.get(stt - 2, 0))], tick="mm")
                def evx(h, stt=stt, pL=pL, pH=pH):
                    h.tensor_add(self.x_sb[:, stt * D:stt * D + 512],
                                 self.x_sb[:, stt * D:stt * D + 512], pL[:, 0:512])
                    return h.tensor_add(self.x_sb[:, stt * D + 512:(stt + 1) * D],
                                        self.x_sb[:, stt * D + 512:(stt + 1) * D],
                                        pH[:, 0:512])
                v_tk[stt] = ph.j("vector", evx,
                                 waits=[("mm", mm_tk[stt * NT + NT - 1])], tick="v")
            ph.run()

    def p_ffn(self, sq, call):
        nc = self.nc
        with ExitStack() as es:
            wgT = es.enter_context(nc.sbuf_tensor(self.uname("wgT"), [128, NDC * II], BF16))
            wuT = es.enter_context(nc.sbuf_tensor(self.uname("wuT"), [128, NDC * II], BF16))
            with ExitStack() as e2:
                q = self.mk_qtmp(e2, 1024)
                lnt = e2.enter_context(nc.sbuf_tensor(self.uname("f_ln"), [128, D], F32))
                tm2 = e2.enter_context(nc.sbuf_tensor(self.uname("f_tm2"), [128, D], F32))
                sums = e2.enter_context(nc.sbuf_tensor(self.uname("f_sums"), [128, 8], F32))
                xq = self.dbuf(e2, "f_xq", 2, D, BF16)
                xqT = self.dbuf(e2, "f_xqT", 2, D, BF16)
                sg = e2.enter_context(nc.sbuf_tensor(self.uname("f_sg"), [128, 512], F32))
                hb = e2.enter_context(nc.sbuf_tensor(self.uname("f_h"), [128, 512], F32))
                hqt = self.dbuf(e2, "f_hq", 2, II, BF16)
                ph = Phase(self, f"n{sq}_{call}_s1")
                ph.j("sync", lambda h: h.dma_start(out=wgT[:], in_=self.sc_wg[:]),
                     tick="io", n=16)
                lu = ph.j("sync", lambda h: h.dma_start(out=wuT[:],
                                                        in_=self.sc_wu[:]),
                          tick="io", n=16)
                sq_tk, mm_tk, hst = {}, {}, {}
                sil_tk, hq_tk, trst, vh2_tk = {}, {}, {}, {}
                sc3h = self.sc_hqT.rearrange("p (ic t) -> p ic t", t=T)
                for tt in range(NT):
                    xs = self.x_sb[:, tt * D:(tt + 1) * D]
                    def st1(h, xs=xs):
                        h.drain()
                        h.tensor_tensor(tm2[:, 0:D], xs, xs, Alu.mult)
                        h.reduce_sum(sums[:, 0:1], xs, AX.X)
                        h.reduce_sum(sums[:, 1:2], tm2[:, 0:D], AX.X)
                        h.drain()
                        h.scalar_tensor_tensor(sums[:, 2:3], sums[:, 0:1],
                                               1.0 / (D * D), sums[:, 0:1],
                                               Alu.mult, Alu.mult)
                        h.tensor_scalar_mul(sums[:, 0:1], sums[:, 0:1], 1.0 / D)
                        h.tensor_scalar_mul(sums[:, 1:2], sums[:, 1:2], 1.0 / D)
                        h.drain()
                        return h.tensor_scalar(sums[:, 1:2], sums[:, 1:2],
                                               sums[:, 2:3], 1e-5,
                                               Alu.subtract, Alu.add)
                    v1 = ph.j("vector", st1, waits=[("sc", sq_tk.get(tt - 1, 0))],
                              tick="v")
                    sq_tk[tt] = ph.j("scalar", lambda h: h.activation(
                        sums[:, 2:3], sums[:, 1:2], Act.Sqrt),
                        waits=[("v", v1)], tick="sc")
                    CX = 768
                    def lnap(h, xs=xs, tt=tt):
                        h.reciprocal(sums[:, 3:4], sums[:, 2:3])
                        h.drain()
                        h.tensor_scalar(lnt[:, 0:D], xs, sums[:, 0:1], sums[:, 3:4],
                                        Alu.subtract, Alu.mult)
                        return self.quant_head(h, lnt[:, 0:D], q, D)
                    vh = ph.j("vector", lnap,
                              waits=[("sc", sq_tk[tt]),
                                     ("io", hst.get(tt - 1, 0)),
                                     ("mm", mm_tk.get((tt - 2, 5), 0))], tick="v")
                    sgq = ph.j("scalar", lambda h: self.quant_sign(
                        h, lnt, q, D, 0, D), waits=[("v", vh)], tick="sc")
                    ph.j("vector", lambda h: self.quant_steps(
                        h, lnt[:, 0:D], q, D, 0, D), tick="v")
                    vq = ph.j("vector", lambda h, tt=tt: self.quant_fin(
                        h, xq[tt % 2], q, D, 0, D), waits=[("sc", sgq)],
                        tick="v")
                    iot = None
                    for dc in range(NDC):
                        iot = ph.j("sync", lambda h, tt=tt, dc=dc:
                                   h.dma_start_transpose(
                                       xqT[tt % 2][:, dc * 128:(dc + 1) * 128],
                                       xq[tt % 2][:, dc * 128:(dc + 1) * 128]),
                                   waits=[("v", vq)], tick="io", n=16)
                    for ic, (i0, iw) in enumerate(ICS):
                        gr = tt * 6 + ic
                        pg = self.ps[2 * (gr % 2)]
                        pu = self.ps[2 * (gr % 2) + 1]
                        def gmm(h, tt=tt, i0=i0, iw=iw, pg=pg, pu=pu):
                            last = None
                            for dc in range(NDC):
                                lt = xqT[tt % 2][:, dc * 128:(dc + 1) * 128]
                                h.matmul(pg[:, 0:iw], lt,
                                         wgT[:, dc * II + i0:dc * II + i0 + iw],
                                         start=(dc == 0), stop=(dc == NDC - 1))
                                last = h.matmul(
                                    pu[:, 0:iw], lt,
                                    wuT[:, dc * II + i0:dc * II + i0 + iw],
                                    start=(dc == 0), stop=(dc == NDC - 1))
                            return last
                        mm_tk[(tt, ic)] = ph.j(
                            "tensor", gmm,
                            waits=[("io", iot), ("io", lu),
                                   ("sc", sil_tk.get(gr - 2, 0)),
                                   ("v", vh2_tk.get(gr - 2, 0))], tick="mm")
                        sil_tk[gr] = ph.j("scalar", lambda h, pg=pg, iw=iw:
                                          h.activation(sg[:, 0:iw], pg[:, 0:iw],
                                                       Act.Silu),
                                          waits=[("mm", mm_tk[(tt, ic)]),
                                                 ("v", vh2_tk.get(gr - 1, 0))],
                                          tick="sc")
                        def hqf(h, tt=tt, i0=i0, iw=iw, pu=pu):
                            h.tensor_tensor(hb[:, 0:iw], sg[:, 0:iw], pu[:, 0:iw],
                                            Alu.mult)
                            return self.quant_head(h, hb[:, 0:iw], q, iw)
                        vh2 = vh2_tk[gr] = ph.j(
                            "vector", hqf,
                            waits=[("sc", sil_tk[gr]),
                                   ("io", hst.get(tt - 2, 0))], tick="v")
                        sgh = ph.j("scalar", lambda h, iw=iw: self.quant_sign(
                            h, hb, q, iw, 0, iw), waits=[("v", vh2)], tick="sc")
                        ph.j("vector", lambda h, iw=iw: self.quant_steps(
                            h, hb[:, 0:iw], q, iw, 0, iw), tick="v")
                        hq_tk[gr] = ph.j("vector", lambda h, tt=tt, i0=i0,
                                         iw=iw: self.quant_fin(
                                             h, hqt[tt % 2][:, i0:i0 + iw], q,
                                             iw, 0, iw),
                                         waits=[("sc", sgh)], tick="v")
                        ncc = iw // 128
                        slot = xq[tt % 2][:, (gr % 2) * 512:(gr % 2) * 512 + 512]
                        tio2 = None
                        for j in range(ncc):
                            tio2 = ph.j("sync", lambda h, tt=tt, i0=i0, j=j,
                                        slot=slot: h.dma_start_transpose(
                                            slot[:, j * 128:(j + 1) * 128],
                                            hqt[tt % 2][:, i0 + j * 128:
                                                        i0 + (j + 1) * 128]),
                                        waits=[("v", hq_tk[gr]),
                                               ("io", trst.get(gr - 2, 0))],
                                        tick="io", n=16)
                        trst[gr] = ph.j("sync", lambda h, tt=tt, i0=i0, ncc=ncc,
                                        slot=slot: h.dma_start(
                                            out=sc3h[:, i0 // 128:
                                                     i0 // 128 + ncc,
                                                     tt * 128:(tt + 1) * 128],
                                            in_=slot[:, 0:ncc * 128].rearrange(
                                                "p (ic j) -> p ic j", j=128)),
                                        tick="io", n=16)
                    hst[tt] = trst[tt * 6 + 5]
                ph.run()
        with ExitStack() as es:
            wdT = es.enter_context(nc.sbuf_tensor(self.uname("wdT"), [128, NIC * D], BF16))
            hqT = self.dbuf(es, "f_hqT", 2, II, BF16)
            sc3h = self.sc_hqT.rearrange("p (ic t) -> p ic t", t=T)
            ph = Phase(self, f"n{sq}_{call}_s2")
            lw = ph.j("sync", lambda h: h.dma_start(out=wdT[:], in_=self.sc_wd[:]),
                      tick="io", n=16)
            mm_tk, v_tk = {}, {}
            for tt in range(NT):
                tio = ph.j("sync", lambda h, tt=tt: h.dma_start(
                    out=hqT[tt % 2][:].rearrange("p (ic j) -> p ic j", j=128),
                    in_=sc3h[:, 0:NIC, tt * 128:(tt + 1) * 128]),
                    waits=[("mm", mm_tk.get((tt - 2, 1), 0))], tick="io", n=16)
                for half in range(2):
                    bank = self.ps[4 + 2 * (tt % 2) + half]
                    def dmm(h, tt=tt, half=half, bank=bank):
                        last = None
                        for icc in range(NIC):
                            last = h.matmul(
                                bank[:, 0:512],
                                hqT[tt % 2][:, icc * 128:(icc + 1) * 128],
                                wdT[:, icc * D + half * 512:
                                    icc * D + half * 512 + 512],
                                start=(icc == 0), stop=(icc == NIC - 1))
                        return last
                    mm_tk[(tt, half)] = ph.j(
                        "tensor", dmm,
                        waits=[("io", tio), ("io", lw),
                               ("v", v_tk.get((tt - 2, half), 0))], tick="mm")
                    v_tk[(tt, half)] = ph.j(
                        "vector", lambda h, tt=tt, half=half, bank=bank:
                        h.tensor_add(
                            self.x_sb[:, tt * D + half * 512:
                                      tt * D + half * 512 + 512],
                            self.x_sb[:, tt * D + half * 512:
                                      tt * D + half * 512 + 512],
                            bank[:, 0:512]),
                        waits=[("mm", mm_tk[(tt, half)])], tick="v")
            ph.run()

    def p_attn_b(self, sq):
        nc = self.nc
        with ExitStack() as es:
            qT = es.enter_context(nc.sbuf_tensor(self.uname("bqT"), [128, NDC * T], BF16))
            kT = es.enter_context(nc.sbuf_tensor(self.uname("bkT"), [128, NDC * T], BF16))
            vnb = es.enter_context(nc.sbuf_tensor(self.uname("bvn"),
                                                  [128, NT * 1040], BF16))
            self.hnc = es.enter_context(
                nc.sbuf_tensor(self.uname("bhnc"), [80, T], BF16))
            self.rowsel = es.enter_context(
                nc.sbuf_tensor(self.uname("brs"), [128, NH * 128], BF16))
            phr = Phase(self, f"b{sq}_rs")
            phr.j("sync", lambda h: h.dma_start(out=self.rowsel[:],
                                               in_=self.rowsel_i[:]),
                  tick="io", n=16)
            phr.run()
            mkB = es.enter_context(nc.sbuf_tensor(self.uname("mkB"), [128, 896], BF16))
            phv = Phase(self, f"b{sq}_von")
            phv.j("gpsimd", lambda h: h.memset(
                vnb.rearrange("p (k o) -> p k o", o=65)[:, :, 64:65], 1.0),
                tick="g")
            phv.run()
            with ExitStack() as e2:
                xlnT = e2.enter_context(
                    nc.sbuf_tensor(self.uname("bxlT"), [128, NDC * 1024], BF16))
                bwb = e2.enter_context(
                    nc.sbuf_tensor(self.uname("bwb"), [128, 2048], BF16))
                self.wbuf = [bwb[:, 0:1024], bwb[:, 1024:2048]]
                bvt = e2.enter_context(
                    nc.sbuf_tensor(self.uname("bvt"), [128, 2048], BF16))
                vtmp = [bvt[:, 0:1024], bvt[:, 1024:2048]]
                for half in range(2):
                    with ExitStack() as e3:
                        ph = Phase(self, f"b{sq}_ln{half}")
                        if half == 0:
                            ph.j("sync", lambda h: h.dma_start(
                                out=mkB[:], in_=self.maskB_i[:]), tick="io", n=16)
                        self.ln_block(ph, list(range(8 * half, 8 * half + 8)),
                                      xlnT, e3, double=True, span=1024,
                                      tmp2=bvt[:].bitcast(F32),
                                      tmp=bwb[:].bitcast(F32))
                        ph.run()
                    self.p_qkv_b_half(sq, half, xlnT, vtmp, qT, kT, vnb)
            with ExitStack() as e4:
                q2 = self.dbuf(e4, "bq2", 2, T, BF16)
                knb = e4.enter_context(
                    nc.sbuf_tensor(self.uname("bknb"), [80, T], BF16))
                ph = Phase(self, f"b{sq}_hn")
                self.hnorm_b(ph, qT, kT, q2, self.hnc, knb)
                ph.run()
                if sq == 0:
                    self.maybe_probe("bnegc", self.negc[:, 0:16])
                    self.maybe_probe("bqT", qT[:])
                    self.maybe_probe("bkT", kT[:])
                    self.maybe_probe("bvn", vnb[:])
                    self.maybe_probe("bxlT", None)
            with ExitStack() as e5:
                outT = e5.enter_context(nc.sbuf_tensor(self.uname("boT"), [128, NDC * 512],
                                                       BF16))
                pb = self.dbuf(e5, "bpb", 2, 512, BF16)
                self.rd = e5.enter_context(nc.sbuf_tensor(self.uname("brd"), [128, 512], F32))
                self.rdb = e5.enter_context(nc.sbuf_tensor(self.uname("brdb"), [128, 512], BF16))
                wim = e5.enter_context(
                    nc.sbuf_tensor(self.uname("bwim"), [128, NDC * 1024], BF16))
                bb = e5.enter_context(
                    nc.sbuf_tensor(self.uname("bbb"), [128, D], BF16))
                phb = Phase(self, f"b{sq}_bb")
                m1 = phb.j("tensor", lambda h: h.matmul(
                    self.ps[7][0:128, 0:512], self.onesr[0:1, 0:128],
                    self.boutb[0:1, 0:512], start=True, stop=True,
                    tile_position=(0, 0)), tick="mm")
                phb.j("vector", lambda h: h.tensor_copy(bb[:, 0:512],
                                                        self.ps[7][:, 0:512]),
                      waits=[("mm", m1)], tick="v")
                m2 = phb.j("tensor", lambda h: h.matmul(
                    self.ps[7][0:128, 0:512], self.onesr[0:1, 0:128],
                    self.boutb[0:1, 512:1024], start=True, stop=True,
                    tile_position=(0, 0)), waits=[("v", self.sv["v"])], tick="mm")
                phb.j("vector", lambda h: h.tensor_copy(bb[:, 512:1024],
                                                        self.ps[7][:, 0:512]),
                      waits=[("mm", m2)], tick="v")
                phb.run()
                self._projb = {"outT": outT, "wim": wim, "wl": None, "bb": bb}
                self._core_es = e5
                self.p_attn_core(sq, "b", qT, kT, vnb, outT, pb, None, mkB,
                                 proj_cb=self.projb_cb)

    def p_qkv_b_half(self, sq, half, xlnT, vtmp, qT, kT, vnb):
        ph = Phase(self, f"b{sq}_qkv{half}")
        w3 = self.sc_wqkvb.rearrange("p (dc e) -> p dc e", e=3 * D)

        def wload(h, ec, buf):
            return h.dma_start(
                out=self.wbuf[buf].rearrange("p (dc j) -> p dc j", j=128),
                in_=w3[:, :, ec * 128:(ec + 1) * 128])

        def rhs(ec, c, dc):
            return xlnT[:, dc * 1024 + c * 512:dc * 1024 + c * 512 + 512]

        def evac(h, ec, c, bank):
            col = half * 1024 + c * 512
            if ec < 8:
                dst = qT[:, ec * T + col:ec * T + col + 512]
            elif ec < 16:
                dst = kT[:, (ec - 8) * T + col:(ec - 8) * T + col + 512]
            else:
                dst = vtmp[ec % 2][:, c * 512:c * 512 + 512]
            return h.activation(dst, bank[:, 0:512], Act.Identity,
                                bias=self.betW[:, ec:ec + 1])

        vn_ev = {}

        def kv_hook(ph, gi, ec, evlast):
            if ec < 16:
                return
            c2 = ec - 16
            for ktl in range(8):
                i = c2 * 8 + ktl
                ktg = half * 8 + ktl
                bank = self.ps[4 + i % 2]
                m = ph.j("tensor", lambda h, ec=ec, ktl=ktl, bank=bank: h.transpose(
                    bank[:, 0:64].bitcast(BF16),
                    vtmp[ec % 2][:, ktl * 128:(ktl + 1) * 128],
                    self.ident[:]),
                    waits=[("ev", evlast), ("v", vn_ev.get(i - 2, 0))], tick="mm")
                def cpy(h, c2=c2, ktg=ktg, bank=bank):
                    h.tensor_copy(
                        vnb[:, ktg * 1040 + (2 * c2) * 65:
                            ktg * 1040 + (2 * c2) * 65 + 64],
                        bank[:, 0:64].bitcast(BF16)[:, 0:64])
                    return h.tensor_copy(
                        vnb[:, ktg * 1040 + (2 * c2 + 1) * 65:
                            ktg * 1040 + (2 * c2 + 1) * 65 + 64],
                        bank[:, 0:64].bitcast(BF16)[:, 64:128])
                vn_ev[i] = ph.j("vector", cpy, waits=[("mm", m)], tick="v")

        self.qkv_phase(ph, list(range(NEC)), wload, rhs, evac, 1024,
                       weng="sync", wsem="io", kv_hook=kv_hook)
        ph.run()

    def projb_cb(self, ph, c, nrm_last):
        st = self._projb
        outT, wim = st["outT"], st["wim"]
        if st["wl"] is None:
            wo3 = self.woutT_b.rearrange("(ec p) j -> p ec j", p=128)
            st["wl"] = ph.j("gpsimd", lambda h: h.dma_start(
                out=wim.rearrange("p (ec j) -> p ec j", j=1024), in_=wo3),
                tick="w", n=16)
        last_v = 0
        for ttl in range(4):
            tt = c * 4 + ttl
            for half in range(2):
                def mmf(h, ttl=ttl, half=half):
                    last = None
                    for ec in range(NDC):
                        last = h.matmul(self.ps[7][:, 0:512],
                                        outT[:, ec * 512 + ttl * 128:
                                             ec * 512 + (ttl + 1) * 128],
                                        wim[:, ec * 1024 + half * 512:
                                            ec * 1024 + half * 512 + 512],
                                        start=(ec == 0), stop=(ec == NDC - 1))
                    return last
                m = ph.j("tensor", mmf,
                         waits=[("w", st["wl"]), ("v", nrm_last),
                                ("v", last_v)], tick="mm")
                def resadd(h, tt=tt, half=half):
                    xs = self.x_sb[:, tt * D + half * 512:
                                   tt * D + half * 512 + 512]
                    h.tensor_add(xs, xs, self.ps[7][:, 0:512])
                    h.drain()
                    return h.tensor_add(xs, xs,
                                        st["bb"][:, half * 512:half * 512 + 512])
                last_v = ph.j("vector", resadd, waits=[("mm", m)], tick="v")
        return last_v

    def build(self):
        self.declare()
        self.p_init()
        if self.upto >= 2:
            self.p_quant_weights()
        if self.upto >= 3:
            self.p_fold_wqkvb()
        if self.upto >= 1:
            self.p_build_Ttab()
        for sq in range(self.nseq):
            self.p_xload(sq)
            if self.upto >= 1:
                self.p_attn_a(sq)
            if self.upto >= 2:
                self.p_ffn(sq, 0)
            if self.upto >= 3:
                self.p_attn_b(sq)
            if self.upto >= 4:
                self.p_ffn(sq, 1)
            self.p_store(sq)
        ph = Phase(self, "fin")
        ph.j("sync", lambda h: h.nop(),
             waits=[("io", self.sv["io"]), ("w", self.sv["w"]),
                    ("v", self.sv["v"]), ("mm", self.sv["mm"]),
                    ("sc", self.sv["sc"]), ("g", self.sv["g"]),
                    ("ev", self.sv["ev"])])
        ph.run(tail=False)
        self.st.close()
        return self.nc


def _tables():
    f = {}
    d = np.arange(1024, dtype=np.float64)
    fidx = np.arange(513, dtype=np.float64)
    angD = 2.0 * np.pi * np.outer(d, fidx) / 1024.0
    for nm, fn in [("cosD", np.cos), ("sinD", np.sin)]:
        tab = fn(angD)
        if nm == "sinD":
            tab[:, 0] = 0.0
            tab[:, 512] = 0.0
        hi = tab.astype(np.float32)
        lo = (tab - hi.astype(np.float64)).astype(np.float32)
        for sfx, arr in [("_hi", hi), ("_lo", lo)]:
            img = arr.reshape(8, 128, 513).transpose(1, 0, 2).reshape(128, 8 * 513)
            f[nm + sfx] = np.ascontiguousarray(img)
    t = np.arange(2048, dtype=np.float64)
    ang1 = 2.0 * np.pi * np.outer(t, np.arange(16, dtype=np.float64) * 128.0) / 2048.0
    ang2 = 2.0 * np.pi * np.outer(t, np.arange(128, dtype=np.float64)) / 2048.0
    for nm, ang, w in [("c1", ang1, 16), ("s1", ang1, 16),
                       ("c2", ang2, 128), ("s2", ang2, 128)]:
        tab = (np.cos if nm[0] == "c" else np.sin)(ang)
        hi = tab.astype(np.float32)
        lo = (tab - hi.astype(np.float64)).astype(np.float32)
        for sfx, arr in [("_hi", hi), ("_lo", lo)]:
            img = arr.reshape(16, 128, w).transpose(1, 0, 2).reshape(128, 16 * w)
            f[nm + sfx] = np.ascontiguousarray(img)
    r = np.arange(128)
    tq = np.arange(512)
    f["maskA"] = np.where(tq[None, :] < 4 * r[:, None] + 3,
                          np.float32(-1e38), np.float32(0.0)).astype(BF)
    w = np.arange(896)
    f["maskB"] = np.where(w[None, :] < r[:, None] + 384,
                          np.float32(-1e38), np.float32(0.0)).astype(BF)
    f["identc"] = np.eye(128, dtype=BF)
    rs = np.zeros((128, 16, 128), dtype=BF)
    for hh in range(16):
        rs[hh, hh, :] = 1.0
        rs[64 + hh, hh, :] = 1.0
    f["rowsel"] = np.ascontiguousarray(rs.reshape(128, 16 * 128))
    hl = np.zeros((128, 8, 16), dtype=BF)
    for dc in range(8):
        for rr in range(128):
            hl[rr, dc, 2 * dc + rr // 64] = 1.0
    f["hlhs"] = np.ascontiguousarray(hl.reshape(128, 128))
    return f


_NC_CACHE = {}


def _get_nc(upto, nseq):
    key = (upto, nseq)
    if key not in _NC_CACHE:
        _NC_CACHE[key] = Bld(upto, nseq).build()
    return _NC_CACHE[key]


def _host_inputs(inputs):
    conv_w = np.ascontiguousarray(inputs["L0_attn_a_conv_w"])
    cw = conv_w.transpose(2, 1, 0)
    cwi = np.zeros((128, 4 * 1024), dtype=np.float32)
    for kap in range(4):
        cwi[0:64, kap * 1024:(kap + 1) * 1024] = cw[kap]
        cwi[64:128, kap * 1024:(kap + 1) * 1024] = cw[kap]
    nullk = np.asarray(inputs["L0_attn_a_null_k"]).reshape(16, 64)
    nki = np.zeros((128, 16), dtype=np.float32)
    nki[0:64, :] = nullk.T
    nki[64:128, :] = nullk.T
    nullv = np.asarray(inputs["L0_attn_a_null_v"], dtype=np.float32).reshape(16, 64)
    nvi = np.concatenate([nullv, np.ones((16, 1), np.float32)], axis=1).reshape(1, 16 * 65)
    nvi = np.ascontiguousarray(nvi)
    col8 = lambda b: np.ascontiguousarray(
        np.asarray(b, dtype=np.float32).reshape(8, 128).T)
    ins = {
        "wqkv_a": np.ascontiguousarray(inputs["L0_attn_a_Wqkv"], dtype=np.float32),
        "convw": cwi,
        "convb": col8(inputs["L0_attn_a_conv_b"]),
        "nullk": nki,
        "nullv": nvi,
        "wout_a": np.ascontiguousarray(inputs["L0_attn_a_W_out"], dtype=np.float32),
        "bout_a": col8(inputs["L0_attn_a_b_out"]),
        "gam_b": col8(inputs["L0_attn_b_ln_gamma"]),
        "bet_b": col8(inputs["L0_attn_b_ln_beta"]),
        "wqkvT_b": np.ascontiguousarray(
            np.asarray(inputs["L0_attn_b_w_qkv"], dtype=np.float32).T),
        "woutT_b": np.ascontiguousarray(
            np.asarray(inputs["L0_attn_b_w_out"], dtype=np.float32).T),
        "bout_b": np.ascontiguousarray(
            np.asarray(inputs["L0_attn_b_b_out"], dtype=np.float32).reshape(1, 1024)),
        "w_gate": np.ascontiguousarray(inputs["L0_ffn_w_gate"], dtype=np.float32),
        "w_up": np.ascontiguousarray(inputs["L0_ffn_w_up"], dtype=np.float32),
        "w_down": np.ascontiguousarray(inputs["L0_ffn_w_down"], dtype=np.float32),
    }
    ins.update(_tables())
    return ins


def kernel(**inputs):
    n_cores, nseq = 8, 2
    x = np.ascontiguousarray(np.asarray(inputs["L0_attn_a_x"], dtype=np.float32))
    ins = _host_inputs(inputs)
    nc = _get_nc(4, nseq)
    in_maps = []
    for core in range(n_cores):
        m = dict(ins)
        m["x"] = np.ascontiguousarray(x[core * nseq:(core + 1) * nseq])
        in_maps.append(m)
    res = run_bass_kernel_spmd(nc, in_maps, core_ids=list(range(n_cores)))
    out = np.concatenate([r["y"] for r in res.results], axis=0)
    return out.astype(np.float32)
